# version 23
# baseline (speedup 1.0000x reference)
import contextlib
import numpy as np

import concourse.bass as bass
import concourse.mybir as mybir
from concourse.alu_op_type import AluOpType as op
from concourse.bass_utils import run_bass_kernel_spmd

F32 = mybir.dt.float32
F32R = mybir.dt.float32r
BF16 = mybir.dt.bfloat16
AF = mybir.ActivationFunctionType
AX = mybir.AxisListType

B_SZ, L, D, DS = 8, 2048, 1024, 64
T = 128
NCH = L // T
NF = 255
NEXP = 12
LN_EPS = 1e-5
DEV = False

_ENGS = ("pe", "dve", "act", "pool", "sp")


class _Rec:
    __slots__ = ("eng", "fn", "waits", "inc", "dma", "sem", "val")

    def __init__(self, eng, fn, dma):
        self.eng, self.fn, self.dma = eng, fn, dma
        self.waits = []
        self.inc = False
        self.sem = None
        self.val = None


class Buf:
    __slots__ = ("last_w", "readers", "name")

    def __init__(self, name=""):
        self.last_w = None
        self.readers = []
        self.name = name


class Prog:
    def __init__(self):
        self.recs = {e: [] for e in _ENGS}
        self.fence_pending = {e: [] for e in _ENGS}

    def emit(self, eng, fn, reads=(), writes=(), dma=False, extra_waits=(),
             touch=None):
        rec = _Rec(eng, fn, dma)
        waits = list(extra_waits)
        for b in reads:
            if b.last_w is not None:
                waits.append(b.last_w)
        for b in writes:
            if b.last_w is not None:
                waits.append(b.last_w)
            waits.extend(b.readers)
        if self.fence_pending[eng]:
            waits.extend(self.fence_pending[eng])
            self.fence_pending[eng] = []
        seen = set()
        for w in waits:
            if w is None or w is rec or id(w) in seen:
                continue
            seen.add(id(w))
            if (not w.dma) and w.eng == eng and eng in ("pe", "sp"):
                continue
            rec.waits.append(w)
            w.inc = True
        self.recs[eng].append(rec)
        for b in reads:
            b.readers.append(rec)
        for b in writes:
            b.last_w = rec
            b.readers = []
        if touch is not None:
            if eng == "act":
                fn2 = (lambda ap: lambda e: e.copy(ap, ap))(touch)
            else:
                fn2 = (lambda ap: lambda e: e.tensor_copy(ap, ap))(touch)
            return self.emit(eng, fn2, reads=list(writes), writes=list(writes))
        return rec

    def fence(self):
        tails = [self.recs[e][-1] for e in _ENGS if self.recs[e]]
        for e in _ENGS:
            self.fence_pending[e] = list(tails)

    def finalize(self, nc, stack):
        esem = {e: stack.enter_context(nc.semaphore(f"sem_{e}")) for e in _ENGS}
        NRING = 6
        dring = {q: [stack.enter_context(nc.semaphore(f"dma_{q}{i}"))
                     for i in range(NRING)] for q in ("sp", "pool")}
        dcnt = {"sp": 0, "pool": 0}
        ecnt = {e: 0 for e in _ENGS}
        for e in _ENGS:
            for rec in self.recs[e]:
                if rec.dma:
                    q = "sp" if e == "sp" else "pool"
                    rec.sem = dring[q][dcnt[q] % NRING]
                    rec.val = (dcnt[q] // NRING + 1) * 16
                    dcnt[q] += 1
                elif rec.inc:
                    ecnt[e] += 1
                    rec.sem = esem[e]
                    rec.val = ecnt[e]

    def replay(self, eng_name, eng):
        watermark = {}
        for rec in self.recs[eng_name]:
            for w in rec.waits:
                assert w.sem is not None, "wait on un-finalized record"
                key = id(w.sem)
                if watermark.get(key, 0) >= w.val:
                    continue
                eng.wait_ge(w.sem, w.val)
                watermark[key] = w.val
            if rec.dma and rec.val > 16:
                key = id(rec.sem)
                if watermark.get(key, 0) < rec.val - 16:
                    eng.wait_ge(rec.sem, rec.val - 16)
                    watermark[key] = rec.val - 16
            if rec.fn is None:
                continue
            instr = rec.fn(eng)
            if rec.sem is not None:
                instr.then_inc(rec.sem, 16 if rec.dma else 1)


def _dft_mats():
    f = np.arange(128)
    tau = np.arange(T)
    ang = 2 * np.pi * np.outer(tau, f) / NF
    cfT = np.cos(ang)
    sfT = -np.sin(ang)
    w = np.full(128, 2.0)
    w[0] = 1.0
    angi = 2 * np.pi * np.outer(f, tau) / NF
    icT = np.cos(angi) * w[:, None] / NF
    isT = -np.sin(angi) * w[:, None] / NF
    m64 = np.concatenate([cfT, sfT, icT, isT], axis=1)
    hi = m64.astype(np.float32)
    lo = (m64 - hi.astype(np.float64)).astype(np.float32)
    return hi, lo


def build_nc():
    nc = bass.Bass("TRN2", target_bir_lowering=False)
    P = Prog()
    EM = P.emit

    dx_tm = nc.dram_tensor("x_tm", [L, D], F32, kind="ExternalInput").ap()
    dx_fm = nc.dram_tensor("x_fm", [NCH, 128, D], F32, kind="ExternalInput").ap()
    dwin = nc.dram_tensor("w_in_r", [128, 8, D], F32, kind="ExternalInput").ap()
    dbin = nc.dram_tensor("b_in_row", [1, D], F32, kind="ExternalInput").ap()
    dgam = nc.dram_tensor("gamma_row", [1, D], F32, kind="ExternalInput").ap()
    dbet = nc.dram_tensor("beta_row", [1, D], F32, kind="ExternalInput").ap()
    dab = nc.dram_tensor("ab_vecs", [DS, 4], F32, kind="ExternalInput").ap()
    ddre = nc.dram_tensor("d_row", [2, D], F32, kind="ExternalInput").ap()
    dcre = nc.dram_tensor("cre_aug", [DS + 1, D], F32R, kind="ExternalInput").ap()
    dcim = nc.dram_tensor("cim_t", [DS, D], F32R, kind="ExternalInput").ap()
    ddft = nc.dram_tensor("dft_mats", [128, 512], F32R, kind="ExternalInput").ap()
    ddftl = nc.dram_tensor("dft_lo", [128, 512], F32R, kind="ExternalInput").ap()
    dones = nc.dram_tensor("ones_row", [1, 128], F32R, kind="ExternalInput").ap()
    donesf = nc.dram_tensor("ones_row_f", [1, 128], F32, kind="ExternalInput").ap()
    diden = nc.dram_tensor("ident_f", [128, 128], F32, kind="ExternalInput").ap()
    didenr = nc.dram_tensor("ident_r", [128, 128], F32R, kind="ExternalInput").ap()
    drw = nc.dram_tensor("rw_r", [128, 8, 8], F32, kind="ExternalInput").ap()
    drb = nc.dram_tensor("rbias", [1, 1], F32, kind="ExternalInput").ap()
    dgup = nc.dram_tensor("gup", [NEXP, 128, 8192], F32, kind="ExternalInput").ap()
    ddown = nc.dram_tensor("down_r", [NEXP, 128, 4096], F32R,
                           kind="ExternalInput").ap()
    dout = nc.dram_tensor("out", [L, D], F32, kind="ExternalOutput").ap()
    if DEV:
        dbg_xg = nc.dram_tensor("dbg_xg", [L, D], F32, kind="ExternalOutput").ap()
        dbg_o1 = nc.dram_tensor("dbg_o1", [L, D], F32, kind="ExternalOutput").ap()
        dbg_w8 = nc.dram_tensor("dbg_w8", [128, NCH, 8], F32,
                                kind="ExternalOutput").ap()
        dbg_y = nc.dram_tensor("dbg_y", [L, D], F32, kind="ExternalOutput").ap()
        dbg_x1 = nc.dram_tensor("dbg_x1", [L, D], F32, kind="ExternalOutput").ap()
        dbg_acc = nc.dram_tensor("dbg_acc", [L, D], F32, kind="ExternalOutput").ap()

    stack = contextlib.ExitStack()

    def S(name, shape, dt):
        return stack.enter_context(nc.sbuf_tensor(name, shape, dt))

    out1_tm = S("out1_tm", [128, NCH, 1024], F32)
    w8n = S("w8n", [128, NCH, 8], F32)
    statA = S("statA", [128, 2, 40], F32)
    iden = S("iden", [128, 128], F32)
    idenr = S("idenr", [128, 128], F32R)
    onesr = S("onesr", [1, 128], F32R)
    onesf = S("onesf", [1, 128], F32)
    epsc = S("epsc", [128, 1], F32)
    rw_sb = S("rw_sb", [128, 8, 8], F32)

    b_out1 = [Buf(f"out1_{c}") for c in range(NCH)]
    b_w8 = [Buf(f"w8_{c}") for c in range(NCH)]
    b_stat = [Buf("statA0"), Buf("statA1")]
    b_tab = Buf("tables")

    stA = contextlib.ExitStack()

    def SA(name, shape, dt):
        return stA.enter_context(nc.sbuf_tensor(name, shape, dt))

    win_sb = SA("win_sb", [128, 8, 1024], BF16)
    x_tm = SA("x_tm_sb", [128, 2, 1024], F32)
    x_fm = SA("x_fm_sb", [128, 2, 1024], BF16)
    xg = SA("xg_sb", [128, 2, 1024], F32R)
    scr = SA("scr", [128, 5, 1024], F32)
    cre_sb = SA("cre_sb", [DS + 1, 1024], F32R)
    cim_sb = SA("cim_sb", [DS, 1024], F32R)
    dft_sb = SA("dft_sb", [128, 512], F32R)
    dftl_sb = SA("dftl_sb", [128, 512], F32R)
    khat = SA("khat", [128, 2048], F32R)
    xh_im = SA("xh_im", [128, 1024], F32)
    pfreq = SA("pfreq", [128, 2048], F32R)
    mprod = SA("mprod", [DS, 2048], F32R)
    mprod2 = SA("mprod2", [DS, 2048], F32R)
    h_re = SA("h_re", [DS, 1024], F32R)
    h_im = SA("h_im", [DS, 1024], F32R)
    srow = SA("srow", [1, 1024], F32)
    dsrow = SA("dsrow", [1, 1024], F32R)
    drow = SA("drow", [1, 1024], F32)
    binrow = SA("binrow", [1, 1024], F32)
    bgam = SA("bgam", [128, 1024], F32)
    bbet = SA("bbet", [128, 1024], F32)
    pbre = SA("pbre", [DS + 1, 128], F32R)
    npbim = SA("npbim", [DS, 128], F32R)
    prevtr = SA("prevtr", [128, 64], F32R)
    prevti = SA("prevti", [128, 64], F32R)
    qre = SA("qre", [DS, 128], F32R)
    nqre = SA("nqre", [DS, 128], F32R)
    nqim = SA("nqim", [DS, 128], F32R)
    dgatre = SA("dgatre", [DS, 64], F32R)
    dgatim = SA("dgatim", [DS, 64], F32R)
    dgnatim = SA("dgnatim", [DS, 64], F32R)
    imask = SA("imask", [DS, 64], F32)

    powt = scr[0:DS, 0, :]
    rowsc = scr[0:1, 1, :]
    abv = scr[0:DS, 3, 0:4]
    krsb = pfreq[:, 0:1024]
    revr = mprod[:, 0:128]
    revi = mprod[:, 128:256]

    ps_a = stA.enter_context(nc.psum_tensor("ps_a", [128, 1024], F32))
    ps_b = stA.enter_context(nc.psum_tensor("ps_b", [128, 1024], F32))
    ps_c = stA.enter_context(nc.psum_tensor("ps_c", [128, 1024], F32))
    px1 = stA.enter_context(nc.psum_tensor("px1", [128, 1024], F32))

    b_win = Buf("win")
    b_xtm = [Buf(f"xtm{i}") for i in range(2)]
    b_xfm = [Buf(f"xfm{i}") for i in range(2)]
    b_xg = [Buf(f"xg{i}") for i in range(2)]
    b_scr = [Buf(f"scr{i}") for i in range(6)]
    b_khat = Buf("khat")
    b_xhim = Buf("xh_im")
    b_pfreq = [Buf("pfreq0"), Buf("pfreq1")]
    b_m1 = [Buf("mprod_0"), Buf("mprod_1")]
    b_m2 = [Buf("mprod2_0"), Buf("mprod2_1")]
    b_h = Buf("h")
    b_srow = Buf("srow")
    b_dsrow = Buf("dsrow")
    b_bc = Buf("bcast_rows")
    b_psa = [Buf("ps_a0"), Buf("ps_a1")]
    b_psb = [Buf("ps_b0"), Buf("ps_b1")]
    b_psc = [Buf("ps_c0"), Buf("ps_c1")]
    b_px1 = [Buf("px1_0"), Buf("px1_1")]
    b_rows = Buf("rowsc")

    EM("sp", lambda e: e.dma_start(iden[:], diden), writes=[b_tab], dma=True)
    EM("sp", lambda e: e.dma_start(idenr[:], didenr), writes=[b_tab], dma=True)
    EM("sp", lambda e: e.dma_start(onesr[:], dones), writes=[b_tab], dma=True)
    EM("sp", lambda e: e.dma_start(onesf[:], donesf), writes=[b_tab], dma=True)
    EM("sp", lambda e: e.dma_start(cre_sb[:], dcre), writes=[b_tab], dma=True)
    EM("sp", lambda e: e.dma_start(cim_sb[:], dcim), writes=[b_tab], dma=True)
    EM("sp", lambda e: e.dma_start(dft_sb[:], ddft), writes=[b_tab], dma=True)
    EM("sp", lambda e: e.dma_start(dftl_sb[:], ddftl), writes=[b_tab], dma=True)
    EM("sp", lambda e: e.dma_start(abv[:], dab), writes=[b_tab], dma=True)
    EM("sp", lambda e: e.dma_start(drow[:], ddre[0:1, :]), writes=[b_tab], dma=True)
    EM("sp", lambda e: e.dma_start(binrow[:], dbin), writes=[b_bc], dma=True)
    EM("sp", lambda e: e.dma_start(rw_sb[:], drw), writes=[b_tab], dma=True)
    EM("pool", lambda e: e.dma_start(win_sb[:], dwin), writes=[b_win], dma=True)
    EM("sp", lambda e: e.dma_start(x_tm[:, 0, :], dx_tm[0:128, :]),
       writes=[b_xtm[0]], dma=True)
    EM("sp", lambda e: e.dma_start(x_tm[:, 1, :], dx_tm[128:256, :]),
       writes=[b_xtm[1]], dma=True)
    EM("pool", lambda e: e.dma_start(x_fm[:, 0, :], dx_fm[0]),
       writes=[b_xfm[0]], dma=True)

    EM("sp", lambda e: e.dma_start(rowsc[:], dgam), writes=[b_rows], dma=True)
    for half in range(2):
        sl = slice(half * 512, half * 512 + 512)
        EM("pe", lambda e, sl=sl: e.matmul(ps_a[:, sl], onesf[:], rowsc[:, sl],
                                           start=True, stop=True),
           reads=[b_rows, b_tab], writes=[b_psa[half]])
    EM("dve", lambda e: e.tensor_copy(bgam[:], ps_a[:]), reads=b_psa, writes=[b_bc])
    EM("sp", lambda e: e.dma_start(rowsc[:], dbet), writes=[b_rows], dma=True)
    for half in range(2):
        sl = slice(half * 512, half * 512 + 512)
        EM("pe", lambda e, sl=sl: e.matmul(ps_b[:, sl], onesf[:], rowsc[:, sl],
                                           start=True, stop=True),
           reads=[b_rows, b_tab], writes=[b_psb[half]])
    EM("dve", lambda e: e.tensor_copy(bbet[:], ps_b[:]), reads=b_psb, writes=[b_bc])
    EM("sp", lambda e: e.dma_start(rowsc[:, 0:1], drb), writes=[b_rows], dma=True)
    EM("pe", lambda e: e.matmul(px1[:, 0:1], onesf[:], rowsc[:, 0:1],
                                start=True, stop=True),
       reads=[b_rows, b_tab], writes=[b_px1[0]])
    EM("dve", lambda e: e.tensor_copy(statA[:, 0, 15:16], px1[:, 0:1]),
       reads=[b_px1[0]], writes=[b_stat[0]])
    EM("dve", lambda e: e.tensor_copy(statA[:, 1, 15:16], px1[:, 0:1]),
       reads=[b_px1[0]], writes=[b_stat[1]])

    PRE = powt[:, 0:161]
    PIM = powt[:, 192:353]
    are, aim = abv[:, 0:1], abv[:, 1:2]
    bre, bim = abv[:, 2:3], abv[:, 3:4]

    def mkt(fn):
        EM("dve", fn, reads=[b_tab], writes=[b_tab])

    EM("dve", lambda e: e.memset(PRE[:, 0:1], 1.0), writes=[b_tab])
    EM("dve", lambda e: e.memset(PIM[:, 0:1], 0.0), writes=[b_tab])
    mkt(lambda e: e.tensor_copy(PRE[:, 1:2], are))
    mkt(lambda e: e.tensor_copy(PIM[:, 1:2], aim))
    have = 2
    while have < 161:
        n = min(have - 1, 161 - have)
        s_re = PRE[:, have - 1:have]
        s_im = PIM[:, have - 1:have]
        pr1 = PRE[:, 1:1 + n]
        pi1 = PIM[:, 1:1 + n]
        dst_r = PRE[:, have:have + n]
        dst_i = PIM[:, have:have + n]
        t0 = powt[:, 354:354 + n]
        mkt(lambda e, a=dst_r, b=pr1, s=s_re: e.tensor_scalar(a, b, s, None, op.mult))
        mkt(lambda e, a=t0, b=pi1, s=s_im: e.tensor_scalar(a, b, s, None, op.mult))
        mkt(lambda e, a=dst_r, b=t0: e.tensor_tensor(a, a, b, op.subtract))
        mkt(lambda e, a=dst_i, b=pr1, s=s_im: e.tensor_scalar(a, b, s, None, op.mult))
        mkt(lambda e, a=t0, b=pi1, s=s_re: e.tensor_scalar(a, b, s, None, op.mult))
        mkt(lambda e, a=dst_i, b=t0: e.tensor_tensor(a, a, b, op.add))
        have += n

    t1 = powt[:, 354:482]
    mkt(lambda e: e.tensor_scalar(pbre[0:DS, :], PRE[:, 0:128], bre, None, op.mult))
    mkt(lambda e: e.tensor_scalar(t1, PIM[:, 0:128], bim, None, op.mult))
    mkt(lambda e: e.scalar_tensor_tensor(pbre[0:DS, :], pbre[0:DS, :].bitcast(F32),
                                         1.0, t1, op.mult, op.subtract))
    EM("dve", lambda e: e.memset(pbre[DS:DS + 1, :].bitcast(F32), 1.0), writes=[b_tab])
    mkt(lambda e: e.tensor_scalar(npbim[:], PRE[:, 0:128], bim, None, op.mult))
    mkt(lambda e: e.tensor_scalar(t1, PIM[:, 0:128], bre, None, op.mult))
    mkt(lambda e: e.scalar_tensor_tensor(npbim[:], npbim[:].bitcast(F32), -1.0, t1,
                                         op.mult, op.subtract))
    mkt(lambda e: e.tensor_copy(revr[:], pbre[0:DS, ::-1].bitcast(F32)))
    mkt(lambda e: e.tensor_scalar(revi[:], npbim[:, ::-1].bitcast(F32), -1.0, None,
                                  op.mult))
    EM("pe", lambda e: e.transpose(ps_a[:, 0:64].bitcast(F32R), revr[:], idenr[0:64, 0:64]),
       reads=[b_tab], writes=[b_psa[0]])
    EM("pe", lambda e: e.transpose(ps_a[:, 512:576].bitcast(F32R), revi[:], idenr[0:64, 0:64]),
       reads=[b_tab], writes=[b_psa[1]])
    EM("dve", lambda e: e.tensor_copy(prevtr[:], ps_a[:, 0:64].bitcast(F32R)),
       reads=[b_psa[0]], writes=[b_tab])
    EM("dve", lambda e: e.tensor_copy(prevti[:], ps_a[:, 512:576].bitcast(F32R)),
       reads=[b_psa[1]], writes=[b_tab])
    mkt(lambda e: e.tensor_copy(qre[:], PRE[:, 1:129]))
    mkt(lambda e: e.tensor_scalar(nqre[:], PRE[:, 1:129], -1.0, None, op.mult))
    mkt(lambda e: e.tensor_scalar(nqim[:], PIM[:, 1:129], -1.0, None, op.mult))
    EM("pool", lambda e: e.iota(imask[:], [[1, 64]], base=0, channel_multiplier=-1,
                                allow_small_or_imprecise_dtypes=True),
       writes=[b_tab])
    mkt(lambda e: e.tensor_scalar(imask[:], imask[:], 0.0, None, op.is_equal))
    atre = PRE[:, 128:129]
    atim = PIM[:, 128:129]
    mkt(lambda e: e.tensor_scalar(dgatre[:], imask[:], atre, None, op.mult))
    mkt(lambda e: e.tensor_scalar(dgatim[:], imask[:], atim, None, op.mult))
    mkt(lambda e: e.tensor_scalar(dgnatim[:], imask[:], atim, -1.0, op.mult, op.mult))

    for half in range(2):
        sl = slice(half * 512, half * 512 + 512)
        EM("pe", lambda e, sl=sl: e.matmul(ps_b[:, sl], pbre[:], cre_sb[:, sl],
                                           start=True, stop=False),
           reads=[b_tab], writes=[b_psb[half]])
        EM("pe", lambda e, sl=sl: e.matmul(ps_b[:, sl], npbim[:], cim_sb[:, sl],
                                           start=False, stop=True),
           reads=[b_tab], writes=[b_psb[half]])
    EM("act", lambda e: e.copy(krsb[:], ps_b[:]), reads=b_psb, writes=[b_tab])
    for half in range(2):
        sl = slice(half * 512, half * 512 + 512)
        EM("pe", lambda e, sl=sl: e.matmul(ps_c[:, sl], dft_sb[:, 0:128],
                                           krsb[:, sl], start=True, stop=False),
           reads=[b_tab], writes=[b_psc[half]])
        EM("pe", lambda e, sl=sl: e.matmul(ps_c[:, sl], dftl_sb[:, 0:128],
                                           krsb[:, sl], start=False, stop=True),
           reads=[b_tab], writes=[b_psc[half]])
    EM("act", lambda e: e.copy(khat[:, 0:1024], ps_c[:]), reads=b_psc,
       writes=[b_khat])
    for half in range(2):
        sl = slice(half * 512, half * 512 + 512)
        EM("pe", lambda e, sl=sl: e.matmul(ps_c[:, sl], dft_sb[:, 128:256],
                                           krsb[:, sl], start=True, stop=False),
           reads=[b_tab], writes=[b_psc[half]])
        EM("pe", lambda e, sl=sl: e.matmul(ps_c[:, sl], dftl_sb[:, 128:256],
                                           krsb[:, sl], start=False, stop=True),
           reads=[b_tab], writes=[b_psc[half]])
    EM("act", lambda e: e.copy(khat[:, 1024:2048], ps_c[:]), reads=b_psc,
       writes=[b_khat])

    EM("dve", lambda e: e.memset(epsc[:], LN_EPS), writes=[b_tab])
    EM("dve", lambda e: e.memset(w8n[:], 0.0), writes=list(b_w8))
    EM("dve", lambda e: e.memset(h_re[:].bitcast(F32), 0.0), writes=[b_h])
    EM("dve", lambda e: e.memset(h_im[:].bitcast(F32), 0.0), writes=[b_h])
    EM("dve", lambda e: e.memset(srow[:], 0.0), writes=[b_srow])
    P.fence()

    (C_SUM0, C_SUM1, C_SUM, C_MU, C_SSQ, C_VAR, C_RST, C_NMX, C_SEXP, C_RSUM,
     C_CGT, C_THR, C_WSUM, C_WINV, C_TMP, C_RB) = range(16)

    KRE = khat[:, 0:1024]
    KIM = khat[:, 1024:2048]

    def _env(c):
        p = c % 2

        class E:
            pass

        v = E()
        v.p = p
        v.sc = lambda i, p=p: statA[:, p, i:i + 1]
        v.s0 = scr[:, 2 * p + 0, :]
        v.s1 = scr[:, 2 * p + 1, :]
        v.s2 = scr[:, 4, :]
        v.b_s0, v.b_s1 = b_scr[2 * p:2 * p + 2]
        v.b_s2 = b_scr[4]
        v.b_st = b_stat[p]
        v.xg_c = xg[:, p, :]
        v.xtm_c = x_tm[:, p, :]
        v.xfm_c = x_fm[:, p, :]
        v.ds_c = dsrow[:]
        v.b_ds = b_dsrow
        return v

    def emit_x1(c):
        v = _env(c)
        if c + 1 < NCH:
            pn = (c + 1) % 2
            EM("pool", lambda e, c=c, pn=pn: e.dma_start(
                x_fm[:, pn, :], dx_fm[c + 1]), writes=[b_xfm[pn]], dma=True)
        for half in range(2):
            sl = slice(half * 512, half * 512 + 512)
            for kt in range(8):
                EM("pe", lambda e, sl=sl, kt=kt, xfm_c=v.xfm_c: e.matmul(
                    px1[:, sl], xfm_c[:, kt * 128:kt * 128 + 128],
                    win_sb[:, kt, sl], start=(kt == 0), stop=False),
                   reads=[b_xfm[v.p], b_win], writes=[b_px1[half]])
            EM("pe", lambda e, sl=sl: e.matmul(
                px1[:, sl], onesf[:], binrow[:, sl], start=False, stop=True),
               reads=[b_bc], writes=[b_px1[half]])

    def emit_ln1(c):
        v = _env(c)
        s0, s1, sc = v.s0, v.s1, v.sc
        b_s0, b_s1, b_st = v.b_s0, v.b_s1, v.b_st
        EM("dve", lambda e, s0=s0, sc=sc: e.tensor_scalar(
            s0, px1[:], 0.0, None, op.add, op.add, accum_out=sc(C_SUM)),
           reads=b_px1, writes=[b_s0, b_st], touch=sc(C_SUM))
        if DEV:
            EM("sp", lambda e, c=c, s0=s0: e.dma_start(
                dbg_x1[c * 128:(c + 1) * 128, :], s0), reads=[b_s0], dma=True)
        EM("act", lambda e, s0=s0, s1=s1, sc=sc: e.activation(
            s1, s0, AF.Square, accum_out=sc(C_SSQ)),
           reads=[b_s0], writes=[b_s1, b_st], touch=sc(C_SSQ))
        EM("dve", lambda e, sc=sc: e.tensor_scalar(sc(C_MU), sc(C_SUM), 1.0 / D,
                                                   None, op.mult),
           reads=[b_st], writes=[b_st])
        EM("dve", lambda e, sc=sc: e.tensor_scalar(sc(C_TMP), sc(C_MU), sc(C_MU),
                                                   None, op.mult),
           reads=[b_st], writes=[b_st])
        EM("dve", lambda e, sc=sc: e.tensor_scalar(sc(C_VAR), sc(C_SSQ), 1.0 / D,
                                                   sc(C_TMP), op.mult, op.subtract),
           reads=[b_st], writes=[b_st])
        EM("act", lambda e, sc=sc: e.activation(sc(C_TMP), sc(C_VAR), AF.Ln,
                                                bias=epsc[:, 0:1]),
           reads=[b_st], writes=[b_st])
        EM("act", lambda e, sc=sc: e.activation(sc(C_RST), sc(C_TMP), AF.Exp,
                                                scale=-0.5),
           reads=[b_st], writes=[b_st])
        EM("dve", lambda e, s0=s0, s1=s1, sc=sc: e.tensor_scalar(
            s1, s0, sc(C_MU), sc(C_RST), op.subtract, op.mult),
           reads=[b_s0, b_st], writes=[b_s1])
        EM("pool", lambda e, s0=s0, s1=s1: e.tensor_tensor(s0, s1, bgam[:],
                                                           op.mult),
           reads=[b_s1, b_bc], writes=[b_s0])
        EM("pool", lambda e, s0=s0, xg_c=v.xg_c: e.tensor_tensor(
            xg_c, s0, bbet[:], op.add),
           reads=[b_s0, b_bc], writes=[b_xg[v.p]])
        if DEV:
            EM("sp", lambda e, c=c, xg_c=v.xg_c: e.dma_start(
                dbg_xg[c * 128:(c + 1) * 128, :], xg_c.bitcast(F32)),
               reads=[b_xg[v.p]], dma=True)

    def emit_dft(c):
        v = _env(c)
        for half in range(2):
            sl = slice(half * 512, half * 512 + 512)
            EM("pe", lambda e, sl=sl, xg_c=v.xg_c: e.matmul(
                ps_a[:, sl], dft_sb[:, 0:128], xg_c[:, sl], start=True,
                stop=False),
               reads=[b_xg[v.p], b_tab], writes=[b_psa[half]])
            EM("pe", lambda e, sl=sl, xg_c=v.xg_c: e.matmul(
                ps_a[:, sl], dftl_sb[:, 0:128], xg_c[:, sl], start=False,
                stop=True),
               reads=[b_xg[v.p], b_tab], writes=[b_psa[half]])
            EM("pe", lambda e, sl=sl, xg_c=v.xg_c: e.matmul(
                ps_b[:, sl], dft_sb[:, 128:256], xg_c[:, sl], start=True,
                stop=False),
               reads=[b_xg[v.p], b_tab], writes=[b_psb[half]])
            EM("pe", lambda e, sl=sl, xg_c=v.xg_c: e.matmul(
                ps_b[:, sl], dftl_sb[:, 128:256], xg_c[:, sl], start=False,
                stop=True),
               reads=[b_xg[v.p], b_tab], writes=[b_psb[half]])

    def emit_prod(c):
        v = _env(c)
        s2, sc = v.s2, v.sc
        b_s2, b_st = v.b_s2, v.b_st
        EM("dve", lambda e, ds_c=v.ds_c: e.tensor_tensor(ds_c, drow[:], srow[:],
                                                         op.mult),
           reads=[b_srow, b_tab], writes=[v.b_ds])
        EM("dve", lambda e: e.tensor_tensor(srow[:], srow[:], ps_a[0:1, :],
                                            op.add),
           reads=b_psa, writes=[b_srow])
        EM("act", lambda e: e.copy(xh_im[:], ps_b[:]), reads=b_psb,
           writes=[b_xhim])
        for half in range(2):
            sl = slice(half * 512, half * 512 + 512)
            s2h = s2[:, sl]
            PRE_F = pfreq[:, half * 1024:half * 1024 + 512]
            PIM_F = pfreq[:, half * 1024 + 512:half * 1024 + 1024]
            M_RR = mprod[:, half * 1024:half * 1024 + 512]
            M_II = mprod[:, half * 1024 + 512:half * 1024 + 1024]
            M_IR = mprod2[:, half * 1024:half * 1024 + 512]
            M_RI = mprod2[:, half * 1024 + 512:half * 1024 + 1024]
            EM("dve", lambda e, sl=sl, PRE_F=PRE_F: e.tensor_tensor(
                PRE_F, ps_a[:, sl], KRE[:, sl].bitcast(F32), op.mult),
               reads=[b_psa[half], b_khat], writes=[b_pfreq[half]])
            EM("dve", lambda e, sl=sl, s2h=s2h: e.tensor_tensor(
                s2h, xh_im[:, sl], KIM[:, sl].bitcast(F32), op.mult),
               reads=[b_xhim, b_khat], writes=[b_s2])
            EM("dve", lambda e, s2h=s2h, PRE_F=PRE_F: e.tensor_tensor(
                PRE_F, PRE_F.bitcast(F32), s2h, op.subtract),
               reads=[b_s2], writes=[b_pfreq[half]])
            EM("dve", lambda e, sl=sl, PIM_F=PIM_F: e.tensor_tensor(
                PIM_F, ps_a[:, sl], KIM[:, sl].bitcast(F32), op.mult),
               reads=[b_psa[half], b_khat], writes=[b_pfreq[half]])
            EM("dve", lambda e, sl=sl, s2h=s2h: e.tensor_tensor(
                s2h, xh_im[:, sl], KRE[:, sl].bitcast(F32), op.mult),
               reads=[b_xhim, b_khat], writes=[b_s2])
            EM("dve", lambda e, s2h=s2h, PIM_F=PIM_F: e.tensor_tensor(
                PIM_F, PIM_F.bitcast(F32), s2h, op.add),
               reads=[b_s2], writes=[b_pfreq[half]])

    def emit_eprod(c):
        v = _env(c)
        for half in range(2):
            sl = slice(half * 512, half * 512 + 512)
            M_RR = mprod[:, half * 1024:half * 1024 + 512]
            M_II = mprod[:, half * 1024 + 512:half * 1024 + 1024]
            M_IR = mprod2[:, half * 1024:half * 1024 + 512]
            M_RI = mprod2[:, half * 1024 + 512:half * 1024 + 1024]
            EM("pool", lambda e, sl=sl, M_RR=M_RR: e.tensor_tensor(
                M_RR, cre_sb[0:DS, sl].bitcast(F32), h_re[:, sl].bitcast(F32),
                op.mult), reads=[b_h, b_tab], writes=[b_m1[half]])
            EM("pool", lambda e, sl=sl, M_II=M_II: e.tensor_tensor(
                M_II, cim_sb[:, sl].bitcast(F32), h_im[:, sl].bitcast(F32),
                op.mult), reads=[b_h, b_tab], writes=[b_m1[half]])
            EM("pool", lambda e, sl=sl, M_IR=M_IR: e.tensor_tensor(
                M_IR, cim_sb[:, sl].bitcast(F32), h_re[:, sl].bitcast(F32),
                op.mult), reads=[b_h, b_tab], writes=[b_m2[half]])
            EM("pool", lambda e, sl=sl, M_RI=M_RI: e.tensor_tensor(
                M_RI, cre_sb[0:DS, sl].bitcast(F32), h_im[:, sl].bitcast(F32),
                op.mult), reads=[b_h, b_tab], writes=[b_m2[half]])

    def emit_carry(c):
        v = _env(c)
        for half in range(2):
            sl = slice(half * 512, half * 512 + 512)
            PRE_F = pfreq[:, half * 1024:half * 1024 + 512]
            PIM_F = pfreq[:, half * 1024 + 512:half * 1024 + 1024]
            M_RR = mprod[:, half * 1024:half * 1024 + 512]
            M_II = mprod[:, half * 1024 + 512:half * 1024 + 1024]
            M_IR = mprod2[:, half * 1024:half * 1024 + 512]
            M_RI = mprod2[:, half * 1024 + 512:half * 1024 + 1024]
            EM("pe", lambda e, sl=sl, ds_c=v.ds_c: e.matmul(
                ps_c[:, sl], onesr[:], ds_c[:, sl], start=True, stop=False),
               reads=[v.b_ds, b_tab], writes=[b_psc[half]])
            EM("pe", lambda e, sl=sl, PRE_F=PRE_F: e.matmul(
                ps_c[:, sl], dft_sb[:, 256:384], PRE_F, start=False, stop=False),
               reads=[b_pfreq[half], b_tab], writes=[b_psc[half]])
            EM("pe", lambda e, sl=sl, PRE_F=PRE_F: e.matmul(
                ps_c[:, sl], dftl_sb[:, 256:384], PRE_F, start=False,
                stop=False),
               reads=[b_pfreq[half], b_tab], writes=[b_psc[half]])
            EM("pe", lambda e, sl=sl, PIM_F=PIM_F: e.matmul(
                ps_c[:, sl], dft_sb[:, 384:512], PIM_F, start=False, stop=False),
               reads=[b_pfreq[half], b_tab], writes=[b_psc[half]])
            EM("pe", lambda e, sl=sl, PIM_F=PIM_F: e.matmul(
                ps_c[:, sl], dftl_sb[:, 384:512], PIM_F, start=False,
                stop=False),
               reads=[b_pfreq[half], b_tab], writes=[b_psc[half]])
            EM("pe", lambda e, sl=sl, M_RR=M_RR: e.matmul(
                ps_c[:, sl], qre[:], M_RR, start=False, stop=False),
               reads=[b_m1[half], b_tab], writes=[b_psc[half]])
            EM("pe", lambda e, sl=sl, M_II=M_II: e.matmul(
                ps_c[:, sl], nqre[:], M_II, start=False, stop=False),
               reads=[b_m1[half], b_tab], writes=[b_psc[half]])
            EM("pe", lambda e, sl=sl, M_IR=M_IR: e.matmul(
                ps_c[:, sl], nqim[:], M_IR, start=False, stop=False),
               reads=[b_m2[half], b_tab], writes=[b_psc[half]])
            EM("pe", lambda e, sl=sl, M_RI=M_RI: e.matmul(
                ps_c[:, sl], nqim[:], M_RI, start=False, stop=True),
               reads=[b_m2[half], b_tab], writes=[b_psc[half]])

    def emit_postconv(c):
        v = _env(c)
        s0, s1, s2, sc = v.s0, v.s1, v.s2, v.sc
        b_s0, b_s1, b_s2, b_st = v.b_s0, v.b_s1, v.b_s2, v.b_st
        if DEV:
            EM("dve", lambda e, s2=s2: e.tensor_copy(s2, ps_c[:]), reads=b_psc,
               writes=[b_s2])
            EM("sp", lambda e, c=c, s2=s2: e.dma_start(
                dbg_y[c * 128:(c + 1) * 128, :], s2), reads=[b_s2], dma=True)
        EM("dve", lambda e, s0=s0, xtm_c=v.xtm_c: e.tensor_tensor(
            s0, ps_c[:], xtm_c, op.add),
           reads=b_psc + [b_xtm[v.p]], writes=[b_s0])
        EM("dve", lambda e, s0=s0, s1=s1, xg_c=v.xg_c, sc=sc:
           e.scalar_tensor_tensor(
               s1, s0, 0.0, xg_c.bitcast(F32), op.add, op.add,
               accum_out=sc(C_SUM)),
           reads=[b_s0, b_xg[v.p]], writes=[b_s1, b_st], touch=sc(C_SUM))
        EM("act", lambda e, s0=s0, s1=s1, sc=sc: e.activation(
            s0, s1, AF.Square, accum_out=sc(C_SSQ)),
           reads=[b_s1], writes=[b_s0, b_st], touch=sc(C_SSQ))
        EM("dve", lambda e, sc=sc: e.tensor_scalar(sc(C_MU), sc(C_SUM), 1.0 / D,
                                                   None, op.mult),
           reads=[b_st], writes=[b_st])
        EM("dve", lambda e, sc=sc: e.tensor_scalar(sc(C_TMP), sc(C_MU), sc(C_MU),
                                                   None, op.mult),
           reads=[b_st], writes=[b_st])
        EM("dve", lambda e, sc=sc: e.tensor_scalar(sc(C_VAR), sc(C_SSQ), 1.0 / D,
                                                   sc(C_TMP), op.mult,
                                                   op.subtract),
           reads=[b_st], writes=[b_st])
        EM("act", lambda e, sc=sc: e.activation(sc(C_TMP), sc(C_VAR), AF.Ln,
                                                bias=epsc[:, 0:1]),
           reads=[b_st], writes=[b_st])
        EM("act", lambda e, sc=sc: e.activation(sc(C_RST), sc(C_TMP), AF.Exp,
                                                scale=-0.5),
           reads=[b_st], writes=[b_st])
        EM("dve", lambda e, c=c, s1=s1, sc=sc: e.tensor_scalar(
            out1_tm[:, c, :], s1, sc(C_MU), sc(C_RST), op.subtract, op.mult),
           reads=[b_s1, b_st], writes=[b_out1[c]])
        if c + 2 < NCH:
            EM("sp", lambda e, c=c, p=v.p: e.dma_start(
                x_tm[:, p, :], dx_tm[(c + 2) * 128:(c + 3) * 128, :]),
               writes=[b_xtm[v.p]], dma=True)

    def emit_h(c):
        v = _env(c)
        for half in range(2):
            sl = slice(half * 512, half * 512 + 512)
            EM("pe", lambda e, sl=sl, xg_c=v.xg_c: e.matmul(
                ps_b[0:DS, sl], prevtr[:], xg_c[:, sl], start=True, stop=False),
               reads=[b_xg[v.p], b_tab], writes=[b_psb[half]])
            EM("pe", lambda e, sl=sl: e.matmul(
                ps_b[0:DS, sl], dgatre[:], h_re[:, sl], start=False, stop=False),
               reads=[b_h, b_tab], writes=[b_psb[half]])
            EM("pe", lambda e, sl=sl: e.matmul(
                ps_b[0:DS, sl], dgnatim[:], h_im[:, sl], start=False, stop=True),
               reads=[b_h, b_tab], writes=[b_psb[half]])
            EM("pe", lambda e, sl=sl, xg_c=v.xg_c: e.matmul(
                ps_c[0:DS, sl], prevti[:], xg_c[:, sl], start=True, stop=False),
               reads=[b_xg[v.p], b_tab], writes=[b_psc[half]])
            EM("pe", lambda e, sl=sl: e.matmul(
                ps_c[0:DS, sl], dgatre[:], h_im[:, sl], start=False, stop=False),
               reads=[b_h, b_tab], writes=[b_psc[half]])
            EM("pe", lambda e, sl=sl: e.matmul(
                ps_c[0:DS, sl], dgatim[:], h_re[:, sl], start=False, stop=True),
               reads=[b_h, b_tab], writes=[b_psc[half]])
        EM("act", lambda e: e.copy(h_re[:], ps_b[0:DS, :]), reads=b_psb,
           writes=[b_h])
        EM("act", lambda e: e.copy(h_im[:], ps_c[0:DS, :]), reads=b_psc,
           writes=[b_h])

    emit_x1(0)
    emit_ln1(0)
    for r in range(NCH):
        emit_dft(r)
        emit_eprod(r)
        emit_prod(r)
        if r + 1 < NCH:
            emit_x1(r + 1)
        emit_carry(r)
        if r + 1 < NCH:
            emit_ln1(r + 1)
        emit_postconv(r)
        emit_h(r)

    if DEV:
        for c in range(NCH):
            EM("sp", lambda e, c=c: e.dma_start(dbg_o1[c * 128:(c + 1) * 128, :],
                                                out1_tm[:, c, :]),
               reads=[b_out1[c]], dma=True)

    P.fence()
    stA.close()

    stB = contextlib.ExitStack()

    def SB(name, shape, dt):
        return stB.enter_context(nc.sbuf_tensor(name, shape, dt))

    out1_fm = SB("out1_fm", [128, 8, 2048], BF16)
    b_o1fm = [Buf(f"o1fm_{c}") for c in range(NCH)]
    gup_sb = SB("gup_sb", [128, 2, 8192], BF16)
    down_sb = SB("down_sb", [128, 2, 4096], F32R)
    a_sl = SB("a_sl", [128, 2, 4, 512], F32R)
    sg_sb = SB("sg_sb", [128, 4, 512], F32)
    bounce = SB("bounce", [128, 2, 1024], F32)
    statB = SB("statB", [128, 64], F32)

    pg = stB.enter_context(nc.psum_tensor("pg", [128, 2048], F32))
    pu = stB.enter_context(nc.psum_tensor("pu", [128, 2048], F32))

    b_gup = [Buf(f"gup{i}") for i in range(2)]
    b_down = [Buf(f"down{i}") for i in range(2)]
    b_asl = [Buf(f"asl{i}") for i in range(2)]
    b_sg = [Buf(f"sg{i}") for i in range(4)]
    b_bnc = [Buf(f"bnc{i}") for i in range(2)]
    b_statB = Buf("statB")
    b_pg = [Buf(f"pg{i}") for i in range(4)]
    b_pu = [Buf(f"pu{i}") for i in range(4)]
    b_outs = [Buf(f"outs{c}") for c in range(NCH)]

    s2b = SB("s2b", [128, 2, 1024], F32)
    b_s2b = [Buf("s2b0"), Buf("s2b1")]
    (C_SUM0, C_SUM1, C_SUM, C_MU, C_SSQ, C_VAR, C_RST, C_NMX, C_SEXP, C_RSUM,
     C_CGT, C_THR, C_WSUM, C_WINV, C_TMP, C_RB) = range(16)

    def emit_tp(c):
        p = c % 2
        tp = pg if (p == 0) else pu
        b_tp = b_pg if (p == 0) else b_pu
        for kt in range(8):
            EM("pe", lambda e, kt=kt, c=c, tp=tp: e.transpose(
                tp[:, kt * 128:kt * 128 + 128],
                out1_tm[:, c, kt * 128:kt * 128 + 128], iden[:]),
               reads=[b_out1[c], b_tab], writes=[b_tp[kt // 4]])
        r_act = EM("act", lambda e, c=c, tp=tp: e.copy(
            out1_fm[:, :, c * 128:c * 128 + 128], tp[:, 0:1024]),
           reads=[b_tp[0], b_tp[1]], writes=[b_o1fm[c]])
        s2 = s2b[:, p, :]
        EM("act", lambda e, s2=s2, tp=tp: e.copy(s2, tp[:, 0:1024]),
           reads=[b_tp[0], b_tp[1]], writes=[b_s2b[p]], extra_waits=[r_act])

    def emit_router(c):
        p = c % 2
        otp = pu if (p == 0) else pg
        b_otp = b_pu if (p == 0) else b_pg
        s2 = s2b[:, p, :]
        pr_c = otp[:, 2040:2048]
        b_pr = b_otp[3]

        def sc(i, p=p):
            return statA[:, p, i:i + 1]

        for kt in range(8):
            EM("pe", lambda e, kt=kt, s2=s2, pr_c=pr_c: e.matmul(
                pr_c, s2[:, kt * 128:kt * 128 + 128], rw_sb[:, kt, :],
                start=(kt == 0), stop=(kt == 7)),
               reads=[b_s2b[p], b_tab], writes=[b_pr])
        LG = statA[:, p, 16:24]
        SCR8 = statA[:, p, 24:32]
        SORT8 = statA[:, p, 0:8]
        TH = statA[:, p, 3:4]
        EQ = statA[:, p, 32:40]
        MQ = w8n[:, c, :]
        b_st = b_stat[p]
        EM("dve", lambda e, sc=sc, pr_c=pr_c: e.tensor_reduce(
            sc(C_NMX), pr_c, AX.X, op.max, negate=True),
           reads=[b_pr], writes=[b_st])
        EM("act", lambda e, LG=LG, sc=sc, pr_c=pr_c: e.activation(
            LG, pr_c, AF.Exp, bias=sc(C_NMX), accum_out=sc(C_SEXP)),
           reads=[b_pr, b_st], writes=[b_st], touch=sc(C_SEXP))
        EM("dve", lambda e, sc=sc: e.reciprocal(sc(C_RSUM), sc(C_SEXP)),
           reads=[b_st], writes=[b_st])
        EM("dve", lambda e, LG=LG, sc=sc: e.tensor_scalar(LG, LG, sc(C_RSUM), None,
                                                          op.mult),
           reads=[b_st], writes=[b_st])
        EM("dve", lambda e, LG=LG, SCR8=SCR8, sc=sc: e.tensor_scalar(
            SCR8, LG, sc(C_RB), None, op.add),
           reads=[b_st], writes=[b_st])
        EM("dve", lambda e, SORT8=SORT8, SCR8=SCR8: e.max(SORT8, SCR8),
           reads=[b_st], writes=[b_st])
        EM("dve", lambda e, MQ=MQ, SCR8=SCR8, TH=TH, sc=sc: e.tensor_scalar(
            MQ, SCR8, TH, None, op.is_gt, op.add, accum_out=sc(C_CGT)),
           reads=[b_st], writes=[b_w8[c], b_st], touch=sc(C_CGT))
        EM("dve", lambda e, sc=sc: e.tensor_scalar(sc(C_THR), sc(C_CGT), -1.0, 4.0,
                                                   op.mult, op.add),
           reads=[b_st], writes=[b_st])
        EM("dve", lambda e, EQ=EQ, SCR8=SCR8, TH=TH: e.tensor_scalar(
            EQ, SCR8, TH, None, op.is_equal),
           reads=[b_st], writes=[b_st])
        EM("dve", lambda e, SCR8=SCR8: e.tensor_scalar(SCR8, SCR8, 0.0, 1.0,
                                                       op.mult, op.add),
           reads=[b_st], writes=[b_st])
        EM("dve", lambda e, SCR8=SCR8, EQ=EQ: e.tensor_tensor_scan(
            SCR8, SCR8, EQ, 0.0, op.mult, op.add),
           reads=[b_st], writes=[b_st])
        EM("dve", lambda e, SCR8=SCR8, EQ=EQ: e.tensor_tensor(SCR8, SCR8, EQ,
                                                              op.subtract),
           reads=[b_st], writes=[b_st])
        EM("dve", lambda e, SCR8=SCR8, sc=sc: e.tensor_scalar(
            SCR8, SCR8, sc(C_THR), None, op.is_lt),
           reads=[b_st], writes=[b_st])
        EM("dve", lambda e, EQ=EQ, SCR8=SCR8: e.tensor_tensor(EQ, EQ, SCR8,
                                                              op.mult),
           reads=[b_st], writes=[b_st])
        EM("dve", lambda e, MQ=MQ, EQ=EQ: e.tensor_tensor(MQ, MQ, EQ, op.add),
           reads=[b_st], writes=[b_w8[c]])
        EM("dve", lambda e, MQ=MQ, LG=LG: e.tensor_tensor(MQ, MQ, LG, op.mult),
           reads=[b_st], writes=[b_w8[c]])
        EM("dve", lambda e, MQ=MQ, EQ=EQ, sc=sc: e.tensor_scalar(
            EQ, MQ, 1.0, None, op.mult, op.add, accum_out=sc(C_WSUM)),
           reads=[b_w8[c]], writes=[b_st], touch=sc(C_WSUM))
        EM("dve", lambda e, sc=sc: e.tensor_scalar(sc(C_WSUM), sc(C_WSUM), 1e-20,
                                                   None, op.add),
           reads=[b_st], writes=[b_st])
        EM("dve", lambda e, sc=sc: e.reciprocal(sc(C_WINV), sc(C_WSUM)),
           reads=[b_st], writes=[b_st])
        EM("dve", lambda e, MQ=MQ, sc=sc: e.tensor_scalar(MQ, MQ, sc(C_WINV), None,
                                                          op.mult),
           reads=[b_st], writes=[b_w8[c]])

    emit_tp(0)
    for c in range(NCH):
        if c + 1 < NCH:
            emit_tp(c + 1)
        emit_router(c)
    if DEV:
        EM("sp", lambda e: e.dma_start(dbg_w8, w8n[:]), reads=list(b_w8), dma=True)

    for e0 in range(2):
        EM("pool", lambda e, e0=e0: e.dma_start(gup_sb[:, e0, :], dgup[e0]),
           writes=[b_gup[e0]], dma=True)
        EM("sp", lambda e, e0=e0: e.dma_start(down_sb[:, e0, :], ddown[e0]),
           writes=[b_down[e0]], dma=True)

    NCHB = 4
    for ex in range(NEXP):
        wslot = ex % 2
        last = (ex == NEXP - 1)
        for cb in range(NCHB):
            cpar = cb % 2
            tok = slice(cb * 512, cb * 512 + 512)
            PGX, b_pgx = (pg, b_pg) if cpar == 0 else (pu, b_pu)
            PUX, b_pux = (pu, b_pu) if cpar == 0 else (pg, b_pg)
            fmc = [b_o1fm[t] for t in range(cb * 4, cb * 4 + 4)]
            for it in range(4):
                psl = PGX[:, it * 512:(it + 1) * 512]
                for kt in range(8):
                    EM("pe", lambda e, psl=psl, kt=kt, it=it, tok=tok, w=wslot:
                       e.matmul(psl,
                                gup_sb[:, w, kt * 512 + it * 128:
                                       kt * 512 + it * 128 + 128],
                                out1_fm[:, kt, tok],
                                start=(kt == 0), stop=(kt == 7)),
                       reads=[b_gup[wslot]] + fmc, writes=[b_pgx[it]])
                EM("act", lambda e, psl=psl, it=it: e.activation(
                    sg_sb[:, it, :], psl, AF.Silu),
                   reads=[b_pgx[it]], writes=[b_sg[it]])
                pul = PUX[:, it * 512:(it + 1) * 512]
                for kt in range(8):
                    EM("pe", lambda e, pul=pul, kt=kt, it=it, tok=tok, w=wslot:
                       e.matmul(pul,
                                gup_sb[:, w, 4096 + kt * 512 + it * 128:
                                       4096 + kt * 512 + it * 128 + 128],
                                out1_fm[:, kt, tok],
                                start=(kt == 0), stop=(kt == 7)),
                       reads=[b_gup[wslot]] + fmc, writes=[b_pux[it]])
                EM("dve", lambda e, pul=pul, it=it, cpar=cpar: e.tensor_tensor(
                    a_sl[:, cpar, it, :], sg_sb[:, it, :], pul, op.mult),
                   reads=[b_sg[it], b_pux[it]], writes=[b_asl[cpar]])
            for lt in range(4):
                c = cb * 4 + lt
                lsl = slice(lt * 128, lt * 128 + 128)
                dbase = (lt % 2) * 1024
                for half in range(2):
                    dsl = PGX[:, dbase + half * 512:dbase + half * 512 + 512]
                    bank = (dbase + half * 512) // 512
                    for it in range(4):
                        EM("pe", lambda e, dsl=dsl, it=it, lsl=lsl, half=half,
                           w=wslot, cpar=cpar: e.matmul(
                               dsl, a_sl[:, cpar, it, lsl],
                               down_sb[:, w, it * 1024 + half * 512:
                                       it * 1024 + half * 512 + 512],
                               start=(it == 0), stop=(it == 3)),
                           reads=[b_asl[cpar], b_down[wslot]],
                           writes=[b_pgx[bank]])
                dfull = PGX[:, dbase:dbase + 1024]
                bk0, bk1 = dbase // 512, dbase // 512 + 1
                if ex < 8:
                    EM("dve", lambda e, dfull=dfull, c=c, ex=ex:
                       e.scalar_tensor_tensor(
                           out1_tm[:, c, :], dfull, w8n[:, c, ex:ex + 1],
                           out1_tm[:, c, :], op.mult, op.add),
                       reads=[b_pgx[bk0], b_pgx[bk1], b_w8[c], b_out1[c]],
                       writes=[b_out1[c]])
                else:
                    EM("dve", lambda e, dfull=dfull, c=c: e.tensor_tensor(
                        out1_tm[:, c, :], out1_tm[:, c, :], dfull, op.add),
                       reads=[b_pgx[bk0], b_pgx[bk1], b_out1[c]],
                       writes=[b_out1[c]])
                if last:
                    EM("dve", lambda e, c=c: e.tensor_scalar(
                        bounce[:, 0, :], out1_tm[:, c, :], 0.0, None, op.add,
                        op.add, accum_out=statB[:, 48:49]),
                       reads=[b_out1[c]], writes=[b_bnc[0], b_statB],
                       touch=statB[:, 48:49])
                    EM("act", lambda e, c=c: e.activation(
                        bounce[:, 1, :], out1_tm[:, c, :], AF.Square,
                        accum_out=statB[:, 49:50]),
                       reads=[b_out1[c]], writes=[b_bnc[1], b_statB],
                       touch=statB[:, 49:50])
                    EM("dve", lambda e, c=c: e.tensor_scalar(
                        statB[:, c:c + 1], statB[:, 48:49], 1.0 / D, None, op.mult),
                       reads=[b_statB], writes=[b_statB])
                    EM("dve", lambda e, c=c: e.tensor_scalar(
                        statB[:, 50:51], statB[:, c:c + 1], statB[:, c:c + 1],
                        None, op.mult),
                       reads=[b_statB], writes=[b_statB])
                    EM("dve", lambda e, c=c: e.tensor_scalar(
                        statB[:, 16 + c:17 + c], statB[:, 49:50], 1.0 / D,
                        statB[:, 50:51], op.mult, op.subtract),
                       reads=[b_statB], writes=[b_statB])
        if ex + 2 < NEXP:
            EM("pool", lambda e, ex=ex, w=(ex + 2) % 2: e.dma_start(
                gup_sb[:, w, :], dgup[ex + 2]), writes=[b_gup[(ex + 2) % 2]],
               dma=True)
            EM("sp", lambda e, ex=ex, w=(ex + 2) % 2: e.dma_start(
                down_sb[:, w, :], ddown[ex + 2]), writes=[b_down[(ex + 2) % 2]],
               dma=True)

    if DEV:
        for c in range(NCH):
            EM("sp", lambda e, c=c: e.dma_start(dbg_acc[c * 128:(c + 1) * 128, :],
                                                out1_tm[:, c, :]),
               reads=[b_out1[c]], dma=True)

    EM("act", lambda e: e.activation(statB[:, 32:48], statB[:, 16:32], AF.Ln,
                                     bias=epsc[:, 0:1]),
       reads=[b_statB, b_tab], writes=[b_statB])
    EM("act", lambda e: e.activation(statB[:, 32:48], statB[:, 32:48], AF.Exp,
                                     scale=-0.5),
       reads=[b_statB], writes=[b_statB])
    for c in range(NCH):
        bp = c % 2
        EM("dve", lambda e, c=c, bp=bp: e.tensor_scalar(
            bounce[:, bp, :], out1_tm[:, c, :], statB[:, c:c + 1],
            statB[:, 32 + c:33 + c], op.subtract, op.mult),
           reads=[b_out1[c], b_statB], writes=[b_bnc[bp]])
        EM("sp", lambda e, c=c, bp=bp: e.dma_start(
            dout[c * 128:(c + 1) * 128, :], bounce[:, bp, :]),
           reads=[b_bnc[bp]], writes=[b_outs[c]], dma=True)

    P.emit("sp", None, reads=list(b_outs))

    P.finalize(nc, stack)
    block = stack.enter_context(nc.Block())
    with stack, stB:
        @block.tensor
        def _(eng):
            P.replay("pe", eng)

        @block.vector
        def _(eng):
            P.replay("dve", eng)

        @block.scalar
        def _(eng):
            P.replay("act", eng)

        @block.gpsimd
        def _(eng):
            P.replay("pool", eng)

        @block.sync
        def _(eng):
            P.replay("sp", eng)

    return nc



def _mk_inputs(core, ins):
    x = np.asarray(ins["L0_attn_x"])
    b = core
    W_in = np.asarray(ins["L0_attn_W_in"])
    win_r = np.ascontiguousarray(W_in.T.reshape(8, 128, 1024).transpose(1, 0, 2))
    xb = np.ascontiguousarray(x[b])
    xfm = np.ascontiguousarray(
        x[b].T.reshape(8, 128, 16, 128).transpose(2, 1, 0, 3).reshape(16, 128, 1024))
    cre_aug = np.concatenate([np.asarray(ins["L0_attn_C_re"]).T,
                              np.asarray(ins["L0_attn_D_re"])[None, :]], axis=0)
    ab = np.stack([np.asarray(ins["L0_attn_A_re"]), np.asarray(ins["L0_attn_A_im"]),
                   np.asarray(ins["L0_attn_B_re"]), np.asarray(ins["L0_attn_B_im"])],
                  axis=1).astype(np.float32)
    rw_r = np.ascontiguousarray(
        np.asarray(ins["L0_ffn_router_w"]).reshape(8, 128, 8).transpose(1, 0, 2))

    def tile_w(w):
        I = w.shape[1]
        return np.ascontiguousarray(
            w.reshape(8, 128, I).transpose(1, 0, 2).reshape(128, 8 * I))

    def tile_d(w):
        return np.ascontiguousarray(
            w.reshape(4, 128, 1024).transpose(1, 0, 2).reshape(128, 4096))

    gate = np.asarray(ins["L0_ffn_gate_w"])
    up = np.asarray(ins["L0_ffn_up_w"])
    down = np.asarray(ins["L0_ffn_down_w"])
    sgw = np.asarray(ins["L0_ffn_s_gate_w"])
    suw = np.asarray(ins["L0_ffn_s_up_w"])
    sdw = np.asarray(ins["L0_ffn_s_down_w"])
    gup = np.stack(
        [np.concatenate([tile_w(gate[e]), tile_w(up[e])], axis=1) for e in range(8)]
        + [np.concatenate([tile_w(sgw[:, q * 512:(q + 1) * 512]),
                           tile_w(suw[:, q * 512:(q + 1) * 512])], axis=1)
           for q in range(4)], axis=0)
    down_r = np.stack([tile_d(down[e]) for e in range(8)]
                      + [tile_d(sdw[q * 512:(q + 1) * 512]) for q in range(4)],
                      axis=0)

    return {
        "x_tm": xb,
        "x_fm": xfm,
        "w_in_r": win_r,
        "b_in_row": np.ascontiguousarray(np.asarray(ins["L0_attn_b_in"])[None, :]),
        "gamma_row": np.ascontiguousarray(np.asarray(ins["L0_attn_gamma"])[None, :]),
        "beta_row": np.ascontiguousarray(np.asarray(ins["L0_attn_beta"])[None, :]),
        "ab_vecs": ab,
        "d_row": np.ascontiguousarray(np.stack([
            np.asarray(ins["L0_attn_D_re"]),
            np.asarray(ins["L0_attn_D_im"])])),
        "cre_aug": np.ascontiguousarray(cre_aug),
        "cim_t": np.ascontiguousarray(np.asarray(ins["L0_attn_C_im"]).T),
        "dft_mats": _dft_mats()[0],
        "dft_lo": _dft_mats()[1],
        "ones_row": np.ones((1, 128), np.float32),
        "ones_row_f": np.ones((1, 128), np.float32),
        "ident_f": np.eye(128, dtype=np.float32),
        "ident_r": np.eye(128, dtype=np.float32),
        "rw_r": rw_r,
        "rbias": np.asarray(ins["L0_ffn_router_bias"]).reshape(1, 1)
        .astype(np.float32),
        "gup": gup,
        "down_r": down_r,
    }


def kernel(**inputs):
    nc = build_nc()
    in_maps = [_mk_inputs(b, inputs) for b in range(B_SZ)]
    res = run_bass_kernel_spmd(nc, in_maps, core_ids=list(range(B_SZ)))
    out = np.stack([np.asarray(res.results[b]["out"]) for b in range(B_SZ)], axis=0)
    return out


# version 24
# speedup vs baseline: 1.0348x; 1.0348x over previous
import contextlib
import numpy as np

import concourse.bass as bass
import concourse.mybir as mybir
from concourse.alu_op_type import AluOpType as op
from concourse.bass_utils import run_bass_kernel_spmd

F32 = mybir.dt.float32
F32R = mybir.dt.float32r
BF16 = mybir.dt.bfloat16
AF = mybir.ActivationFunctionType
AX = mybir.AxisListType

B_SZ, L, D, DS = 8, 2048, 1024, 64
T = 128
NCH = L // T
NF = 255
NEXP = 12
LN_EPS = 1e-5
DEV = False

_ENGS = ("pe", "dve", "act", "pool", "sp")


class _Rec:
    __slots__ = ("eng", "fn", "waits", "inc", "dma", "sem", "val")

    def __init__(self, eng, fn, dma):
        self.eng, self.fn, self.dma = eng, fn, dma
        self.waits = []
        self.inc = False
        self.sem = None
        self.val = None


class Buf:
    __slots__ = ("last_w", "readers", "name")

    def __init__(self, name=""):
        self.last_w = None
        self.readers = []
        self.name = name


class Prog:
    def __init__(self):
        self.recs = {e: [] for e in _ENGS}
        self.fence_pending = {e: [] for e in _ENGS}

    def emit(self, eng, fn, reads=(), writes=(), dma=False, extra_waits=(),
             touch=None):
        rec = _Rec(eng, fn, dma)
        waits = list(extra_waits)
        for b in reads:
            if b.last_w is not None:
                waits.append(b.last_w)
        for b in writes:
            if b.last_w is not None:
                waits.append(b.last_w)
            waits.extend(b.readers)
        if self.fence_pending[eng]:
            waits.extend(self.fence_pending[eng])
            self.fence_pending[eng] = []
        seen = set()
        for w in waits:
            if w is None or w is rec or id(w) in seen:
                continue
            seen.add(id(w))
            if (not w.dma) and w.eng == eng and eng in ("pe", "sp"):
                continue
            rec.waits.append(w)
            w.inc = True
        self.recs[eng].append(rec)
        for b in reads:
            b.readers.append(rec)
        for b in writes:
            b.last_w = rec
            b.readers = []
        if touch is not None:
            if eng == "act":
                fn2 = (lambda ap: lambda e: e.copy(ap, ap))(touch)
            else:
                fn2 = (lambda ap: lambda e: e.tensor_copy(ap, ap))(touch)
            return self.emit(eng, fn2, reads=list(writes), writes=list(writes))
        return rec

    def fence(self):
        tails = [self.recs[e][-1] for e in _ENGS if self.recs[e]]
        for e in _ENGS:
            self.fence_pending[e] = list(tails)

    def finalize(self, nc, stack):
        esem = {e: stack.enter_context(nc.semaphore(f"sem_{e}")) for e in _ENGS}
        NRING = 6
        dring = {q: [stack.enter_context(nc.semaphore(f"dma_{q}{i}"))
                     for i in range(NRING)] for q in ("sp", "pool")}
        dcnt = {"sp": 0, "pool": 0}
        ecnt = {e: 0 for e in _ENGS}
        for e in _ENGS:
            for rec in self.recs[e]:
                if rec.dma:
                    q = "sp" if e == "sp" else "pool"
                    rec.sem = dring[q][dcnt[q] % NRING]
                    rec.val = (dcnt[q] // NRING + 1) * 16
                    dcnt[q] += 1
                elif rec.inc:
                    ecnt[e] += 1
                    rec.sem = esem[e]
                    rec.val = ecnt[e]

    def replay(self, eng_name, eng):
        watermark = {}
        for rec in self.recs[eng_name]:
            for w in rec.waits:
                assert w.sem is not None, "wait on un-finalized record"
                key = id(w.sem)
                if watermark.get(key, 0) >= w.val:
                    continue
                eng.wait_ge(w.sem, w.val)
                watermark[key] = w.val
            if rec.dma and rec.val > 16:
                key = id(rec.sem)
                if watermark.get(key, 0) < rec.val - 16:
                    eng.wait_ge(rec.sem, rec.val - 16)
                    watermark[key] = rec.val - 16
            if rec.fn is None:
                continue
            instr = rec.fn(eng)
            if rec.sem is not None:
                instr.then_inc(rec.sem, 16 if rec.dma else 1)


def _dft_mats():
    f = np.arange(128)
    tau = np.arange(T)
    ang = 2 * np.pi * np.outer(tau, f) / NF
    cfT = np.cos(ang)
    sfT = -np.sin(ang)
    w = np.full(128, 2.0)
    w[0] = 1.0
    angi = 2 * np.pi * np.outer(f, tau) / NF
    icT = np.cos(angi) * w[:, None] / NF
    isT = -np.sin(angi) * w[:, None] / NF
    m64 = np.concatenate([cfT, sfT, icT, isT], axis=1)
    hi = m64.astype(np.float32)
    lo = (m64 - hi.astype(np.float64)).astype(np.float32)
    return hi, lo


def build_nc():
    nc = bass.Bass("TRN2", target_bir_lowering=False)
    P = Prog()
    EM = P.emit

    dx_tm = nc.dram_tensor("x_tm", [L, D], F32, kind="ExternalInput").ap()
    dx_fm = nc.dram_tensor("x_fm", [NCH, 128, D], F32, kind="ExternalInput").ap()
    dwin = nc.dram_tensor("w_in_r", [128, 8, D], F32, kind="ExternalInput").ap()
    dbin = nc.dram_tensor("b_in_row", [1, D], F32, kind="ExternalInput").ap()
    dgam = nc.dram_tensor("gamma_row", [1, D], F32, kind="ExternalInput").ap()
    dbet = nc.dram_tensor("beta_row", [1, D], F32, kind="ExternalInput").ap()
    dab = nc.dram_tensor("ab_vecs", [DS, 4], F32, kind="ExternalInput").ap()
    ddre = nc.dram_tensor("d_row", [2, D], F32, kind="ExternalInput").ap()
    dcre = nc.dram_tensor("cre_aug", [DS + 1, D], F32R, kind="ExternalInput").ap()
    dcim = nc.dram_tensor("cim_t", [DS, D], F32R, kind="ExternalInput").ap()
    ddft = nc.dram_tensor("dft_mats", [128, 512], F32R, kind="ExternalInput").ap()
    ddftl = nc.dram_tensor("dft_lo", [128, 512], F32R, kind="ExternalInput").ap()
    dones = nc.dram_tensor("ones_row", [1, 128], F32R, kind="ExternalInput").ap()
    donesf = nc.dram_tensor("ones_row_f", [1, 128], F32, kind="ExternalInput").ap()
    diden = nc.dram_tensor("ident_f", [128, 128], F32, kind="ExternalInput").ap()
    didenr = nc.dram_tensor("ident_r", [128, 128], F32R, kind="ExternalInput").ap()
    drw = nc.dram_tensor("rw_r", [128, 8, 8], F32, kind="ExternalInput").ap()
    drb = nc.dram_tensor("rbias", [1, 1], F32, kind="ExternalInput").ap()
    dgup = nc.dram_tensor("gup", [NEXP, 128, 8192], F32, kind="ExternalInput").ap()
    ddown = nc.dram_tensor("down_r", [NEXP, 128, 4096], F32R,
                           kind="ExternalInput").ap()
    dout = nc.dram_tensor("out", [L, D], F32, kind="ExternalOutput").ap()
    if DEV:
        dbg_xg = nc.dram_tensor("dbg_xg", [L, D], F32, kind="ExternalOutput").ap()
        dbg_o1 = nc.dram_tensor("dbg_o1", [L, D], F32, kind="ExternalOutput").ap()
        dbg_w8 = nc.dram_tensor("dbg_w8", [128, NCH, 8], F32,
                                kind="ExternalOutput").ap()
        dbg_y = nc.dram_tensor("dbg_y", [L, D], F32, kind="ExternalOutput").ap()
        dbg_x1 = nc.dram_tensor("dbg_x1", [L, D], F32, kind="ExternalOutput").ap()
        dbg_acc = nc.dram_tensor("dbg_acc", [L, D], F32, kind="ExternalOutput").ap()

    stack = contextlib.ExitStack()

    def S(name, shape, dt):
        return stack.enter_context(nc.sbuf_tensor(name, shape, dt))

    out1_tm = S("out1_tm", [128, NCH, 1024], F32)
    w8n = S("w8n", [128, NCH, 8], F32)
    statA = S("statA", [128, 2, 40], F32)
    iden = S("iden", [128, 128], F32)
    idenr = S("idenr", [128, 128], F32R)
    onesr = S("onesr", [1, 128], F32R)
    onesf = S("onesf", [1, 128], F32)
    epsc = S("epsc", [128, 1], F32)
    rw_sb = S("rw_sb", [128, 8, 8], F32)

    b_out1 = [Buf(f"out1_{c}") for c in range(NCH)]
    b_w8 = [Buf(f"w8_{c}") for c in range(NCH)]
    b_stat = [Buf("statA0"), Buf("statA1")]
    b_tab = Buf("tables")

    stA = contextlib.ExitStack()

    def SA(name, shape, dt):
        return stA.enter_context(nc.sbuf_tensor(name, shape, dt))

    win_sb = SA("win_sb", [128, 8, 1024], BF16)
    x_tm = SA("x_tm_sb", [128, 2, 1024], F32)
    x_fm = SA("x_fm_sb", [128, 2, 1024], BF16)
    xg = SA("xg_sb", [128, 2, 1024], F32R)
    scr = SA("scr", [128, 5, 1024], F32)
    cre_sb = SA("cre_sb", [DS + 1, 1024], F32R)
    cim_sb = SA("cim_sb", [DS, 1024], F32R)
    dft_sb = SA("dft_sb", [128, 512], F32R)
    dftl_sb = SA("dftl_sb", [128, 512], F32R)
    khat = SA("khat", [128, 2048], F32R)
    xh_im = SA("xh_im", [128, 1024], F32)
    pfreq = SA("pfreq", [128, 2048], F32R)
    mprod = SA("mprod", [DS, 2048], F32R)
    mprod2 = SA("mprod2", [DS, 2048], F32R)
    h_re = SA("h_re", [DS, 1024], F32R)
    h_im = SA("h_im", [DS, 1024], F32R)
    srow = SA("srow", [1, 1024], F32)
    dsrow = SA("dsrow", [1, 1024], F32R)
    drow = SA("drow", [1, 1024], F32)
    binrow = SA("binrow", [1, 1024], F32)
    bgam = SA("bgam", [128, 1024], F32)
    bbet = SA("bbet", [128, 1024], F32)
    pbre = SA("pbre", [DS + 1, 128], F32R)
    npbim = SA("npbim", [DS, 128], F32R)
    prevtr = SA("prevtr", [128, 64], F32R)
    prevti = SA("prevti", [128, 64], F32R)
    qre = SA("qre", [DS, 128], F32R)
    nqre = SA("nqre", [DS, 128], F32R)
    nqim = SA("nqim", [DS, 128], F32R)
    dgatre = SA("dgatre", [DS, 64], F32R)
    dgatim = SA("dgatim", [DS, 64], F32R)
    dgnatim = SA("dgnatim", [DS, 64], F32R)
    imask = SA("imask", [DS, 64], F32)

    powt = scr[0:DS, 0, :]
    rowsc = scr[0:1, 1, :]
    abv = scr[0:DS, 3, 0:4]
    krsb = pfreq[:, 0:1024]
    revr = mprod[:, 0:128]
    revi = mprod[:, 128:256]

    ps_a = stA.enter_context(nc.psum_tensor("ps_a", [128, 1024], F32))
    ps_b = stA.enter_context(nc.psum_tensor("ps_b", [128, 1024], F32))
    ps_c = stA.enter_context(nc.psum_tensor("ps_c", [128, 1024], F32))
    px1 = stA.enter_context(nc.psum_tensor("px1", [128, 1024], F32))

    b_win = Buf("win")
    b_xtm = [Buf(f"xtm{i}") for i in range(2)]
    b_xfm = [Buf(f"xfm{i}") for i in range(2)]
    b_xg = [Buf(f"xg{i}") for i in range(2)]
    b_scr = [Buf(f"scr{i}") for i in range(6)]
    b_khat = Buf("khat")
    b_xhim = Buf("xh_im")
    b_pfreq = [Buf("pfreq0"), Buf("pfreq1")]
    b_m1 = [Buf("mprod_0"), Buf("mprod_1")]
    b_m2 = [Buf("mprod2_0"), Buf("mprod2_1")]
    b_h = Buf("h")
    b_srow = Buf("srow")
    b_dsrow = Buf("dsrow")
    b_bc = Buf("bcast_rows")
    b_psa = [Buf("ps_a0"), Buf("ps_a1")]
    b_psb = [Buf("ps_b0"), Buf("ps_b1")]
    b_psc = [Buf("ps_c0"), Buf("ps_c1")]
    b_px1 = [Buf("px1_0"), Buf("px1_1")]
    b_rows = Buf("rowsc")

    EM("sp", lambda e: e.dma_start(abv[:], dab), writes=[b_tab], dma=True)
    EM("sp", lambda e: e.dma_start(cre_sb[:], dcre), writes=[b_tab], dma=True)
    EM("sp", lambda e: e.dma_start(cim_sb[:], dcim), writes=[b_tab], dma=True)
    EM("sp", lambda e: e.dma_start(dft_sb[:], ddft), writes=[b_tab], dma=True)
    EM("sp", lambda e: e.dma_start(dftl_sb[:], ddftl), writes=[b_tab], dma=True)
    EM("sp", lambda e: e.dma_start(idenr[:], didenr), writes=[b_tab], dma=True)
    EM("sp", lambda e: e.dma_start(onesr[:], dones), writes=[b_tab], dma=True)
    EM("sp", lambda e: e.dma_start(onesf[:], donesf), writes=[b_tab], dma=True)
    EM("sp", lambda e: e.dma_start(iden[:], diden), writes=[b_tab], dma=True)
    EM("sp", lambda e: e.dma_start(drow[:], ddre[0:1, :]), writes=[b_tab], dma=True)
    EM("sp", lambda e: e.dma_start(binrow[:], dbin), writes=[b_bc], dma=True)
    EM("sp", lambda e: e.dma_start(rw_sb[:], drw), writes=[b_tab], dma=True)
    EM("pool", lambda e: e.dma_start(win_sb[:], dwin), writes=[b_win], dma=True)
    EM("sp", lambda e: e.dma_start(x_tm[:, 0, :], dx_tm[0:128, :]),
       writes=[b_xtm[0]], dma=True)
    EM("sp", lambda e: e.dma_start(x_tm[:, 1, :], dx_tm[128:256, :]),
       writes=[b_xtm[1]], dma=True)
    EM("pool", lambda e: e.dma_start(x_fm[:, 0, :], dx_fm[0]),
       writes=[b_xfm[0]], dma=True)

    EM("sp", lambda e: e.dma_start(rowsc[:], dgam), writes=[b_rows], dma=True)
    for half in range(2):
        sl = slice(half * 512, half * 512 + 512)
        EM("pe", lambda e, sl=sl: e.matmul(ps_a[:, sl], onesf[:], rowsc[:, sl],
                                           start=True, stop=True),
           reads=[b_rows, b_tab], writes=[b_psa[half]])
    EM("dve", lambda e: e.tensor_copy(bgam[:], ps_a[:]), reads=b_psa, writes=[b_bc])
    EM("sp", lambda e: e.dma_start(rowsc[:], dbet), writes=[b_rows], dma=True)
    for half in range(2):
        sl = slice(half * 512, half * 512 + 512)
        EM("pe", lambda e, sl=sl: e.matmul(ps_b[:, sl], onesf[:], rowsc[:, sl],
                                           start=True, stop=True),
           reads=[b_rows, b_tab], writes=[b_psb[half]])
    EM("dve", lambda e: e.tensor_copy(bbet[:], ps_b[:]), reads=b_psb, writes=[b_bc])
    EM("sp", lambda e: e.dma_start(rowsc[:, 0:1], drb), writes=[b_rows], dma=True)
    EM("pe", lambda e: e.matmul(px1[:, 0:1], onesf[:], rowsc[:, 0:1],
                                start=True, stop=True),
       reads=[b_rows, b_tab], writes=[b_px1[0]])
    EM("dve", lambda e: e.tensor_copy(statA[:, 0, 15:16], px1[:, 0:1]),
       reads=[b_px1[0]], writes=[b_stat[0]])
    EM("dve", lambda e: e.tensor_copy(statA[:, 1, 15:16], px1[:, 0:1]),
       reads=[b_px1[0]], writes=[b_stat[1]])

    PRE = powt[:, 0:161]
    PIM = powt[:, 192:353]
    are, aim = abv[:, 0:1], abv[:, 1:2]
    bre, bim = abv[:, 2:3], abv[:, 3:4]

    def mkt(fn):
        EM("dve", fn, reads=[b_tab], writes=[b_tab])

    EM("dve", lambda e: e.memset(PRE[:, 0:1], 1.0), writes=[b_tab])
    EM("dve", lambda e: e.memset(PIM[:, 0:1], 0.0), writes=[b_tab])
    mkt(lambda e: e.tensor_copy(PRE[:, 1:2], are))
    mkt(lambda e: e.tensor_copy(PIM[:, 1:2], aim))
    have = 2
    while have < 161:
        n = min(have - 1, 161 - have)
        s_re = PRE[:, have - 1:have]
        s_im = PIM[:, have - 1:have]
        pr1 = PRE[:, 1:1 + n]
        pi1 = PIM[:, 1:1 + n]
        dst_r = PRE[:, have:have + n]
        dst_i = PIM[:, have:have + n]
        t0 = powt[:, 354:354 + n]
        mkt(lambda e, a=dst_r, b=pr1, s=s_re: e.tensor_scalar(a, b, s, None, op.mult))
        mkt(lambda e, a=t0, b=pi1, s=s_im: e.tensor_scalar(a, b, s, None, op.mult))
        mkt(lambda e, a=dst_r, b=t0: e.tensor_tensor(a, a, b, op.subtract))
        mkt(lambda e, a=dst_i, b=pr1, s=s_im: e.tensor_scalar(a, b, s, None, op.mult))
        mkt(lambda e, a=t0, b=pi1, s=s_re: e.tensor_scalar(a, b, s, None, op.mult))
        mkt(lambda e, a=dst_i, b=t0: e.tensor_tensor(a, a, b, op.add))
        have += n

    t1 = powt[:, 354:482]
    mkt(lambda e: e.tensor_scalar(pbre[0:DS, :], PRE[:, 0:128], bre, None, op.mult))
    mkt(lambda e: e.tensor_scalar(t1, PIM[:, 0:128], bim, None, op.mult))
    mkt(lambda e: e.scalar_tensor_tensor(pbre[0:DS, :], pbre[0:DS, :].bitcast(F32),
                                         1.0, t1, op.mult, op.subtract))
    EM("dve", lambda e: e.memset(pbre[DS:DS + 1, :].bitcast(F32), 1.0), writes=[b_tab])
    mkt(lambda e: e.tensor_scalar(npbim[:], PRE[:, 0:128], bim, None, op.mult))
    mkt(lambda e: e.tensor_scalar(t1, PIM[:, 0:128], bre, None, op.mult))
    mkt(lambda e: e.scalar_tensor_tensor(npbim[:], npbim[:].bitcast(F32), -1.0, t1,
                                         op.mult, op.subtract))
    mkt(lambda e: e.tensor_copy(revr[:], pbre[0:DS, ::-1].bitcast(F32)))
    mkt(lambda e: e.tensor_scalar(revi[:], npbim[:, ::-1].bitcast(F32), -1.0, None,
                                  op.mult))
    EM("pe", lambda e: e.transpose(ps_a[:, 0:64].bitcast(F32R), revr[:], idenr[0:64, 0:64]),
       reads=[b_tab], writes=[b_psa[0]])
    EM("pe", lambda e: e.transpose(ps_a[:, 512:576].bitcast(F32R), revi[:], idenr[0:64, 0:64]),
       reads=[b_tab], writes=[b_psa[1]])
    EM("dve", lambda e: e.tensor_copy(prevtr[:], ps_a[:, 0:64].bitcast(F32R)),
       reads=[b_psa[0]], writes=[b_tab])
    EM("dve", lambda e: e.tensor_copy(prevti[:], ps_a[:, 512:576].bitcast(F32R)),
       reads=[b_psa[1]], writes=[b_tab])
    mkt(lambda e: e.tensor_copy(qre[:], PRE[:, 1:129]))
    mkt(lambda e: e.tensor_scalar(nqre[:], PRE[:, 1:129], -1.0, None, op.mult))
    mkt(lambda e: e.tensor_scalar(nqim[:], PIM[:, 1:129], -1.0, None, op.mult))
    EM("pool", lambda e: e.iota(imask[:], [[1, 64]], base=0, channel_multiplier=-1,
                                allow_small_or_imprecise_dtypes=True),
       writes=[b_tab])
    mkt(lambda e: e.tensor_scalar(imask[:], imask[:], 0.0, None, op.is_equal))
    atre = PRE[:, 128:129]
    atim = PIM[:, 128:129]
    mkt(lambda e: e.tensor_scalar(dgatre[:], imask[:], atre, None, op.mult))
    mkt(lambda e: e.tensor_scalar(dgatim[:], imask[:], atim, None, op.mult))
    mkt(lambda e: e.tensor_scalar(dgnatim[:], imask[:], atim, -1.0, op.mult, op.mult))

    for half in range(2):
        sl = slice(half * 512, half * 512 + 512)
        EM("pe", lambda e, sl=sl: e.matmul(ps_b[:, sl], pbre[:], cre_sb[:, sl],
                                           start=True, stop=False),
           reads=[b_tab], writes=[b_psb[half]])
        EM("pe", lambda e, sl=sl: e.matmul(ps_b[:, sl], npbim[:], cim_sb[:, sl],
                                           start=False, stop=True),
           reads=[b_tab], writes=[b_psb[half]])
    EM("act", lambda e: e.copy(krsb[:], ps_b[:]), reads=b_psb, writes=[b_tab])
    for half in range(2):
        sl = slice(half * 512, half * 512 + 512)
        EM("pe", lambda e, sl=sl: e.matmul(ps_c[:, sl], dft_sb[:, 0:128],
                                           krsb[:, sl], start=True, stop=False),
           reads=[b_tab], writes=[b_psc[half]])
        EM("pe", lambda e, sl=sl: e.matmul(ps_c[:, sl], dftl_sb[:, 0:128],
                                           krsb[:, sl], start=False, stop=True),
           reads=[b_tab], writes=[b_psc[half]])
    EM("act", lambda e: e.copy(khat[:, 0:1024], ps_c[:]), reads=b_psc,
       writes=[b_khat])
    for half in range(2):
        sl = slice(half * 512, half * 512 + 512)
        EM("pe", lambda e, sl=sl: e.matmul(ps_c[:, sl], dft_sb[:, 128:256],
                                           krsb[:, sl], start=True, stop=False),
           reads=[b_tab], writes=[b_psc[half]])
        EM("pe", lambda e, sl=sl: e.matmul(ps_c[:, sl], dftl_sb[:, 128:256],
                                           krsb[:, sl], start=False, stop=True),
           reads=[b_tab], writes=[b_psc[half]])
    EM("act", lambda e: e.copy(khat[:, 1024:2048], ps_c[:]), reads=b_psc,
       writes=[b_khat])

    EM("dve", lambda e: e.memset(epsc[:], LN_EPS), writes=[b_tab])
    EM("dve", lambda e: e.memset(w8n[:], 0.0), writes=list(b_w8))
    EM("dve", lambda e: e.memset(h_re[:].bitcast(F32), 0.0), writes=[b_h])
    EM("dve", lambda e: e.memset(h_im[:].bitcast(F32), 0.0), writes=[b_h])
    EM("dve", lambda e: e.memset(srow[:], 0.0), writes=[b_srow])
    P.fence()

    (C_SUM0, C_SUM1, C_SUM, C_MU, C_SSQ, C_VAR, C_RST, C_NMX, C_SEXP, C_RSUM,
     C_CGT, C_THR, C_WSUM, C_WINV, C_TMP, C_RB) = range(16)

    KRE = khat[:, 0:1024]
    KIM = khat[:, 1024:2048]

    def _env(c):
        p = c % 2

        class E:
            pass

        v = E()
        v.p = p
        v.sc = lambda i, p=p: statA[:, p, i:i + 1]
        v.s0 = scr[:, 2 * p + 0, :]
        v.s1 = scr[:, 2 * p + 1, :]
        v.s2 = scr[:, 4, :]
        v.b_s0, v.b_s1 = b_scr[2 * p:2 * p + 2]
        v.b_s2 = b_scr[4]
        v.b_st = b_stat[p]
        v.xg_c = xg[:, p, :]
        v.xtm_c = x_tm[:, p, :]
        v.xfm_c = x_fm[:, p, :]
        v.ds_c = dsrow[:]
        v.b_ds = b_dsrow
        return v

    def emit_x1(c):
        v = _env(c)
        if c + 1 < NCH:
            pn = (c + 1) % 2
            EM("pool", lambda e, c=c, pn=pn: e.dma_start(
                x_fm[:, pn, :], dx_fm[c + 1]), writes=[b_xfm[pn]], dma=True)
        for half in range(2):
            sl = slice(half * 512, half * 512 + 512)
            for kt in range(8):
                EM("pe", lambda e, sl=sl, kt=kt, xfm_c=v.xfm_c: e.matmul(
                    px1[:, sl], xfm_c[:, kt * 128:kt * 128 + 128],
                    win_sb[:, kt, sl], start=(kt == 0), stop=False),
                   reads=[b_xfm[v.p], b_win], writes=[b_px1[half]])
            EM("pe", lambda e, sl=sl: e.matmul(
                px1[:, sl], onesf[:], binrow[:, sl], start=False, stop=True),
               reads=[b_bc], writes=[b_px1[half]])

    def emit_ln1(c):
        v = _env(c)
        s0, s1, sc = v.s0, v.s1, v.sc
        b_s0, b_s1, b_st = v.b_s0, v.b_s1, v.b_st
        EM("dve", lambda e, s0=s0, sc=sc: e.tensor_scalar(
            s0, px1[:], 0.0, None, op.add, op.add, accum_out=sc(C_SUM)),
           reads=b_px1, writes=[b_s0, b_st], touch=sc(C_SUM))
        if DEV:
            EM("sp", lambda e, c=c, s0=s0: e.dma_start(
                dbg_x1[c * 128:(c + 1) * 128, :], s0), reads=[b_s0], dma=True)
        EM("act", lambda e, s0=s0, s1=s1, sc=sc: e.activation(
            s1, s0, AF.Square, accum_out=sc(C_SSQ)),
           reads=[b_s0], writes=[b_s1, b_st], touch=sc(C_SSQ))
        EM("dve", lambda e, sc=sc: e.tensor_scalar(sc(C_MU), sc(C_SUM), 1.0 / D,
                                                   None, op.mult),
           reads=[b_st], writes=[b_st])
        EM("dve", lambda e, sc=sc: e.tensor_scalar(sc(C_TMP), sc(C_MU), sc(C_MU),
                                                   None, op.mult),
           reads=[b_st], writes=[b_st])
        EM("dve", lambda e, sc=sc: e.tensor_scalar(sc(C_VAR), sc(C_SSQ), 1.0 / D,
                                                   sc(C_TMP), op.mult, op.subtract),
           reads=[b_st], writes=[b_st])
        EM("act", lambda e, sc=sc: e.activation(sc(C_TMP), sc(C_VAR), AF.Ln,
                                                bias=epsc[:, 0:1]),
           reads=[b_st], writes=[b_st])
        EM("act", lambda e, sc=sc: e.activation(sc(C_RST), sc(C_TMP), AF.Exp,
                                                scale=-0.5),
           reads=[b_st], writes=[b_st])
        EM("dve", lambda e, s0=s0, s1=s1, sc=sc: e.tensor_scalar(
            s1, s0, sc(C_MU), sc(C_RST), op.subtract, op.mult),
           reads=[b_s0, b_st], writes=[b_s1])
        EM("pool", lambda e, s0=s0, s1=s1: e.tensor_tensor(s0, s1, bgam[:],
                                                           op.mult),
           reads=[b_s1, b_bc], writes=[b_s0])
        EM("pool", lambda e, s0=s0, xg_c=v.xg_c: e.tensor_tensor(
            xg_c, s0, bbet[:], op.add),
           reads=[b_s0, b_bc], writes=[b_xg[v.p]])
        if DEV:
            EM("sp", lambda e, c=c, xg_c=v.xg_c: e.dma_start(
                dbg_xg[c * 128:(c + 1) * 128, :], xg_c.bitcast(F32)),
               reads=[b_xg[v.p]], dma=True)

    def emit_dft(c):
        v = _env(c)
        for half in range(2):
            sl = slice(half * 512, half * 512 + 512)
            EM("pe", lambda e, sl=sl, xg_c=v.xg_c: e.matmul(
                ps_a[:, sl], dft_sb[:, 0:128], xg_c[:, sl], start=True,
                stop=False),
               reads=[b_xg[v.p], b_tab], writes=[b_psa[half]])
            EM("pe", lambda e, sl=sl, xg_c=v.xg_c: e.matmul(
                ps_a[:, sl], dftl_sb[:, 0:128], xg_c[:, sl], start=False,
                stop=True),
               reads=[b_xg[v.p], b_tab], writes=[b_psa[half]])
            EM("pe", lambda e, sl=sl, xg_c=v.xg_c: e.matmul(
                ps_b[:, sl], dft_sb[:, 128:256], xg_c[:, sl], start=True,
                stop=False),
               reads=[b_xg[v.p], b_tab], writes=[b_psb[half]])
            EM("pe", lambda e, sl=sl, xg_c=v.xg_c: e.matmul(
                ps_b[:, sl], dftl_sb[:, 128:256], xg_c[:, sl], start=False,
                stop=True),
               reads=[b_xg[v.p], b_tab], writes=[b_psb[half]])

    def emit_prod(c):
        v = _env(c)
        s2, sc = v.s2, v.sc
        b_s2, b_st = v.b_s2, v.b_st
        EM("dve", lambda e, ds_c=v.ds_c: e.tensor_tensor(ds_c, drow[:], srow[:],
                                                         op.mult),
           reads=[b_srow, b_tab], writes=[v.b_ds])
        EM("dve", lambda e: e.tensor_tensor(srow[:], srow[:], ps_a[0:1, :],
                                            op.add),
           reads=b_psa, writes=[b_srow])
        EM("act", lambda e: e.copy(xh_im[:], ps_b[:]), reads=b_psb,
           writes=[b_xhim])
        for half in range(2):
            sl = slice(half * 512, half * 512 + 512)
            s2h = s2[:, sl]
            PRE_F = pfreq[:, half * 1024:half * 1024 + 512]
            PIM_F = pfreq[:, half * 1024 + 512:half * 1024 + 1024]
            M_RR = mprod[:, half * 1024:half * 1024 + 512]
            M_II = mprod[:, half * 1024 + 512:half * 1024 + 1024]
            M_IR = mprod2[:, half * 1024:half * 1024 + 512]
            M_RI = mprod2[:, half * 1024 + 512:half * 1024 + 1024]
            EM("dve", lambda e, sl=sl, PRE_F=PRE_F: e.tensor_tensor(
                PRE_F, ps_a[:, sl], KRE[:, sl].bitcast(F32), op.mult),
               reads=[b_psa[half], b_khat], writes=[b_pfreq[half]])
            EM("dve", lambda e, sl=sl, s2h=s2h: e.tensor_tensor(
                s2h, xh_im[:, sl], KIM[:, sl].bitcast(F32), op.mult),
               reads=[b_xhim, b_khat], writes=[b_s2])
            EM("dve", lambda e, s2h=s2h, PRE_F=PRE_F: e.tensor_tensor(
                PRE_F, PRE_F.bitcast(F32), s2h, op.subtract),
               reads=[b_s2], writes=[b_pfreq[half]])
            EM("dve", lambda e, sl=sl, PIM_F=PIM_F: e.tensor_tensor(
                PIM_F, ps_a[:, sl], KIM[:, sl].bitcast(F32), op.mult),
               reads=[b_psa[half], b_khat], writes=[b_pfreq[half]])
            EM("dve", lambda e, sl=sl, s2h=s2h: e.tensor_tensor(
                s2h, xh_im[:, sl], KRE[:, sl].bitcast(F32), op.mult),
               reads=[b_xhim, b_khat], writes=[b_s2])
            EM("dve", lambda e, s2h=s2h, PIM_F=PIM_F: e.tensor_tensor(
                PIM_F, PIM_F.bitcast(F32), s2h, op.add),
               reads=[b_s2], writes=[b_pfreq[half]])

    def emit_eprod(c):
        v = _env(c)
        for half in range(2):
            sl = slice(half * 512, half * 512 + 512)
            M_RR = mprod[:, half * 1024:half * 1024 + 512]
            M_II = mprod[:, half * 1024 + 512:half * 1024 + 1024]
            M_IR = mprod2[:, half * 1024:half * 1024 + 512]
            M_RI = mprod2[:, half * 1024 + 512:half * 1024 + 1024]
            EM("pool", lambda e, sl=sl, M_RR=M_RR: e.tensor_tensor(
                M_RR, cre_sb[0:DS, sl].bitcast(F32), h_re[:, sl].bitcast(F32),
                op.mult), reads=[b_h, b_tab], writes=[b_m1[half]])
            EM("pool", lambda e, sl=sl, M_II=M_II: e.tensor_tensor(
                M_II, cim_sb[:, sl].bitcast(F32), h_im[:, sl].bitcast(F32),
                op.mult), reads=[b_h, b_tab], writes=[b_m1[half]])
            EM("pool", lambda e, sl=sl, M_IR=M_IR: e.tensor_tensor(
                M_IR, cim_sb[:, sl].bitcast(F32), h_re[:, sl].bitcast(F32),
                op.mult), reads=[b_h, b_tab], writes=[b_m2[half]])
            EM("pool", lambda e, sl=sl, M_RI=M_RI: e.tensor_tensor(
                M_RI, cre_sb[0:DS, sl].bitcast(F32), h_im[:, sl].bitcast(F32),
                op.mult), reads=[b_h, b_tab], writes=[b_m2[half]])

    def emit_carry(c):
        v = _env(c)
        for half in range(2):
            sl = slice(half * 512, half * 512 + 512)
            PRE_F = pfreq[:, half * 1024:half * 1024 + 512]
            PIM_F = pfreq[:, half * 1024 + 512:half * 1024 + 1024]
            M_RR = mprod[:, half * 1024:half * 1024 + 512]
            M_II = mprod[:, half * 1024 + 512:half * 1024 + 1024]
            M_IR = mprod2[:, half * 1024:half * 1024 + 512]
            M_RI = mprod2[:, half * 1024 + 512:half * 1024 + 1024]
            EM("pe", lambda e, sl=sl, ds_c=v.ds_c: e.matmul(
                ps_c[:, sl], onesr[:], ds_c[:, sl], start=True, stop=False),
               reads=[v.b_ds, b_tab], writes=[b_psc[half]])
            EM("pe", lambda e, sl=sl, PRE_F=PRE_F: e.matmul(
                ps_c[:, sl], dft_sb[:, 256:384], PRE_F, start=False, stop=False),
               reads=[b_pfreq[half], b_tab], writes=[b_psc[half]])
            EM("pe", lambda e, sl=sl, PRE_F=PRE_F: e.matmul(
                ps_c[:, sl], dftl_sb[:, 256:384], PRE_F, start=False,
                stop=False),
               reads=[b_pfreq[half], b_tab], writes=[b_psc[half]])
            EM("pe", lambda e, sl=sl, PIM_F=PIM_F: e.matmul(
                ps_c[:, sl], dft_sb[:, 384:512], PIM_F, start=False, stop=False),
               reads=[b_pfreq[half], b_tab], writes=[b_psc[half]])
            EM("pe", lambda e, sl=sl, PIM_F=PIM_F: e.matmul(
                ps_c[:, sl], dftl_sb[:, 384:512], PIM_F, start=False,
                stop=False),
               reads=[b_pfreq[half], b_tab], writes=[b_psc[half]])
            EM("pe", lambda e, sl=sl, M_RR=M_RR: e.matmul(
                ps_c[:, sl], qre[:], M_RR, start=False, stop=False),
               reads=[b_m1[half], b_tab], writes=[b_psc[half]])
            EM("pe", lambda e, sl=sl, M_II=M_II: e.matmul(
                ps_c[:, sl], nqre[:], M_II, start=False, stop=False),
               reads=[b_m1[half], b_tab], writes=[b_psc[half]])
            EM("pe", lambda e, sl=sl, M_IR=M_IR: e.matmul(
                ps_c[:, sl], nqim[:], M_IR, start=False, stop=False),
               reads=[b_m2[half], b_tab], writes=[b_psc[half]])
            EM("pe", lambda e, sl=sl, M_RI=M_RI: e.matmul(
                ps_c[:, sl], nqim[:], M_RI, start=False, stop=True),
               reads=[b_m2[half], b_tab], writes=[b_psc[half]])

    def emit_postconv(c):
        v = _env(c)
        s0, s1, s2, sc = v.s0, v.s1, v.s2, v.sc
        b_s0, b_s1, b_s2, b_st = v.b_s0, v.b_s1, v.b_s2, v.b_st
        if DEV:
            EM("dve", lambda e, s2=s2: e.tensor_copy(s2, ps_c[:]), reads=b_psc,
               writes=[b_s2])
            EM("sp", lambda e, c=c, s2=s2: e.dma_start(
                dbg_y[c * 128:(c + 1) * 128, :], s2), reads=[b_s2], dma=True)
        EM("dve", lambda e, s0=s0, xtm_c=v.xtm_c: e.tensor_tensor(
            s0, ps_c[:], xtm_c, op.add),
           reads=b_psc + [b_xtm[v.p]], writes=[b_s0])
        EM("dve", lambda e, s0=s0, s1=s1, xg_c=v.xg_c, sc=sc:
           e.scalar_tensor_tensor(
               s1, s0, 0.0, xg_c.bitcast(F32), op.add, op.add,
               accum_out=sc(C_SUM)),
           reads=[b_s0, b_xg[v.p]], writes=[b_s1, b_st], touch=sc(C_SUM))
        EM("act", lambda e, s0=s0, s1=s1, sc=sc: e.activation(
            s0, s1, AF.Square, accum_out=sc(C_SSQ)),
           reads=[b_s1], writes=[b_s0, b_st], touch=sc(C_SSQ))
        EM("dve", lambda e, sc=sc: e.tensor_scalar(sc(C_MU), sc(C_SUM), 1.0 / D,
                                                   None, op.mult),
           reads=[b_st], writes=[b_st])
        EM("dve", lambda e, sc=sc: e.tensor_scalar(sc(C_TMP), sc(C_MU), sc(C_MU),
                                                   None, op.mult),
           reads=[b_st], writes=[b_st])
        EM("dve", lambda e, sc=sc: e.tensor_scalar(sc(C_VAR), sc(C_SSQ), 1.0 / D,
                                                   sc(C_TMP), op.mult,
                                                   op.subtract),
           reads=[b_st], writes=[b_st])
        EM("act", lambda e, sc=sc: e.activation(sc(C_TMP), sc(C_VAR), AF.Ln,
                                                bias=epsc[:, 0:1]),
           reads=[b_st], writes=[b_st])
        EM("act", lambda e, sc=sc: e.activation(sc(C_RST), sc(C_TMP), AF.Exp,
                                                scale=-0.5),
           reads=[b_st], writes=[b_st])
        EM("dve", lambda e, c=c, s1=s1, sc=sc: e.tensor_scalar(
            out1_tm[:, c, :], s1, sc(C_MU), sc(C_RST), op.subtract, op.mult),
           reads=[b_s1, b_st], writes=[b_out1[c]])
        if c + 2 < NCH:
            EM("sp", lambda e, c=c, p=v.p: e.dma_start(
                x_tm[:, p, :], dx_tm[(c + 2) * 128:(c + 3) * 128, :]),
               writes=[b_xtm[v.p]], dma=True)

    def emit_h(c):
        v = _env(c)
        for half in range(2):
            sl = slice(half * 512, half * 512 + 512)
            EM("pe", lambda e, sl=sl, xg_c=v.xg_c: e.matmul(
                ps_b[0:DS, sl], prevtr[:], xg_c[:, sl], start=True, stop=False),
               reads=[b_xg[v.p], b_tab], writes=[b_psb[half]])
            EM("pe", lambda e, sl=sl: e.matmul(
                ps_b[0:DS, sl], dgatre[:], h_re[:, sl], start=False, stop=False),
               reads=[b_h, b_tab], writes=[b_psb[half]])
            EM("pe", lambda e, sl=sl: e.matmul(
                ps_b[0:DS, sl], dgnatim[:], h_im[:, sl], start=False, stop=True),
               reads=[b_h, b_tab], writes=[b_psb[half]])
            EM("pe", lambda e, sl=sl, xg_c=v.xg_c: e.matmul(
                ps_c[0:DS, sl], prevti[:], xg_c[:, sl], start=True, stop=False),
               reads=[b_xg[v.p], b_tab], writes=[b_psc[half]])
            EM("pe", lambda e, sl=sl: e.matmul(
                ps_c[0:DS, sl], dgatre[:], h_im[:, sl], start=False, stop=False),
               reads=[b_h, b_tab], writes=[b_psc[half]])
            EM("pe", lambda e, sl=sl: e.matmul(
                ps_c[0:DS, sl], dgatim[:], h_re[:, sl], start=False, stop=True),
               reads=[b_h, b_tab], writes=[b_psc[half]])
        EM("act", lambda e: e.copy(h_re[:], ps_b[0:DS, :]), reads=b_psb,
           writes=[b_h])
        EM("act", lambda e: e.copy(h_im[:], ps_c[0:DS, :]), reads=b_psc,
           writes=[b_h])

    emit_x1(0)
    emit_ln1(0)
    for r in range(NCH):
        emit_dft(r)
        emit_eprod(r)
        emit_prod(r)
        if r + 1 < NCH:
            emit_x1(r + 1)
        emit_carry(r)
        if r + 1 < NCH:
            emit_ln1(r + 1)
        emit_postconv(r)
        emit_h(r)

    if DEV:
        for c in range(NCH):
            EM("sp", lambda e, c=c: e.dma_start(dbg_o1[c * 128:(c + 1) * 128, :],
                                                out1_tm[:, c, :]),
               reads=[b_out1[c]], dma=True)

    P.fence()
    stA.close()

    stB = contextlib.ExitStack()

    def SB(name, shape, dt):
        return stB.enter_context(nc.sbuf_tensor(name, shape, dt))

    out1_fm = SB("out1_fm", [128, 8, 2048], BF16)
    b_o1fm = [Buf(f"o1fm_{c}") for c in range(NCH)]
    gup_sb = SB("gup_sb", [128, 2, 8192], BF16)
    down_sb = SB("down_sb", [128, 2, 4096], F32R)
    a_sl = SB("a_sl", [128, 2, 4, 512], F32R)
    sg_sb = SB("sg_sb", [128, 4, 512], F32)
    bounce = SB("bounce", [128, 2, 1024], F32)
    statB = SB("statB", [128, 64], F32)

    pg = stB.enter_context(nc.psum_tensor("pg", [128, 2048], F32))
    pu = stB.enter_context(nc.psum_tensor("pu", [128, 2048], F32))

    b_gup = [Buf(f"gup{i}") for i in range(2)]
    b_down = [Buf(f"down{i}") for i in range(2)]
    b_asl = [Buf(f"asl{i}") for i in range(2)]
    b_sg = [Buf(f"sg{i}") for i in range(4)]
    b_bnc = [Buf(f"bnc{i}") for i in range(2)]
    b_statB = Buf("statB")
    b_pg = [Buf(f"pg{i}") for i in range(4)]
    b_pu = [Buf(f"pu{i}") for i in range(4)]
    b_outs = [Buf(f"outs{c}") for c in range(NCH)]

    s2b = SB("s2b", [128, 2, 1024], F32)
    b_s2b = [Buf("s2b0"), Buf("s2b1")]
    (C_SUM0, C_SUM1, C_SUM, C_MU, C_SSQ, C_VAR, C_RST, C_NMX, C_SEXP, C_RSUM,
     C_CGT, C_THR, C_WSUM, C_WINV, C_TMP, C_RB) = range(16)

    def emit_tp(c):
        p = c % 2
        tp = pg if (p == 0) else pu
        b_tp = b_pg if (p == 0) else b_pu
        for kt in range(8):
            EM("pe", lambda e, kt=kt, c=c, tp=tp: e.transpose(
                tp[:, kt * 128:kt * 128 + 128],
                out1_tm[:, c, kt * 128:kt * 128 + 128], iden[:]),
               reads=[b_out1[c], b_tab], writes=[b_tp[kt // 4]])
        r_act = EM("act", lambda e, c=c, tp=tp: e.copy(
            out1_fm[:, :, c * 128:c * 128 + 128], tp[:, 0:1024]),
           reads=[b_tp[0], b_tp[1]], writes=[b_o1fm[c]])
        s2 = s2b[:, p, :]
        EM("act", lambda e, s2=s2, tp=tp: e.copy(s2, tp[:, 0:1024]),
           reads=[b_tp[0], b_tp[1]], writes=[b_s2b[p]], extra_waits=[r_act])

    def emit_router(c):
        p = c % 2
        otp = pu if (p == 0) else pg
        b_otp = b_pu if (p == 0) else b_pg
        s2 = s2b[:, p, :]
        bk = 3 if ((c >> 1) & 1) == 0 else 2
        pr_c = otp[:, bk * 512 + 504:bk * 512 + 512]
        b_pr = b_otp[bk]

        def sc(i, p=p):
            return statA[:, p, i:i + 1]

        for kt in range(8):
            EM("pe", lambda e, kt=kt, s2=s2, pr_c=pr_c: e.matmul(
                pr_c, s2[:, kt * 128:kt * 128 + 128], rw_sb[:, kt, :],
                start=(kt == 0), stop=(kt == 7)),
               reads=[b_s2b[p], b_tab], writes=[b_pr])
        LG = statA[:, p, 16:24]
        SCR8 = statA[:, p, 24:32]
        SORT8 = statA[:, p, 0:8]
        TH = statA[:, p, 3:4]
        EQ = statA[:, p, 32:40]
        MQ = w8n[:, c, :]
        b_st = b_stat[p]
        EM("dve", lambda e, sc=sc, pr_c=pr_c: e.tensor_reduce(
            sc(C_NMX), pr_c, AX.X, op.max, negate=True),
           reads=[b_pr], writes=[b_st])
        EM("act", lambda e, LG=LG, sc=sc, pr_c=pr_c: e.activation(
            LG, pr_c, AF.Exp, bias=sc(C_NMX), accum_out=sc(C_SEXP)),
           reads=[b_pr, b_st], writes=[b_st], touch=sc(C_SEXP))
        EM("dve", lambda e, sc=sc: e.reciprocal(sc(C_RSUM), sc(C_SEXP)),
           reads=[b_st], writes=[b_st])
        EM("dve", lambda e, LG=LG, sc=sc: e.tensor_scalar(LG, LG, sc(C_RSUM), None,
                                                          op.mult),
           reads=[b_st], writes=[b_st])
        EM("dve", lambda e, LG=LG, SCR8=SCR8, sc=sc: e.tensor_scalar(
            SCR8, LG, sc(C_RB), None, op.add),
           reads=[b_st], writes=[b_st])
        EM("dve", lambda e, SORT8=SORT8, SCR8=SCR8: e.max(SORT8, SCR8),
           reads=[b_st], writes=[b_st])
        EM("dve", lambda e, MQ=MQ, SCR8=SCR8, TH=TH, sc=sc: e.tensor_scalar(
            MQ, SCR8, TH, None, op.is_gt, op.add, accum_out=sc(C_CGT)),
           reads=[b_st], writes=[b_w8[c], b_st], touch=sc(C_CGT))
        EM("dve", lambda e, sc=sc: e.tensor_scalar(sc(C_THR), sc(C_CGT), -1.0, 4.0,
                                                   op.mult, op.add),
           reads=[b_st], writes=[b_st])
        EM("dve", lambda e, EQ=EQ, SCR8=SCR8, TH=TH: e.tensor_scalar(
            EQ, SCR8, TH, None, op.is_equal),
           reads=[b_st], writes=[b_st])
        EM("dve", lambda e, SCR8=SCR8: e.tensor_scalar(SCR8, SCR8, 0.0, 1.0,
                                                       op.mult, op.add),
           reads=[b_st], writes=[b_st])
        EM("dve", lambda e, SCR8=SCR8, EQ=EQ: e.tensor_tensor_scan(
            SCR8, SCR8, EQ, 0.0, op.mult, op.add),
           reads=[b_st], writes=[b_st])
        EM("dve", lambda e, SCR8=SCR8, EQ=EQ: e.tensor_tensor(SCR8, SCR8, EQ,
                                                              op.subtract),
           reads=[b_st], writes=[b_st])
        EM("dve", lambda e, SCR8=SCR8, sc=sc: e.tensor_scalar(
            SCR8, SCR8, sc(C_THR), None, op.is_lt),
           reads=[b_st], writes=[b_st])
        EM("dve", lambda e, EQ=EQ, SCR8=SCR8: e.tensor_tensor(EQ, EQ, SCR8,
                                                              op.mult),
           reads=[b_st], writes=[b_st])
        EM("dve", lambda e, MQ=MQ, EQ=EQ: e.tensor_tensor(MQ, MQ, EQ, op.add),
           reads=[b_st], writes=[b_w8[c]])
        EM("dve", lambda e, MQ=MQ, LG=LG: e.tensor_tensor(MQ, MQ, LG, op.mult),
           reads=[b_st], writes=[b_w8[c]])
        EM("dve", lambda e, MQ=MQ, EQ=EQ, sc=sc: e.tensor_scalar(
            EQ, MQ, 1.0, None, op.mult, op.add, accum_out=sc(C_WSUM)),
           reads=[b_w8[c]], writes=[b_st], touch=sc(C_WSUM))
        EM("dve", lambda e, sc=sc: e.tensor_scalar(sc(C_WSUM), sc(C_WSUM), 1e-20,
                                                   None, op.add),
           reads=[b_st], writes=[b_st])
        EM("dve", lambda e, sc=sc: e.reciprocal(sc(C_WINV), sc(C_WSUM)),
           reads=[b_st], writes=[b_st])
        EM("dve", lambda e, MQ=MQ, sc=sc: e.tensor_scalar(MQ, MQ, sc(C_WINV), None,
                                                          op.mult),
           reads=[b_st], writes=[b_w8[c]])

    emit_tp(0)
    for c in range(NCH):
        if c + 1 < NCH:
            emit_tp(c + 1)
        emit_router(c)
    if DEV:
        EM("sp", lambda e: e.dma_start(dbg_w8, w8n[:]), reads=list(b_w8), dma=True)

    for e0 in range(2):
        EM("pool", lambda e, e0=e0: e.dma_start(gup_sb[:, e0, :], dgup[e0]),
           writes=[b_gup[e0]], dma=True)
        EM("sp", lambda e, e0=e0: e.dma_start(down_sb[:, e0, :], ddown[e0]),
           writes=[b_down[e0]], dma=True)

    NCHB = 4
    for ex in range(NEXP):
        wslot = ex % 2
        last = (ex == NEXP - 1)
        for cb in range(NCHB):
            cpar = cb % 2
            tok = slice(cb * 512, cb * 512 + 512)
            PGX, b_pgx = (pg, b_pg) if cpar == 0 else (pu, b_pu)
            PUX, b_pux = (pu, b_pu) if cpar == 0 else (pg, b_pg)
            fmc = [b_o1fm[t] for t in range(cb * 4, cb * 4 + 4)]
            for it in range(4):
                psl = PGX[:, it * 512:(it + 1) * 512]
                for kt in range(8):
                    EM("pe", lambda e, psl=psl, kt=kt, it=it, tok=tok, w=wslot:
                       e.matmul(psl,
                                gup_sb[:, w, kt * 512 + it * 128:
                                       kt * 512 + it * 128 + 128],
                                out1_fm[:, kt, tok],
                                start=(kt == 0), stop=(kt == 7)),
                       reads=[b_gup[wslot]] + fmc, writes=[b_pgx[it]])
                EM("act", lambda e, psl=psl, it=it: e.activation(
                    sg_sb[:, it, :], psl, AF.Silu),
                   reads=[b_pgx[it]], writes=[b_sg[it]])
                pul = PUX[:, it * 512:(it + 1) * 512]
                for kt in range(8):
                    EM("pe", lambda e, pul=pul, kt=kt, it=it, tok=tok, w=wslot:
                       e.matmul(pul,
                                gup_sb[:, w, 4096 + kt * 512 + it * 128:
                                       4096 + kt * 512 + it * 128 + 128],
                                out1_fm[:, kt, tok],
                                start=(kt == 0), stop=(kt == 7)),
                       reads=[b_gup[wslot]] + fmc, writes=[b_pux[it]])
                EM("dve", lambda e, pul=pul, it=it, cpar=cpar: e.tensor_tensor(
                    a_sl[:, cpar, it, :], sg_sb[:, it, :], pul, op.mult),
                   reads=[b_sg[it], b_pux[it]], writes=[b_asl[cpar]])
            for lt in range(4):
                c = cb * 4 + lt
                lsl = slice(lt * 128, lt * 128 + 128)
                dbase = (lt % 2) * 1024
                for half in range(2):
                    dsl = PGX[:, dbase + half * 512:dbase + half * 512 + 512]
                    bank = (dbase + half * 512) // 512
                    for it in range(4):
                        EM("pe", lambda e, dsl=dsl, it=it, lsl=lsl, half=half,
                           w=wslot, cpar=cpar: e.matmul(
                               dsl, a_sl[:, cpar, it, lsl],
                               down_sb[:, w, it * 1024 + half * 512:
                                       it * 1024 + half * 512 + 512],
                               start=(it == 0), stop=(it == 3)),
                           reads=[b_asl[cpar], b_down[wslot]],
                           writes=[b_pgx[bank]])
                dfull = PGX[:, dbase:dbase + 1024]
                bk0, bk1 = dbase // 512, dbase // 512 + 1
                if ex < 8:
                    EM("dve", lambda e, dfull=dfull, c=c, ex=ex:
                       e.scalar_tensor_tensor(
                           out1_tm[:, c, :], dfull, w8n[:, c, ex:ex + 1],
                           out1_tm[:, c, :], op.mult, op.add),
                       reads=[b_pgx[bk0], b_pgx[bk1], b_w8[c], b_out1[c]],
                       writes=[b_out1[c]])
                else:
                    EM("dve", lambda e, dfull=dfull, c=c: e.tensor_tensor(
                        out1_tm[:, c, :], out1_tm[:, c, :], dfull, op.add),
                       reads=[b_pgx[bk0], b_pgx[bk1], b_out1[c]],
                       writes=[b_out1[c]])
                if last:
                    EM("dve", lambda e, c=c: e.tensor_scalar(
                        bounce[:, 0, :], out1_tm[:, c, :], 0.0, None, op.add,
                        op.add, accum_out=statB[:, 48:49]),
                       reads=[b_out1[c]], writes=[b_bnc[0], b_statB],
                       touch=statB[:, 48:49])
                    EM("act", lambda e, c=c: e.activation(
                        bounce[:, 1, :], out1_tm[:, c, :], AF.Square,
                        accum_out=statB[:, 49:50]),
                       reads=[b_out1[c]], writes=[b_bnc[1], b_statB],
                       touch=statB[:, 49:50])
                    EM("dve", lambda e, c=c: e.tensor_scalar(
                        statB[:, c:c + 1], statB[:, 48:49], 1.0 / D, None, op.mult),
                       reads=[b_statB], writes=[b_statB])
                    EM("dve", lambda e, c=c: e.tensor_scalar(
                        statB[:, 50:51], statB[:, c:c + 1], statB[:, c:c + 1],
                        None, op.mult),
                       reads=[b_statB], writes=[b_statB])
                    EM("dve", lambda e, c=c: e.tensor_scalar(
                        statB[:, 16 + c:17 + c], statB[:, 49:50], 1.0 / D,
                        statB[:, 50:51], op.mult, op.subtract),
                       reads=[b_statB], writes=[b_statB])
        if ex + 2 < NEXP:
            EM("pool", lambda e, ex=ex, w=(ex + 2) % 2: e.dma_start(
                gup_sb[:, w, :], dgup[ex + 2]), writes=[b_gup[(ex + 2) % 2]],
               dma=True)
            EM("sp", lambda e, ex=ex, w=(ex + 2) % 2: e.dma_start(
                down_sb[:, w, :], ddown[ex + 2]), writes=[b_down[(ex + 2) % 2]],
               dma=True)

    if DEV:
        for c in range(NCH):
            EM("sp", lambda e, c=c: e.dma_start(dbg_acc[c * 128:(c + 1) * 128, :],
                                                out1_tm[:, c, :]),
               reads=[b_out1[c]], dma=True)

    EM("act", lambda e: e.activation(statB[:, 32:48], statB[:, 16:32], AF.Ln,
                                     bias=epsc[:, 0:1]),
       reads=[b_statB, b_tab], writes=[b_statB])
    EM("act", lambda e: e.activation(statB[:, 32:48], statB[:, 32:48], AF.Exp,
                                     scale=-0.5),
       reads=[b_statB], writes=[b_statB])
    for c in range(NCH):
        bp = c % 2
        EM("dve", lambda e, c=c, bp=bp: e.tensor_scalar(
            bounce[:, bp, :], out1_tm[:, c, :], statB[:, c:c + 1],
            statB[:, 32 + c:33 + c], op.subtract, op.mult),
           reads=[b_out1[c], b_statB], writes=[b_bnc[bp]])
        EM("sp", lambda e, c=c, bp=bp: e.dma_start(
            dout[c * 128:(c + 1) * 128, :], bounce[:, bp, :]),
           reads=[b_bnc[bp]], writes=[b_outs[c]], dma=True)

    P.emit("sp", None, reads=list(b_outs))

    P.finalize(nc, stack)
    block = stack.enter_context(nc.Block())
    with stack, stB:
        @block.tensor
        def _(eng):
            P.replay("pe", eng)

        @block.vector
        def _(eng):
            P.replay("dve", eng)

        @block.scalar
        def _(eng):
            P.replay("act", eng)

        @block.gpsimd
        def _(eng):
            P.replay("pool", eng)

        @block.sync
        def _(eng):
            P.replay("sp", eng)

    return nc



def _mk_inputs(core, ins):
    x = np.asarray(ins["L0_attn_x"])
    b = core
    W_in = np.asarray(ins["L0_attn_W_in"])
    win_r = np.ascontiguousarray(W_in.T.reshape(8, 128, 1024).transpose(1, 0, 2))
    xb = np.ascontiguousarray(x[b])
    xfm = np.ascontiguousarray(
        x[b].T.reshape(8, 128, 16, 128).transpose(2, 1, 0, 3).reshape(16, 128, 1024))
    cre_aug = np.concatenate([np.asarray(ins["L0_attn_C_re"]).T,
                              np.asarray(ins["L0_attn_D_re"])[None, :]], axis=0)
    ab = np.stack([np.asarray(ins["L0_attn_A_re"]), np.asarray(ins["L0_attn_A_im"]),
                   np.asarray(ins["L0_attn_B_re"]), np.asarray(ins["L0_attn_B_im"])],
                  axis=1).astype(np.float32)
    rw_r = np.ascontiguousarray(
        np.asarray(ins["L0_ffn_router_w"]).reshape(8, 128, 8).transpose(1, 0, 2))

    def tile_w(w):
        I = w.shape[1]
        return np.ascontiguousarray(
            w.reshape(8, 128, I).transpose(1, 0, 2).reshape(128, 8 * I))

    def tile_d(w):
        return np.ascontiguousarray(
            w.reshape(4, 128, 1024).transpose(1, 0, 2).reshape(128, 4096))

    gate = np.asarray(ins["L0_ffn_gate_w"])
    up = np.asarray(ins["L0_ffn_up_w"])
    down = np.asarray(ins["L0_ffn_down_w"])
    sgw = np.asarray(ins["L0_ffn_s_gate_w"])
    suw = np.asarray(ins["L0_ffn_s_up_w"])
    sdw = np.asarray(ins["L0_ffn_s_down_w"])
    gup = np.stack(
        [np.concatenate([tile_w(gate[e]), tile_w(up[e])], axis=1) for e in range(8)]
        + [np.concatenate([tile_w(sgw[:, q * 512:(q + 1) * 512]),
                           tile_w(suw[:, q * 512:(q + 1) * 512])], axis=1)
           for q in range(4)], axis=0)
    down_r = np.stack([tile_d(down[e]) for e in range(8)]
                      + [tile_d(sdw[q * 512:(q + 1) * 512]) for q in range(4)],
                      axis=0)

    return {
        "x_tm": xb,
        "x_fm": xfm,
        "w_in_r": win_r,
        "b_in_row": np.ascontiguousarray(np.asarray(ins["L0_attn_b_in"])[None, :]),
        "gamma_row": np.ascontiguousarray(np.asarray(ins["L0_attn_gamma"])[None, :]),
        "beta_row": np.ascontiguousarray(np.asarray(ins["L0_attn_beta"])[None, :]),
        "ab_vecs": ab,
        "d_row": np.ascontiguousarray(np.stack([
            np.asarray(ins["L0_attn_D_re"]),
            np.asarray(ins["L0_attn_D_im"])])),
        "cre_aug": np.ascontiguousarray(cre_aug),
        "cim_t": np.ascontiguousarray(np.asarray(ins["L0_attn_C_im"]).T),
        "dft_mats": _dft_mats()[0],
        "dft_lo": _dft_mats()[1],
        "ones_row": np.ones((1, 128), np.float32),
        "ones_row_f": np.ones((1, 128), np.float32),
        "ident_f": np.eye(128, dtype=np.float32),
        "ident_r": np.eye(128, dtype=np.float32),
        "rw_r": rw_r,
        "rbias": np.asarray(ins["L0_ffn_router_bias"]).reshape(1, 1)
        .astype(np.float32),
        "gup": gup,
        "down_r": down_r,
    }


def kernel(**inputs):
    nc = build_nc()
    in_maps = [_mk_inputs(b, inputs) for b in range(B_SZ)]
    res = run_bass_kernel_spmd(nc, in_maps, core_ids=list(range(B_SZ)))
    out = np.stack([np.asarray(res.results[b]["out"]) for b in range(B_SZ)], axis=0)
    return out


# version 25 (final)
# speedup vs baseline: 1.0392x; 1.0042x over previous
import contextlib
import numpy as np

import concourse.bass as bass
import concourse.mybir as mybir
from concourse.alu_op_type import AluOpType as op
from concourse.bass_utils import run_bass_kernel_spmd

F32 = mybir.dt.float32
F32R = mybir.dt.float32r
BF16 = mybir.dt.bfloat16
AF = mybir.ActivationFunctionType
AX = mybir.AxisListType

B_SZ, L, D, DS = 8, 2048, 1024, 64
T = 128
NCH = L // T
NF = 255
NEXP = 12
LN_EPS = 1e-5
DEV = False

_ENGS = ("pe", "dve", "act", "pool", "sp")


class _Rec:
    __slots__ = ("eng", "fn", "waits", "inc", "dma", "sem", "val")

    def __init__(self, eng, fn, dma):
        self.eng, self.fn, self.dma = eng, fn, dma
        self.waits = []
        self.inc = False
        self.sem = None
        self.val = None


class Buf:
    __slots__ = ("last_w", "readers", "name")

    def __init__(self, name=""):
        self.last_w = None
        self.readers = []
        self.name = name


class Prog:
    def __init__(self):
        self.recs = {e: [] for e in _ENGS}
        self.fence_pending = {e: [] for e in _ENGS}

    def emit(self, eng, fn, reads=(), writes=(), dma=False, extra_waits=(),
             touch=None):
        rec = _Rec(eng, fn, dma)
        waits = list(extra_waits)
        for b in reads:
            if b.last_w is not None:
                waits.append(b.last_w)
        for b in writes:
            if b.last_w is not None:
                waits.append(b.last_w)
            waits.extend(b.readers)
        if self.fence_pending[eng]:
            waits.extend(self.fence_pending[eng])
            self.fence_pending[eng] = []
        seen = set()
        for w in waits:
            if w is None or w is rec or id(w) in seen:
                continue
            seen.add(id(w))
            if (not w.dma) and w.eng == eng and eng in ("pe", "sp"):
                continue
            rec.waits.append(w)
            w.inc = True
        self.recs[eng].append(rec)
        for b in reads:
            b.readers.append(rec)
        for b in writes:
            b.last_w = rec
            b.readers = []
        if touch is not None:
            if eng == "act":
                fn2 = (lambda ap: lambda e: e.copy(ap, ap))(touch)
            else:
                fn2 = (lambda ap: lambda e: e.tensor_copy(ap, ap))(touch)
            return self.emit(eng, fn2, reads=list(writes), writes=list(writes))
        return rec

    def fence(self):
        tails = [self.recs[e][-1] for e in _ENGS if self.recs[e]]
        for e in _ENGS:
            self.fence_pending[e] = list(tails)

    def finalize(self, nc, stack):
        esem = {e: stack.enter_context(nc.semaphore(f"sem_{e}")) for e in _ENGS}
        NRING = 6
        dring = {q: [stack.enter_context(nc.semaphore(f"dma_{q}{i}"))
                     for i in range(NRING)] for q in ("sp", "pool")}
        dcnt = {"sp": 0, "pool": 0}
        ecnt = {e: 0 for e in _ENGS}
        for e in _ENGS:
            for rec in self.recs[e]:
                if rec.dma:
                    q = "sp" if e == "sp" else "pool"
                    rec.sem = dring[q][dcnt[q] % NRING]
                    rec.val = (dcnt[q] // NRING + 1) * 16
                    dcnt[q] += 1
                elif rec.inc:
                    ecnt[e] += 1
                    rec.sem = esem[e]
                    rec.val = ecnt[e]

    def replay(self, eng_name, eng):
        watermark = {}
        for rec in self.recs[eng_name]:
            for w in rec.waits:
                assert w.sem is not None, "wait on un-finalized record"
                key = id(w.sem)
                if watermark.get(key, 0) >= w.val:
                    continue
                eng.wait_ge(w.sem, w.val)
                watermark[key] = w.val
            if rec.dma and rec.val > 16:
                key = id(rec.sem)
                if watermark.get(key, 0) < rec.val - 16:
                    eng.wait_ge(rec.sem, rec.val - 16)
                    watermark[key] = rec.val - 16
            if rec.fn is None:
                continue
            instr = rec.fn(eng)
            if rec.sem is not None:
                instr.then_inc(rec.sem, 16 if rec.dma else 1)


def _dft_mats():
    f = np.arange(128)
    tau = np.arange(T)
    ang = 2 * np.pi * np.outer(tau, f) / NF
    cfT = np.cos(ang)
    sfT = -np.sin(ang)
    w = np.full(128, 2.0)
    w[0] = 1.0
    angi = 2 * np.pi * np.outer(f, tau) / NF
    icT = np.cos(angi) * w[:, None] / NF
    isT = -np.sin(angi) * w[:, None] / NF
    m64 = np.concatenate([cfT, sfT, icT, isT], axis=1)
    hi = m64.astype(np.float32)
    lo = (m64 - hi.astype(np.float64)).astype(np.float32)
    return hi, lo


def build_nc():
    nc = bass.Bass("TRN2", target_bir_lowering=False)
    P = Prog()
    EM = P.emit

    dx_tm = nc.dram_tensor("x_tm", [L, D], F32, kind="ExternalInput").ap()
    dx_fm = nc.dram_tensor("x_fm", [NCH, 128, D], F32, kind="ExternalInput").ap()
    dwin = nc.dram_tensor("w_in_r", [128, 8, D], F32, kind="ExternalInput").ap()
    dbin = nc.dram_tensor("b_in_row", [1, D], F32, kind="ExternalInput").ap()
    dgam = nc.dram_tensor("gamma_row", [1, D], F32, kind="ExternalInput").ap()
    dbet = nc.dram_tensor("beta_row", [1, D], F32, kind="ExternalInput").ap()
    dab = nc.dram_tensor("ab_vecs", [DS, 4], F32, kind="ExternalInput").ap()
    ddre = nc.dram_tensor("d_row", [2, D], F32, kind="ExternalInput").ap()
    dcre = nc.dram_tensor("cre_aug", [DS + 1, D], F32R, kind="ExternalInput").ap()
    dcim = nc.dram_tensor("cim_t", [DS, D], F32R, kind="ExternalInput").ap()
    ddft = nc.dram_tensor("dft_mats", [128, 512], F32R, kind="ExternalInput").ap()
    ddftl = nc.dram_tensor("dft_lo", [128, 512], F32R, kind="ExternalInput").ap()
    dones = nc.dram_tensor("ones_row", [1, 128], F32R, kind="ExternalInput").ap()
    donesf = nc.dram_tensor("ones_row_f", [1, 128], F32, kind="ExternalInput").ap()
    diden = nc.dram_tensor("ident_f", [128, 128], F32, kind="ExternalInput").ap()
    didenr = nc.dram_tensor("ident_r", [128, 128], F32R, kind="ExternalInput").ap()
    drw = nc.dram_tensor("rw_r", [128, 8, 8], F32, kind="ExternalInput").ap()
    drb = nc.dram_tensor("rbias", [1, 1], F32, kind="ExternalInput").ap()
    dgup = nc.dram_tensor("gup", [NEXP, 128, 8192], F32, kind="ExternalInput").ap()
    ddown = nc.dram_tensor("down_r", [NEXP, 128, 4096], F32R,
                           kind="ExternalInput").ap()
    dout = nc.dram_tensor("out", [L, D], F32, kind="ExternalOutput").ap()
    if DEV:
        dbg_xg = nc.dram_tensor("dbg_xg", [L, D], F32, kind="ExternalOutput").ap()
        dbg_o1 = nc.dram_tensor("dbg_o1", [L, D], F32, kind="ExternalOutput").ap()
        dbg_w8 = nc.dram_tensor("dbg_w8", [128, NCH, 8], F32,
                                kind="ExternalOutput").ap()
        dbg_y = nc.dram_tensor("dbg_y", [L, D], F32, kind="ExternalOutput").ap()
        dbg_x1 = nc.dram_tensor("dbg_x1", [L, D], F32, kind="ExternalOutput").ap()
        dbg_acc = nc.dram_tensor("dbg_acc", [L, D], F32, kind="ExternalOutput").ap()

    stack = contextlib.ExitStack()

    def S(name, shape, dt):
        return stack.enter_context(nc.sbuf_tensor(name, shape, dt))

    out1_tm = S("out1_tm", [128, NCH, 1024], F32)
    w8n = S("w8n", [128, NCH, 8], F32)
    statA = S("statA", [128, 2, 40], F32)
    iden = S("iden", [128, 128], F32)
    idenr = S("idenr", [128, 128], F32R)
    onesr = S("onesr", [1, 128], F32R)
    onesf = S("onesf", [1, 128], F32)
    epsc = S("epsc", [128, 1], F32)
    rw_sb = S("rw_sb", [128, 8, 8], F32)

    b_out1 = [Buf(f"out1_{c}") for c in range(NCH)]
    b_w8 = [Buf(f"w8_{c}") for c in range(NCH)]
    b_stat = [Buf("statA0"), Buf("statA1")]
    b_tab = Buf("tables")

    stA = contextlib.ExitStack()

    def SA(name, shape, dt):
        return stA.enter_context(nc.sbuf_tensor(name, shape, dt))

    win_sb = SA("win_sb", [128, 8, 1024], BF16)
    x_tm = SA("x_tm_sb", [128, 2, 1024], F32)
    x_fm = SA("x_fm_sb", [128, 2, 1024], BF16)
    xg = SA("xg_sb", [128, 2, 1024], F32R)
    scr = SA("scr", [128, 5, 1024], F32)
    cre_sb = SA("cre_sb", [DS + 1, 1024], F32R)
    cim_sb = SA("cim_sb", [DS, 1024], F32R)
    dft_sb = SA("dft_sb", [128, 512], F32R)
    dftl_sb = SA("dftl_sb", [128, 512], F32R)
    khat = SA("khat", [128, 2048], F32R)
    xh_im = SA("xh_im", [128, 1024], F32)
    pfreq = SA("pfreq", [128, 2048], F32R)
    mprod = SA("mprod", [DS, 2048], F32R)
    mprod2 = SA("mprod2", [DS, 2048], F32R)
    h_re = SA("h_re", [DS, 1024], F32R)
    h_im = SA("h_im", [DS, 1024], F32R)
    srow = SA("srow", [1, 1024], F32)
    dsrow = SA("dsrow", [1, 1024], F32R)
    drow = SA("drow", [1, 1024], F32)
    binrow = SA("binrow", [1, 1024], F32)
    bgam = SA("bgam", [128, 1024], F32)
    bbet = SA("bbet", [128, 1024], F32)
    pbre = SA("pbre", [DS + 1, 128], F32R)
    npbim = SA("npbim", [DS, 128], F32R)
    prevtr = SA("prevtr", [128, 64], F32R)
    prevti = SA("prevti", [128, 64], F32R)
    qre = SA("qre", [DS, 128], F32R)
    nqre = SA("nqre", [DS, 128], F32R)
    nqim = SA("nqim", [DS, 128], F32R)
    dgatre = SA("dgatre", [DS, 64], F32R)
    dgatim = SA("dgatim", [DS, 64], F32R)
    dgnatim = SA("dgnatim", [DS, 64], F32R)
    imask = SA("imask", [DS, 64], F32)

    powt = scr[0:DS, 0, :]
    rowsc = scr[0:1, 1, :]
    abv = scr[0:DS, 3, 0:4]
    krsb = pfreq[:, 0:1024]
    revr = mprod[:, 0:128]
    revi = mprod[:, 128:256]

    ps_a = stA.enter_context(nc.psum_tensor("ps_a", [128, 1024], F32))
    ps_b = stA.enter_context(nc.psum_tensor("ps_b", [128, 1024], F32))
    ps_c = stA.enter_context(nc.psum_tensor("ps_c", [128, 1024], F32))
    px1 = stA.enter_context(nc.psum_tensor("px1", [128, 1024], F32))

    b_win = Buf("win")
    b_xtm = [Buf(f"xtm{i}") for i in range(2)]
    b_xfm = [Buf(f"xfm{i}") for i in range(2)]
    b_xg = [Buf(f"xg{i}") for i in range(2)]
    b_scr = [Buf(f"scr{i}") for i in range(6)]
    b_khat = Buf("khat")
    b_xhim = Buf("xh_im")
    b_pfreq = [Buf("pfreq0"), Buf("pfreq1")]
    b_m1 = [Buf("mprod_0"), Buf("mprod_1")]
    b_m2 = [Buf("mprod2_0"), Buf("mprod2_1")]
    b_h = Buf("h")
    b_srow = Buf("srow")
    b_dsrow = Buf("dsrow")
    b_bc = Buf("bcast_rows")
    b_psa = [Buf("ps_a0"), Buf("ps_a1")]
    b_psb = [Buf("ps_b0"), Buf("ps_b1")]
    b_psc = [Buf("ps_c0"), Buf("ps_c1")]
    b_px1 = [Buf("px1_0"), Buf("px1_1")]
    b_rows = Buf("rowsc")

    EM("sp", lambda e: e.dma_start(abv[:], dab), writes=[b_tab], dma=True)
    EM("sp", lambda e: e.dma_start(cre_sb[:], dcre), writes=[b_tab], dma=True)
    EM("sp", lambda e: e.dma_start(cim_sb[:], dcim), writes=[b_tab], dma=True)
    EM("sp", lambda e: e.dma_start(dft_sb[:], ddft), writes=[b_tab], dma=True)
    EM("sp", lambda e: e.dma_start(dftl_sb[:], ddftl), writes=[b_tab], dma=True)
    EM("sp", lambda e: e.dma_start(idenr[:], didenr), writes=[b_tab], dma=True)
    EM("sp", lambda e: e.dma_start(onesr[:], dones), writes=[b_tab], dma=True)
    EM("sp", lambda e: e.dma_start(onesf[:], donesf), writes=[b_tab], dma=True)
    EM("sp", lambda e: e.dma_start(iden[:], diden), writes=[b_tab], dma=True)
    EM("sp", lambda e: e.dma_start(drow[:], ddre[0:1, :]), writes=[b_tab], dma=True)
    EM("sp", lambda e: e.dma_start(binrow[:], dbin), writes=[b_bc], dma=True)
    EM("sp", lambda e: e.dma_start(rw_sb[:], drw), writes=[b_tab], dma=True)
    EM("pool", lambda e: e.dma_start(win_sb[:], dwin), writes=[b_win], dma=True)
    EM("sp", lambda e: e.dma_start(x_tm[:, 0, :], dx_tm[0:128, :]),
       writes=[b_xtm[0]], dma=True)
    EM("sp", lambda e: e.dma_start(x_tm[:, 1, :], dx_tm[128:256, :]),
       writes=[b_xtm[1]], dma=True)
    EM("pool", lambda e: e.dma_start(x_fm[:, 0, :], dx_fm[0]),
       writes=[b_xfm[0]], dma=True)

    EM("sp", lambda e: e.dma_start(rowsc[:], dgam), writes=[b_rows], dma=True)
    for half in range(2):
        sl = slice(half * 512, half * 512 + 512)
        EM("pe", lambda e, sl=sl: e.matmul(ps_a[:, sl], onesf[:], rowsc[:, sl],
                                           start=True, stop=True),
           reads=[b_rows, b_tab], writes=[b_psa[half]])
    EM("dve", lambda e: e.tensor_copy(bgam[:], ps_a[:]), reads=b_psa, writes=[b_bc])
    EM("sp", lambda e: e.dma_start(rowsc[:], dbet), writes=[b_rows], dma=True)
    for half in range(2):
        sl = slice(half * 512, half * 512 + 512)
        EM("pe", lambda e, sl=sl: e.matmul(ps_b[:, sl], onesf[:], rowsc[:, sl],
                                           start=True, stop=True),
           reads=[b_rows, b_tab], writes=[b_psb[half]])
    EM("dve", lambda e: e.tensor_copy(bbet[:], ps_b[:]), reads=b_psb, writes=[b_bc])
    EM("sp", lambda e: e.dma_start(rowsc[:, 0:1], drb), writes=[b_rows], dma=True)
    EM("pe", lambda e: e.matmul(px1[:, 0:1], onesf[:], rowsc[:, 0:1],
                                start=True, stop=True),
       reads=[b_rows, b_tab], writes=[b_px1[0]])
    EM("dve", lambda e: e.tensor_copy(statA[:, 0, 15:16], px1[:, 0:1]),
       reads=[b_px1[0]], writes=[b_stat[0]])
    EM("dve", lambda e: e.tensor_copy(statA[:, 1, 15:16], px1[:, 0:1]),
       reads=[b_px1[0]], writes=[b_stat[1]])

    for half in range(2):
        sl = slice(half * 512, half * 512 + 512)
        for kt in range(8):
            EM("pe", lambda e, sl=sl, kt=kt: e.matmul(
                px1[:, sl], x_fm[:, 0, kt * 128:kt * 128 + 128],
                win_sb[:, kt, sl], start=(kt == 0), stop=False),
               reads=[b_xfm[0], b_win], writes=[b_px1[half]])
        EM("pe", lambda e, sl=sl: e.matmul(
            px1[:, sl], onesf[:], binrow[:, sl], start=False, stop=True),
           reads=[b_bc], writes=[b_px1[half]])
    EM("pool", lambda e: e.dma_start(x_fm[:, 1, :], dx_fm[1]),
       writes=[b_xfm[1]], dma=True)

    PRE = powt[:, 0:161]
    PIM = powt[:, 192:353]
    are, aim = abv[:, 0:1], abv[:, 1:2]
    bre, bim = abv[:, 2:3], abv[:, 3:4]

    def mkt(fn):
        EM("dve", fn, reads=[b_tab], writes=[b_tab])

    EM("dve", lambda e: e.memset(PRE[:, 0:1], 1.0), writes=[b_tab])
    EM("dve", lambda e: e.memset(PIM[:, 0:1], 0.0), writes=[b_tab])
    mkt(lambda e: e.tensor_copy(PRE[:, 1:2], are))
    mkt(lambda e: e.tensor_copy(PIM[:, 1:2], aim))
    have = 2
    while have < 161:
        n = min(have - 1, 161 - have)
        s_re = PRE[:, have - 1:have]
        s_im = PIM[:, have - 1:have]
        pr1 = PRE[:, 1:1 + n]
        pi1 = PIM[:, 1:1 + n]
        dst_r = PRE[:, have:have + n]
        dst_i = PIM[:, have:have + n]
        t0 = powt[:, 354:354 + n]
        mkt(lambda e, a=dst_r, b=pr1, s=s_re: e.tensor_scalar(a, b, s, None, op.mult))
        mkt(lambda e, a=t0, b=pi1, s=s_im: e.tensor_scalar(a, b, s, None, op.mult))
        mkt(lambda e, a=dst_r, b=t0: e.tensor_tensor(a, a, b, op.subtract))
        mkt(lambda e, a=dst_i, b=pr1, s=s_im: e.tensor_scalar(a, b, s, None, op.mult))
        mkt(lambda e, a=t0, b=pi1, s=s_re: e.tensor_scalar(a, b, s, None, op.mult))
        mkt(lambda e, a=dst_i, b=t0: e.tensor_tensor(a, a, b, op.add))
        have += n

    t1 = powt[:, 354:482]
    mkt(lambda e: e.tensor_scalar(pbre[0:DS, :], PRE[:, 0:128], bre, None, op.mult))
    mkt(lambda e: e.tensor_scalar(t1, PIM[:, 0:128], bim, None, op.mult))
    mkt(lambda e: e.scalar_tensor_tensor(pbre[0:DS, :], pbre[0:DS, :].bitcast(F32),
                                         1.0, t1, op.mult, op.subtract))
    EM("dve", lambda e: e.memset(pbre[DS:DS + 1, :].bitcast(F32), 1.0), writes=[b_tab])
    mkt(lambda e: e.tensor_scalar(npbim[:], PRE[:, 0:128], bim, None, op.mult))
    mkt(lambda e: e.tensor_scalar(t1, PIM[:, 0:128], bre, None, op.mult))
    mkt(lambda e: e.scalar_tensor_tensor(npbim[:], npbim[:].bitcast(F32), -1.0, t1,
                                         op.mult, op.subtract))
    mkt(lambda e: e.tensor_copy(revr[:], pbre[0:DS, ::-1].bitcast(F32)))
    mkt(lambda e: e.tensor_scalar(revi[:], npbim[:, ::-1].bitcast(F32), -1.0, None,
                                  op.mult))
    EM("pe", lambda e: e.transpose(ps_a[:, 0:64].bitcast(F32R), revr[:], idenr[0:64, 0:64]),
       reads=[b_tab], writes=[b_psa[0]])
    EM("pe", lambda e: e.transpose(ps_a[:, 512:576].bitcast(F32R), revi[:], idenr[0:64, 0:64]),
       reads=[b_tab], writes=[b_psa[1]])
    EM("dve", lambda e: e.tensor_copy(prevtr[:], ps_a[:, 0:64].bitcast(F32R)),
       reads=[b_psa[0]], writes=[b_tab])
    EM("dve", lambda e: e.tensor_copy(prevti[:], ps_a[:, 512:576].bitcast(F32R)),
       reads=[b_psa[1]], writes=[b_tab])
    mkt(lambda e: e.tensor_copy(qre[:], PRE[:, 1:129]))
    mkt(lambda e: e.tensor_scalar(nqre[:], PRE[:, 1:129], -1.0, None, op.mult))
    mkt(lambda e: e.tensor_scalar(nqim[:], PIM[:, 1:129], -1.0, None, op.mult))
    EM("pool", lambda e: e.iota(imask[:], [[1, 64]], base=0, channel_multiplier=-1,
                                allow_small_or_imprecise_dtypes=True),
       writes=[b_tab])
    mkt(lambda e: e.tensor_scalar(imask[:], imask[:], 0.0, None, op.is_equal))
    atre = PRE[:, 128:129]
    atim = PIM[:, 128:129]
    mkt(lambda e: e.tensor_scalar(dgatre[:], imask[:], atre, None, op.mult))
    mkt(lambda e: e.tensor_scalar(dgatim[:], imask[:], atim, None, op.mult))
    mkt(lambda e: e.tensor_scalar(dgnatim[:], imask[:], atim, -1.0, op.mult, op.mult))

    for half in range(2):
        sl = slice(half * 512, half * 512 + 512)
        EM("pe", lambda e, sl=sl: e.matmul(ps_b[:, sl], pbre[:], cre_sb[:, sl],
                                           start=True, stop=False),
           reads=[b_tab], writes=[b_psb[half]])
        EM("pe", lambda e, sl=sl: e.matmul(ps_b[:, sl], npbim[:], cim_sb[:, sl],
                                           start=False, stop=True),
           reads=[b_tab], writes=[b_psb[half]])
    EM("act", lambda e: e.copy(krsb[:], ps_b[:]), reads=b_psb, writes=[b_tab])
    for half in range(2):
        sl = slice(half * 512, half * 512 + 512)
        EM("pe", lambda e, sl=sl: e.matmul(ps_c[:, sl], dft_sb[:, 0:128],
                                           krsb[:, sl], start=True, stop=False),
           reads=[b_tab], writes=[b_psc[half]])
        EM("pe", lambda e, sl=sl: e.matmul(ps_c[:, sl], dftl_sb[:, 0:128],
                                           krsb[:, sl], start=False, stop=True),
           reads=[b_tab], writes=[b_psc[half]])
    EM("act", lambda e: e.copy(khat[:, 0:1024], ps_c[:]), reads=b_psc,
       writes=[b_khat])
    for half in range(2):
        sl = slice(half * 512, half * 512 + 512)
        EM("pe", lambda e, sl=sl: e.matmul(ps_c[:, sl], dft_sb[:, 128:256],
                                           krsb[:, sl], start=True, stop=False),
           reads=[b_tab], writes=[b_psc[half]])
        EM("pe", lambda e, sl=sl: e.matmul(ps_c[:, sl], dftl_sb[:, 128:256],
                                           krsb[:, sl], start=False, stop=True),
           reads=[b_tab], writes=[b_psc[half]])
    EM("act", lambda e: e.copy(khat[:, 1024:2048], ps_c[:]), reads=b_psc,
       writes=[b_khat])

    EM("dve", lambda e: e.memset(epsc[:], LN_EPS), writes=[b_tab])
    EM("dve", lambda e: e.memset(w8n[:], 0.0), writes=list(b_w8))
    EM("dve", lambda e: e.memset(h_re[:].bitcast(F32), 0.0), writes=[b_h])
    EM("dve", lambda e: e.memset(h_im[:].bitcast(F32), 0.0), writes=[b_h])
    EM("dve", lambda e: e.memset(srow[:], 0.0), writes=[b_srow])
    P.fence()

    (C_SUM0, C_SUM1, C_SUM, C_MU, C_SSQ, C_VAR, C_RST, C_NMX, C_SEXP, C_RSUM,
     C_CGT, C_THR, C_WSUM, C_WINV, C_TMP, C_RB) = range(16)

    KRE = khat[:, 0:1024]
    KIM = khat[:, 1024:2048]

    def _env(c):
        p = c % 2

        class E:
            pass

        v = E()
        v.p = p
        v.sc = lambda i, p=p: statA[:, p, i:i + 1]
        v.s0 = scr[:, 2 * p + 0, :]
        v.s1 = scr[:, 2 * p + 1, :]
        v.s2 = scr[:, 4, :]
        v.b_s0, v.b_s1 = b_scr[2 * p:2 * p + 2]
        v.b_s2 = b_scr[4]
        v.b_st = b_stat[p]
        v.xg_c = xg[:, p, :]
        v.xtm_c = x_tm[:, p, :]
        v.xfm_c = x_fm[:, p, :]
        v.ds_c = dsrow[:]
        v.b_ds = b_dsrow
        return v

    def emit_x1(c):
        v = _env(c)
        if c + 1 < NCH:
            pn = (c + 1) % 2
            EM("pool", lambda e, c=c, pn=pn: e.dma_start(
                x_fm[:, pn, :], dx_fm[c + 1]), writes=[b_xfm[pn]], dma=True)
        for half in range(2):
            sl = slice(half * 512, half * 512 + 512)
            for kt in range(8):
                EM("pe", lambda e, sl=sl, kt=kt, xfm_c=v.xfm_c: e.matmul(
                    px1[:, sl], xfm_c[:, kt * 128:kt * 128 + 128],
                    win_sb[:, kt, sl], start=(kt == 0), stop=False),
                   reads=[b_xfm[v.p], b_win], writes=[b_px1[half]])
            EM("pe", lambda e, sl=sl: e.matmul(
                px1[:, sl], onesf[:], binrow[:, sl], start=False, stop=True),
               reads=[b_bc], writes=[b_px1[half]])

    def emit_ln1(c):
        v = _env(c)
        s0, s1, sc = v.s0, v.s1, v.sc
        b_s0, b_s1, b_st = v.b_s0, v.b_s1, v.b_st
        EM("dve", lambda e, s0=s0, sc=sc: e.tensor_scalar(
            s0, px1[:], 0.0, None, op.add, op.add, accum_out=sc(C_SUM)),
           reads=b_px1, writes=[b_s0, b_st], touch=sc(C_SUM))
        if DEV:
            EM("sp", lambda e, c=c, s0=s0: e.dma_start(
                dbg_x1[c * 128:(c + 1) * 128, :], s0), reads=[b_s0], dma=True)
        EM("act", lambda e, s0=s0, s1=s1, sc=sc: e.activation(
            s1, s0, AF.Square, accum_out=sc(C_SSQ)),
           reads=[b_s0], writes=[b_s1, b_st], touch=sc(C_SSQ))
        EM("dve", lambda e, sc=sc: e.tensor_scalar(sc(C_MU), sc(C_SUM), 1.0 / D,
                                                   None, op.mult),
           reads=[b_st], writes=[b_st])
        EM("dve", lambda e, sc=sc: e.tensor_scalar(sc(C_TMP), sc(C_MU), sc(C_MU),
                                                   None, op.mult),
           reads=[b_st], writes=[b_st])
        EM("dve", lambda e, sc=sc: e.tensor_scalar(sc(C_VAR), sc(C_SSQ), 1.0 / D,
                                                   sc(C_TMP), op.mult, op.subtract),
           reads=[b_st], writes=[b_st])
        EM("act", lambda e, sc=sc: e.activation(sc(C_TMP), sc(C_VAR), AF.Ln,
                                                bias=epsc[:, 0:1]),
           reads=[b_st], writes=[b_st])
        EM("act", lambda e, sc=sc: e.activation(sc(C_RST), sc(C_TMP), AF.Exp,
                                                scale=-0.5),
           reads=[b_st], writes=[b_st])
        EM("dve", lambda e, s0=s0, s1=s1, sc=sc: e.tensor_scalar(
            s1, s0, sc(C_MU), sc(C_RST), op.subtract, op.mult),
           reads=[b_s0, b_st], writes=[b_s1])
        EM("pool", lambda e, s0=s0, s1=s1: e.tensor_tensor(s0, s1, bgam[:],
                                                           op.mult),
           reads=[b_s1, b_bc], writes=[b_s0])
        EM("pool", lambda e, s0=s0, xg_c=v.xg_c: e.tensor_tensor(
            xg_c, s0, bbet[:], op.add),
           reads=[b_s0, b_bc], writes=[b_xg[v.p]])
        if DEV:
            EM("sp", lambda e, c=c, xg_c=v.xg_c: e.dma_start(
                dbg_xg[c * 128:(c + 1) * 128, :], xg_c.bitcast(F32)),
               reads=[b_xg[v.p]], dma=True)

    def emit_dft(c):
        v = _env(c)
        for half in range(2):
            sl = slice(half * 512, half * 512 + 512)
            EM("pe", lambda e, sl=sl, xg_c=v.xg_c: e.matmul(
                ps_a[:, sl], dft_sb[:, 0:128], xg_c[:, sl], start=True,
                stop=False),
               reads=[b_xg[v.p], b_tab], writes=[b_psa[half]])
            EM("pe", lambda e, sl=sl, xg_c=v.xg_c: e.matmul(
                ps_a[:, sl], dftl_sb[:, 0:128], xg_c[:, sl], start=False,
                stop=True),
               reads=[b_xg[v.p], b_tab], writes=[b_psa[half]])
            EM("pe", lambda e, sl=sl, xg_c=v.xg_c: e.matmul(
                ps_b[:, sl], dft_sb[:, 128:256], xg_c[:, sl], start=True,
                stop=False),
               reads=[b_xg[v.p], b_tab], writes=[b_psb[half]])
            EM("pe", lambda e, sl=sl, xg_c=v.xg_c: e.matmul(
                ps_b[:, sl], dftl_sb[:, 128:256], xg_c[:, sl], start=False,
                stop=True),
               reads=[b_xg[v.p], b_tab], writes=[b_psb[half]])

    def emit_prod(c):
        v = _env(c)
        s2, sc = v.s2, v.sc
        b_s2, b_st = v.b_s2, v.b_st
        EM("dve", lambda e, ds_c=v.ds_c: e.tensor_tensor(ds_c, drow[:], srow[:],
                                                         op.mult),
           reads=[b_srow, b_tab], writes=[v.b_ds])
        EM("dve", lambda e: e.tensor_tensor(srow[:], srow[:], ps_a[0:1, :],
                                            op.add),
           reads=b_psa, writes=[b_srow])
        EM("act", lambda e: e.copy(xh_im[:], ps_b[:]), reads=b_psb,
           writes=[b_xhim])
        for half in range(2):
            sl = slice(half * 512, half * 512 + 512)
            s2h = s2[:, sl]
            PRE_F = pfreq[:, half * 1024:half * 1024 + 512]
            PIM_F = pfreq[:, half * 1024 + 512:half * 1024 + 1024]
            M_RR = mprod[:, half * 1024:half * 1024 + 512]
            M_II = mprod[:, half * 1024 + 512:half * 1024 + 1024]
            M_IR = mprod2[:, half * 1024:half * 1024 + 512]
            M_RI = mprod2[:, half * 1024 + 512:half * 1024 + 1024]
            EM("dve", lambda e, sl=sl, PRE_F=PRE_F: e.tensor_tensor(
                PRE_F, ps_a[:, sl], KRE[:, sl].bitcast(F32), op.mult),
               reads=[b_psa[half], b_khat], writes=[b_pfreq[half]])
            EM("dve", lambda e, sl=sl, s2h=s2h: e.tensor_tensor(
                s2h, xh_im[:, sl], KIM[:, sl].bitcast(F32), op.mult),
               reads=[b_xhim, b_khat], writes=[b_s2])
            EM("dve", lambda e, s2h=s2h, PRE_F=PRE_F: e.tensor_tensor(
                PRE_F, PRE_F.bitcast(F32), s2h, op.subtract),
               reads=[b_s2], writes=[b_pfreq[half]])
            EM("dve", lambda e, sl=sl, PIM_F=PIM_F: e.tensor_tensor(
                PIM_F, ps_a[:, sl], KIM[:, sl].bitcast(F32), op.mult),
               reads=[b_psa[half], b_khat], writes=[b_pfreq[half]])
            EM("dve", lambda e, sl=sl, s2h=s2h: e.tensor_tensor(
                s2h, xh_im[:, sl], KRE[:, sl].bitcast(F32), op.mult),
               reads=[b_xhim, b_khat], writes=[b_s2])
            EM("dve", lambda e, s2h=s2h, PIM_F=PIM_F: e.tensor_tensor(
                PIM_F, PIM_F.bitcast(F32), s2h, op.add),
               reads=[b_s2], writes=[b_pfreq[half]])

    def emit_eprod(c):
        v = _env(c)
        for half in range(2):
            sl = slice(half * 512, half * 512 + 512)
            M_RR = mprod[:, half * 1024:half * 1024 + 512]
            M_II = mprod[:, half * 1024 + 512:half * 1024 + 1024]
            M_IR = mprod2[:, half * 1024:half * 1024 + 512]
            M_RI = mprod2[:, half * 1024 + 512:half * 1024 + 1024]
            EM("pool", lambda e, sl=sl, M_RR=M_RR: e.tensor_tensor(
                M_RR, cre_sb[0:DS, sl].bitcast(F32), h_re[:, sl].bitcast(F32),
                op.mult), reads=[b_h, b_tab], writes=[b_m1[half]])
            EM("pool", lambda e, sl=sl, M_II=M_II: e.tensor_tensor(
                M_II, cim_sb[:, sl].bitcast(F32), h_im[:, sl].bitcast(F32),
                op.mult), reads=[b_h, b_tab], writes=[b_m1[half]])
            EM("pool", lambda e, sl=sl, M_IR=M_IR: e.tensor_tensor(
                M_IR, cim_sb[:, sl].bitcast(F32), h_re[:, sl].bitcast(F32),
                op.mult), reads=[b_h, b_tab], writes=[b_m2[half]])
            EM("pool", lambda e, sl=sl, M_RI=M_RI: e.tensor_tensor(
                M_RI, cre_sb[0:DS, sl].bitcast(F32), h_im[:, sl].bitcast(F32),
                op.mult), reads=[b_h, b_tab], writes=[b_m2[half]])

    def emit_carry(c):
        v = _env(c)
        for half in range(2):
            sl = slice(half * 512, half * 512 + 512)
            PRE_F = pfreq[:, half * 1024:half * 1024 + 512]
            PIM_F = pfreq[:, half * 1024 + 512:half * 1024 + 1024]
            M_RR = mprod[:, half * 1024:half * 1024 + 512]
            M_II = mprod[:, half * 1024 + 512:half * 1024 + 1024]
            M_IR = mprod2[:, half * 1024:half * 1024 + 512]
            M_RI = mprod2[:, half * 1024 + 512:half * 1024 + 1024]
            EM("pe", lambda e, sl=sl, ds_c=v.ds_c: e.matmul(
                ps_c[:, sl], onesr[:], ds_c[:, sl], start=True, stop=False),
               reads=[v.b_ds, b_tab], writes=[b_psc[half]])
            EM("pe", lambda e, sl=sl, PRE_F=PRE_F: e.matmul(
                ps_c[:, sl], dft_sb[:, 256:384], PRE_F, start=False, stop=False),
               reads=[b_pfreq[half], b_tab], writes=[b_psc[half]])
            EM("pe", lambda e, sl=sl, PRE_F=PRE_F: e.matmul(
                ps_c[:, sl], dftl_sb[:, 256:384], PRE_F, start=False,
                stop=False),
               reads=[b_pfreq[half], b_tab], writes=[b_psc[half]])
            EM("pe", lambda e, sl=sl, PIM_F=PIM_F: e.matmul(
                ps_c[:, sl], dft_sb[:, 384:512], PIM_F, start=False, stop=False),
               reads=[b_pfreq[half], b_tab], writes=[b_psc[half]])
            EM("pe", lambda e, sl=sl, PIM_F=PIM_F: e.matmul(
                ps_c[:, sl], dftl_sb[:, 384:512], PIM_F, start=False,
                stop=False),
               reads=[b_pfreq[half], b_tab], writes=[b_psc[half]])
            EM("pe", lambda e, sl=sl, M_RR=M_RR: e.matmul(
                ps_c[:, sl], qre[:], M_RR, start=False, stop=False),
               reads=[b_m1[half], b_tab], writes=[b_psc[half]])
            EM("pe", lambda e, sl=sl, M_II=M_II: e.matmul(
                ps_c[:, sl], nqre[:], M_II, start=False, stop=False),
               reads=[b_m1[half], b_tab], writes=[b_psc[half]])
            EM("pe", lambda e, sl=sl, M_IR=M_IR: e.matmul(
                ps_c[:, sl], nqim[:], M_IR, start=False, stop=False),
               reads=[b_m2[half], b_tab], writes=[b_psc[half]])
            EM("pe", lambda e, sl=sl, M_RI=M_RI: e.matmul(
                ps_c[:, sl], nqim[:], M_RI, start=False, stop=True),
               reads=[b_m2[half], b_tab], writes=[b_psc[half]])

    def emit_postconv(c):
        v = _env(c)
        s0, s1, s2, sc = v.s0, v.s1, v.s2, v.sc
        b_s0, b_s1, b_s2, b_st = v.b_s0, v.b_s1, v.b_s2, v.b_st
        if DEV:
            EM("dve", lambda e, s2=s2: e.tensor_copy(s2, ps_c[:]), reads=b_psc,
               writes=[b_s2])
            EM("sp", lambda e, c=c, s2=s2: e.dma_start(
                dbg_y[c * 128:(c + 1) * 128, :], s2), reads=[b_s2], dma=True)
        EM("dve", lambda e, s0=s0, xtm_c=v.xtm_c: e.tensor_tensor(
            s0, ps_c[:], xtm_c, op.add),
           reads=b_psc + [b_xtm[v.p]], writes=[b_s0])
        EM("dve", lambda e, s0=s0, s1=s1, xg_c=v.xg_c, sc=sc:
           e.scalar_tensor_tensor(
               s1, s0, 0.0, xg_c.bitcast(F32), op.add, op.add,
               accum_out=sc(C_SUM)),
           reads=[b_s0, b_xg[v.p]], writes=[b_s1, b_st], touch=sc(C_SUM))
        EM("act", lambda e, s0=s0, s1=s1, sc=sc: e.activation(
            s0, s1, AF.Square, accum_out=sc(C_SSQ)),
           reads=[b_s1], writes=[b_s0, b_st], touch=sc(C_SSQ))
        EM("dve", lambda e, sc=sc: e.tensor_scalar(sc(C_MU), sc(C_SUM), 1.0 / D,
                                                   None, op.mult),
           reads=[b_st], writes=[b_st])
        EM("dve", lambda e, sc=sc: e.tensor_scalar(sc(C_TMP), sc(C_MU), sc(C_MU),
                                                   None, op.mult),
           reads=[b_st], writes=[b_st])
        EM("dve", lambda e, sc=sc: e.tensor_scalar(sc(C_VAR), sc(C_SSQ), 1.0 / D,
                                                   sc(C_TMP), op.mult,
                                                   op.subtract),
           reads=[b_st], writes=[b_st])
        EM("act", lambda e, sc=sc: e.activation(sc(C_TMP), sc(C_VAR), AF.Ln,
                                                bias=epsc[:, 0:1]),
           reads=[b_st], writes=[b_st])
        EM("act", lambda e, sc=sc: e.activation(sc(C_RST), sc(C_TMP), AF.Exp,
                                                scale=-0.5),
           reads=[b_st], writes=[b_st])
        EM("dve", lambda e, c=c, s1=s1, sc=sc: e.tensor_scalar(
            out1_tm[:, c, :], s1, sc(C_MU), sc(C_RST), op.subtract, op.mult),
           reads=[b_s1, b_st], writes=[b_out1[c]])
        if c + 2 < NCH:
            EM("sp", lambda e, c=c, p=v.p: e.dma_start(
                x_tm[:, p, :], dx_tm[(c + 2) * 128:(c + 3) * 128, :]),
               writes=[b_xtm[v.p]], dma=True)

    def emit_h(c):
        v = _env(c)
        for half in range(2):
            sl = slice(half * 512, half * 512 + 512)
            EM("pe", lambda e, sl=sl, xg_c=v.xg_c: e.matmul(
                ps_b[0:DS, sl], prevtr[:], xg_c[:, sl], start=True, stop=False),
               reads=[b_xg[v.p], b_tab], writes=[b_psb[half]])
            EM("pe", lambda e, sl=sl: e.matmul(
                ps_b[0:DS, sl], dgatre[:], h_re[:, sl], start=False, stop=False),
               reads=[b_h, b_tab], writes=[b_psb[half]])
            EM("pe", lambda e, sl=sl: e.matmul(
                ps_b[0:DS, sl], dgnatim[:], h_im[:, sl], start=False, stop=True),
               reads=[b_h, b_tab], writes=[b_psb[half]])
            EM("pe", lambda e, sl=sl, xg_c=v.xg_c: e.matmul(
                ps_c[0:DS, sl], prevti[:], xg_c[:, sl], start=True, stop=False),
               reads=[b_xg[v.p], b_tab], writes=[b_psc[half]])
            EM("pe", lambda e, sl=sl: e.matmul(
                ps_c[0:DS, sl], dgatre[:], h_im[:, sl], start=False, stop=False),
               reads=[b_h, b_tab], writes=[b_psc[half]])
            EM("pe", lambda e, sl=sl: e.matmul(
                ps_c[0:DS, sl], dgatim[:], h_re[:, sl], start=False, stop=True),
               reads=[b_h, b_tab], writes=[b_psc[half]])
        EM("act", lambda e: e.copy(h_re[:], ps_b[0:DS, :]), reads=b_psb,
           writes=[b_h])
        EM("act", lambda e: e.copy(h_im[:], ps_c[0:DS, :]), reads=b_psc,
           writes=[b_h])

    emit_ln1(0)
    for r in range(NCH):
        emit_dft(r)
        emit_eprod(r)
        emit_prod(r)
        if r + 1 < NCH:
            emit_x1(r + 1)
        emit_carry(r)
        if r + 1 < NCH:
            emit_ln1(r + 1)
        emit_postconv(r)
        emit_h(r)

    if DEV:
        for c in range(NCH):
            EM("sp", lambda e, c=c: e.dma_start(dbg_o1[c * 128:(c + 1) * 128, :],
                                                out1_tm[:, c, :]),
               reads=[b_out1[c]], dma=True)

    P.fence()
    stA.close()

    stB = contextlib.ExitStack()

    def SB(name, shape, dt):
        return stB.enter_context(nc.sbuf_tensor(name, shape, dt))

    out1_fm = SB("out1_fm", [128, 8, 2048], BF16)
    b_o1fm = [Buf(f"o1fm_{c}") for c in range(NCH)]
    gup_sb = SB("gup_sb", [128, 2, 8192], BF16)
    down_sb = SB("down_sb", [128, 2, 4096], F32R)
    a_sl = SB("a_sl", [128, 2, 4, 512], F32R)
    sg_sb = SB("sg_sb", [128, 4, 512], F32)
    bounce = SB("bounce", [128, 2, 1024], F32)
    statB = SB("statB", [128, 64], F32)

    pg = stB.enter_context(nc.psum_tensor("pg", [128, 2048], F32))
    pu = stB.enter_context(nc.psum_tensor("pu", [128, 2048], F32))

    b_gup = [Buf(f"gup{i}") for i in range(2)]
    b_down = [Buf(f"down{i}") for i in range(2)]
    b_asl = [Buf(f"asl{i}") for i in range(2)]
    b_sg = [Buf(f"sg{i}") for i in range(4)]
    b_bnc = [Buf(f"bnc{i}") for i in range(2)]
    b_statB = Buf("statB")
    b_pg = [Buf(f"pg{i}") for i in range(4)]
    b_pu = [Buf(f"pu{i}") for i in range(4)]
    b_outs = [Buf(f"outs{c}") for c in range(NCH)]

    s2b = SB("s2b", [128, 2, 1024], F32)
    b_s2b = [Buf("s2b0"), Buf("s2b1")]
    (C_SUM0, C_SUM1, C_SUM, C_MU, C_SSQ, C_VAR, C_RST, C_NMX, C_SEXP, C_RSUM,
     C_CGT, C_THR, C_WSUM, C_WINV, C_TMP, C_RB) = range(16)

    def emit_tp(c):
        p = c % 2
        tp = pg if (p == 0) else pu
        b_tp = b_pg if (p == 0) else b_pu
        for kt in range(8):
            EM("pe", lambda e, kt=kt, c=c, tp=tp: e.transpose(
                tp[:, kt * 128:kt * 128 + 128],
                out1_tm[:, c, kt * 128:kt * 128 + 128], iden[:]),
               reads=[b_out1[c], b_tab], writes=[b_tp[kt // 4]])
        r_act = EM("act", lambda e, c=c, tp=tp: e.copy(
            out1_fm[:, :, c * 128:c * 128 + 128], tp[:, 0:1024]),
           reads=[b_tp[0], b_tp[1]], writes=[b_o1fm[c]])
        s2 = s2b[:, p, :]
        EM("act", lambda e, s2=s2, tp=tp: e.copy(s2, tp[:, 0:1024]),
           reads=[b_tp[0], b_tp[1]], writes=[b_s2b[p]], extra_waits=[r_act])

    def emit_router(c):
        p = c % 2
        otp = pu if (p == 0) else pg
        b_otp = b_pu if (p == 0) else b_pg
        s2 = s2b[:, p, :]
        bk = 3 if ((c >> 1) & 1) == 0 else 2
        pr_c = otp[:, bk * 512 + 504:bk * 512 + 512]
        b_pr = b_otp[bk]

        def sc(i, p=p):
            return statA[:, p, i:i + 1]

        for kt in range(8):
            EM("pe", lambda e, kt=kt, s2=s2, pr_c=pr_c: e.matmul(
                pr_c, s2[:, kt * 128:kt * 128 + 128], rw_sb[:, kt, :],
                start=(kt == 0), stop=(kt == 7)),
               reads=[b_s2b[p], b_tab], writes=[b_pr])
        LG = statA[:, p, 16:24]
        SCR8 = statA[:, p, 24:32]
        SORT8 = statA[:, p, 0:8]
        TH = statA[:, p, 3:4]
        EQ = statA[:, p, 32:40]
        MQ = w8n[:, c, :]
        b_st = b_stat[p]
        EM("dve", lambda e, sc=sc, pr_c=pr_c: e.tensor_reduce(
            sc(C_NMX), pr_c, AX.X, op.max, negate=True),
           reads=[b_pr], writes=[b_st])
        EM("act", lambda e, LG=LG, sc=sc, pr_c=pr_c: e.activation(
            LG, pr_c, AF.Exp, bias=sc(C_NMX), accum_out=sc(C_SEXP)),
           reads=[b_pr, b_st], writes=[b_st], touch=sc(C_SEXP))
        EM("dve", lambda e, sc=sc: e.reciprocal(sc(C_RSUM), sc(C_SEXP)),
           reads=[b_st], writes=[b_st])
        EM("dve", lambda e, LG=LG, sc=sc: e.tensor_scalar(LG, LG, sc(C_RSUM), None,
                                                          op.mult),
           reads=[b_st], writes=[b_st])
        EM("dve", lambda e, LG=LG, SCR8=SCR8, sc=sc: e.tensor_scalar(
            SCR8, LG, sc(C_RB), None, op.add),
           reads=[b_st], writes=[b_st])
        EM("dve", lambda e, SORT8=SORT8, SCR8=SCR8: e.max(SORT8, SCR8),
           reads=[b_st], writes=[b_st])
        EM("dve", lambda e, MQ=MQ, SCR8=SCR8, TH=TH, sc=sc: e.tensor_scalar(
            MQ, SCR8, TH, None, op.is_gt, op.add, accum_out=sc(C_CGT)),
           reads=[b_st], writes=[b_w8[c], b_st], touch=sc(C_CGT))
        EM("dve", lambda e, sc=sc: e.tensor_scalar(sc(C_THR), sc(C_CGT), -1.0, 4.0,
                                                   op.mult, op.add),
           reads=[b_st], writes=[b_st])
        EM("dve", lambda e, EQ=EQ, SCR8=SCR8, TH=TH: e.tensor_scalar(
            EQ, SCR8, TH, None, op.is_equal),
           reads=[b_st], writes=[b_st])
        EM("dve", lambda e, SCR8=SCR8: e.tensor_scalar(SCR8, SCR8, 0.0, 1.0,
                                                       op.mult, op.add),
           reads=[b_st], writes=[b_st])
        EM("dve", lambda e, SCR8=SCR8, EQ=EQ: e.tensor_tensor_scan(
            SCR8, SCR8, EQ, 0.0, op.mult, op.add),
           reads=[b_st], writes=[b_st])
        EM("dve", lambda e, SCR8=SCR8, EQ=EQ: e.tensor_tensor(SCR8, SCR8, EQ,
                                                              op.subtract),
           reads=[b_st], writes=[b_st])
        EM("dve", lambda e, SCR8=SCR8, sc=sc: e.tensor_scalar(
            SCR8, SCR8, sc(C_THR), None, op.is_lt),
           reads=[b_st], writes=[b_st])
        EM("dve", lambda e, EQ=EQ, SCR8=SCR8: e.tensor_tensor(EQ, EQ, SCR8,
                                                              op.mult),
           reads=[b_st], writes=[b_st])
        EM("dve", lambda e, MQ=MQ, EQ=EQ: e.tensor_tensor(MQ, MQ, EQ, op.add),
           reads=[b_st], writes=[b_w8[c]])
        EM("dve", lambda e, MQ=MQ, LG=LG: e.tensor_tensor(MQ, MQ, LG, op.mult),
           reads=[b_st], writes=[b_w8[c]])
        EM("dve", lambda e, MQ=MQ, EQ=EQ, sc=sc: e.tensor_scalar(
            EQ, MQ, 1.0, None, op.mult, op.add, accum_out=sc(C_WSUM)),
           reads=[b_w8[c]], writes=[b_st], touch=sc(C_WSUM))
        EM("dve", lambda e, sc=sc: e.tensor_scalar(sc(C_WSUM), sc(C_WSUM), 1e-20,
                                                   None, op.add),
           reads=[b_st], writes=[b_st])
        EM("dve", lambda e, sc=sc: e.reciprocal(sc(C_WINV), sc(C_WSUM)),
           reads=[b_st], writes=[b_st])
        EM("dve", lambda e, MQ=MQ, sc=sc: e.tensor_scalar(MQ, MQ, sc(C_WINV), None,
                                                          op.mult),
           reads=[b_st], writes=[b_w8[c]])

    emit_tp(0)
    for c in range(NCH):
        if c + 1 < NCH:
            emit_tp(c + 1)
        emit_router(c)
    if DEV:
        EM("sp", lambda e: e.dma_start(dbg_w8, w8n[:]), reads=list(b_w8), dma=True)

    for e0 in range(2):
        EM("pool", lambda e, e0=e0: e.dma_start(gup_sb[:, e0, :], dgup[e0]),
           writes=[b_gup[e0]], dma=True)
        EM("sp", lambda e, e0=e0: e.dma_start(down_sb[:, e0, :], ddown[e0]),
           writes=[b_down[e0]], dma=True)

    NCHB = 4
    for ex in range(NEXP):
        wslot = ex % 2
        last = (ex == NEXP - 1)
        for cb in range(NCHB):
            cpar = cb % 2
            tok = slice(cb * 512, cb * 512 + 512)
            PGX, b_pgx = (pg, b_pg) if cpar == 0 else (pu, b_pu)
            PUX, b_pux = (pu, b_pu) if cpar == 0 else (pg, b_pg)
            fmc = [b_o1fm[t] for t in range(cb * 4, cb * 4 + 4)]
            for it in range(4):
                psl = PGX[:, it * 512:(it + 1) * 512]
                for kt in range(8):
                    EM("pe", lambda e, psl=psl, kt=kt, it=it, tok=tok, w=wslot:
                       e.matmul(psl,
                                gup_sb[:, w, kt * 512 + it * 128:
                                       kt * 512 + it * 128 + 128],
                                out1_fm[:, kt, tok],
                                start=(kt == 0), stop=(kt == 7)),
                       reads=[b_gup[wslot]] + fmc, writes=[b_pgx[it]])
                EM("act", lambda e, psl=psl, it=it: e.activation(
                    sg_sb[:, it, :], psl, AF.Silu),
                   reads=[b_pgx[it]], writes=[b_sg[it]])
                pul = PUX[:, it * 512:(it + 1) * 512]
                for kt in range(8):
                    EM("pe", lambda e, pul=pul, kt=kt, it=it, tok=tok, w=wslot:
                       e.matmul(pul,
                                gup_sb[:, w, 4096 + kt * 512 + it * 128:
                                       4096 + kt * 512 + it * 128 + 128],
                                out1_fm[:, kt, tok],
                                start=(kt == 0), stop=(kt == 7)),
                       reads=[b_gup[wslot]] + fmc, writes=[b_pux[it]])
                EM("dve", lambda e, pul=pul, it=it, cpar=cpar: e.tensor_tensor(
                    a_sl[:, cpar, it, :], sg_sb[:, it, :], pul, op.mult),
                   reads=[b_sg[it], b_pux[it]], writes=[b_asl[cpar]])
            for lt in range(4):
                c = cb * 4 + lt
                lsl = slice(lt * 128, lt * 128 + 128)
                dbase = (lt % 2) * 1024
                for half in range(2):
                    dsl = PGX[:, dbase + half * 512:dbase + half * 512 + 512]
                    bank = (dbase + half * 512) // 512
                    for it in range(4):
                        EM("pe", lambda e, dsl=dsl, it=it, lsl=lsl, half=half,
                           w=wslot, cpar=cpar: e.matmul(
                               dsl, a_sl[:, cpar, it, lsl],
                               down_sb[:, w, it * 1024 + half * 512:
                                       it * 1024 + half * 512 + 512],
                               start=(it == 0), stop=(it == 3)),
                           reads=[b_asl[cpar], b_down[wslot]],
                           writes=[b_pgx[bank]])
                dfull = PGX[:, dbase:dbase + 1024]
                bk0, bk1 = dbase // 512, dbase // 512 + 1
                if ex < 8:
                    EM("dve", lambda e, dfull=dfull, c=c, ex=ex:
                       e.scalar_tensor_tensor(
                           out1_tm[:, c, :], dfull, w8n[:, c, ex:ex + 1],
                           out1_tm[:, c, :], op.mult, op.add),
                       reads=[b_pgx[bk0], b_pgx[bk1], b_w8[c], b_out1[c]],
                       writes=[b_out1[c]])
                else:
                    EM("dve", lambda e, dfull=dfull, c=c: e.tensor_tensor(
                        out1_tm[:, c, :], out1_tm[:, c, :], dfull, op.add),
                       reads=[b_pgx[bk0], b_pgx[bk1], b_out1[c]],
                       writes=[b_out1[c]])
                if last:
                    EM("dve", lambda e, c=c: e.tensor_scalar(
                        bounce[:, 0, :], out1_tm[:, c, :], 0.0, None, op.add,
                        op.add, accum_out=statB[:, 48:49]),
                       reads=[b_out1[c]], writes=[b_bnc[0], b_statB],
                       touch=statB[:, 48:49])
                    EM("act", lambda e, c=c: e.activation(
                        bounce[:, 1, :], out1_tm[:, c, :], AF.Square,
                        accum_out=statB[:, 49:50]),
                       reads=[b_out1[c]], writes=[b_bnc[1], b_statB],
                       touch=statB[:, 49:50])
                    EM("dve", lambda e, c=c: e.tensor_scalar(
                        statB[:, c:c + 1], statB[:, 48:49], 1.0 / D, None, op.mult),
                       reads=[b_statB], writes=[b_statB])
                    EM("dve", lambda e, c=c: e.tensor_scalar(
                        statB[:, 50:51], statB[:, c:c + 1], statB[:, c:c + 1],
                        None, op.mult),
                       reads=[b_statB], writes=[b_statB])
                    EM("dve", lambda e, c=c: e.tensor_scalar(
                        statB[:, 16 + c:17 + c], statB[:, 49:50], 1.0 / D,
                        statB[:, 50:51], op.mult, op.subtract),
                       reads=[b_statB], writes=[b_statB])
        if ex + 2 < NEXP:
            EM("pool", lambda e, ex=ex, w=(ex + 2) % 2: e.dma_start(
                gup_sb[:, w, :], dgup[ex + 2]), writes=[b_gup[(ex + 2) % 2]],
               dma=True)
            EM("sp", lambda e, ex=ex, w=(ex + 2) % 2: e.dma_start(
                down_sb[:, w, :], ddown[ex + 2]), writes=[b_down[(ex + 2) % 2]],
               dma=True)

    if DEV:
        for c in range(NCH):
            EM("sp", lambda e, c=c: e.dma_start(dbg_acc[c * 128:(c + 1) * 128, :],
                                                out1_tm[:, c, :]),
               reads=[b_out1[c]], dma=True)

    EM("act", lambda e: e.activation(statB[:, 32:48], statB[:, 16:32], AF.Ln,
                                     bias=epsc[:, 0:1]),
       reads=[b_statB, b_tab], writes=[b_statB])
    EM("act", lambda e: e.activation(statB[:, 32:48], statB[:, 32:48], AF.Exp,
                                     scale=-0.5),
       reads=[b_statB], writes=[b_statB])
    for c in range(NCH):
        bp = c % 2
        EM("dve", lambda e, c=c, bp=bp: e.tensor_scalar(
            bounce[:, bp, :], out1_tm[:, c, :], statB[:, c:c + 1],
            statB[:, 32 + c:33 + c], op.subtract, op.mult),
           reads=[b_out1[c], b_statB], writes=[b_bnc[bp]])
        EM("sp", lambda e, c=c, bp=bp: e.dma_start(
            dout[c * 128:(c + 1) * 128, :], bounce[:, bp, :]),
           reads=[b_bnc[bp]], writes=[b_outs[c]], dma=True)

    P.emit("sp", None, reads=list(b_outs))

    P.finalize(nc, stack)
    block = stack.enter_context(nc.Block())
    with stack, stB:
        @block.tensor
        def _(eng):
            P.replay("pe", eng)

        @block.vector
        def _(eng):
            P.replay("dve", eng)

        @block.scalar
        def _(eng):
            P.replay("act", eng)

        @block.gpsimd
        def _(eng):
            P.replay("pool", eng)

        @block.sync
        def _(eng):
            P.replay("sp", eng)

    return nc



def _mk_inputs(core, ins):
    x = np.asarray(ins["L0_attn_x"])
    b = core
    W_in = np.asarray(ins["L0_attn_W_in"])
    win_r = np.ascontiguousarray(W_in.T.reshape(8, 128, 1024).transpose(1, 0, 2))
    xb = np.ascontiguousarray(x[b])
    xfm = np.ascontiguousarray(
        x[b].T.reshape(8, 128, 16, 128).transpose(2, 1, 0, 3).reshape(16, 128, 1024))
    cre_aug = np.concatenate([np.asarray(ins["L0_attn_C_re"]).T,
                              np.asarray(ins["L0_attn_D_re"])[None, :]], axis=0)
    ab = np.stack([np.asarray(ins["L0_attn_A_re"]), np.asarray(ins["L0_attn_A_im"]),
                   np.asarray(ins["L0_attn_B_re"]), np.asarray(ins["L0_attn_B_im"])],
                  axis=1).astype(np.float32)
    rw_r = np.ascontiguousarray(
        np.asarray(ins["L0_ffn_router_w"]).reshape(8, 128, 8).transpose(1, 0, 2))

    def tile_w(w):
        I = w.shape[1]
        return np.ascontiguousarray(
            w.reshape(8, 128, I).transpose(1, 0, 2).reshape(128, 8 * I))

    def tile_d(w):
        return np.ascontiguousarray(
            w.reshape(4, 128, 1024).transpose(1, 0, 2).reshape(128, 4096))

    gate = np.asarray(ins["L0_ffn_gate_w"])
    up = np.asarray(ins["L0_ffn_up_w"])
    down = np.asarray(ins["L0_ffn_down_w"])
    sgw = np.asarray(ins["L0_ffn_s_gate_w"])
    suw = np.asarray(ins["L0_ffn_s_up_w"])
    sdw = np.asarray(ins["L0_ffn_s_down_w"])
    gup = np.stack(
        [np.concatenate([tile_w(gate[e]), tile_w(up[e])], axis=1) for e in range(8)]
        + [np.concatenate([tile_w(sgw[:, q * 512:(q + 1) * 512]),
                           tile_w(suw[:, q * 512:(q + 1) * 512])], axis=1)
           for q in range(4)], axis=0)
    down_r = np.stack([tile_d(down[e]) for e in range(8)]
                      + [tile_d(sdw[q * 512:(q + 1) * 512]) for q in range(4)],
                      axis=0)

    return {
        "x_tm": xb,
        "x_fm": xfm,
        "w_in_r": win_r,
        "b_in_row": np.ascontiguousarray(np.asarray(ins["L0_attn_b_in"])[None, :]),
        "gamma_row": np.ascontiguousarray(np.asarray(ins["L0_attn_gamma"])[None, :]),
        "beta_row": np.ascontiguousarray(np.asarray(ins["L0_attn_beta"])[None, :]),
        "ab_vecs": ab,
        "d_row": np.ascontiguousarray(np.stack([
            np.asarray(ins["L0_attn_D_re"]),
            np.asarray(ins["L0_attn_D_im"])])),
        "cre_aug": np.ascontiguousarray(cre_aug),
        "cim_t": np.ascontiguousarray(np.asarray(ins["L0_attn_C_im"]).T),
        "dft_mats": _dft_mats()[0],
        "dft_lo": _dft_mats()[1],
        "ones_row": np.ones((1, 128), np.float32),
        "ones_row_f": np.ones((1, 128), np.float32),
        "ident_f": np.eye(128, dtype=np.float32),
        "ident_r": np.eye(128, dtype=np.float32),
        "rw_r": rw_r,
        "rbias": np.asarray(ins["L0_ffn_router_bias"]).reshape(1, 1)
        .astype(np.float32),
        "gup": gup,
        "down_r": down_r,
    }


def kernel(**inputs):
    nc = build_nc()
    in_maps = [_mk_inputs(b, inputs) for b in range(B_SZ)]
    res = run_bass_kernel_spmd(nc, in_maps, core_ids=list(range(B_SZ)))
    out = np.stack([np.asarray(res.results[b]["out"]) for b in range(B_SZ)], axis=0)
    return out


# version 28
# speedup vs baseline: 1.0456x; 1.0062x over previous
import contextlib
import numpy as np

import concourse.bass as bass
import concourse.mybir as mybir
from concourse.alu_op_type import AluOpType as op
from concourse.bass_utils import run_bass_kernel_spmd

F32 = mybir.dt.float32
F32R = mybir.dt.float32r
BF16 = mybir.dt.bfloat16
AF = mybir.ActivationFunctionType
AX = mybir.AxisListType

B_SZ, L, D, DS = 8, 2048, 1024, 64
T = 128
NCH = L // T
NF = 255
NEXP = 12
LN_EPS = 1e-5
DEV = False

_ENGS = ("pe", "dve", "act", "pool", "sp")


class _Rec:
    __slots__ = ("eng", "fn", "waits", "inc", "dma", "sem", "val")

    def __init__(self, eng, fn, dma):
        self.eng, self.fn, self.dma = eng, fn, dma
        self.waits = []
        self.inc = False
        self.sem = None
        self.val = None


class Buf:
    __slots__ = ("last_w", "readers", "name")

    def __init__(self, name=""):
        self.last_w = None
        self.readers = []
        self.name = name


class Prog:
    def __init__(self):
        self.recs = {e: [] for e in _ENGS}
        self.fence_pending = {e: [] for e in _ENGS}

    def emit(self, eng, fn, reads=(), writes=(), dma=False, extra_waits=(),
             touch=None):
        rec = _Rec(eng, fn, dma)
        waits = list(extra_waits)
        for b in reads:
            if b.last_w is not None:
                waits.append(b.last_w)
        for b in writes:
            if b.last_w is not None:
                waits.append(b.last_w)
            waits.extend(b.readers)
        if self.fence_pending[eng]:
            waits.extend(self.fence_pending[eng])
            self.fence_pending[eng] = []
        seen = set()
        for w in waits:
            if w is None or w is rec or id(w) in seen:
                continue
            seen.add(id(w))
            if (not w.dma) and w.eng == eng and eng in ("pe", "sp"):
                continue
            rec.waits.append(w)
            w.inc = True
        self.recs[eng].append(rec)
        for b in reads:
            b.readers.append(rec)
        for b in writes:
            b.last_w = rec
            b.readers = []
        if touch is not None:
            if eng == "act":
                fn2 = (lambda ap: lambda e: e.copy(ap, ap))(touch)
            else:
                fn2 = (lambda ap: lambda e: e.tensor_copy(ap, ap))(touch)
            return self.emit(eng, fn2, reads=list(writes), writes=list(writes))
        return rec

    def fence(self):
        tails = [self.recs[e][-1] for e in _ENGS if self.recs[e]]
        for e in _ENGS:
            self.fence_pending[e] = list(tails)

    def finalize(self, nc, stack):
        esem = {e: stack.enter_context(nc.semaphore(f"sem_{e}")) for e in _ENGS}
        NRING = 6
        dring = {q: [stack.enter_context(nc.semaphore(f"dma_{q}{i}"))
                     for i in range(NRING)] for q in ("sp", "pool")}
        dcnt = {"sp": 0, "pool": 0}
        ecnt = {e: 0 for e in _ENGS}
        for e in _ENGS:
            for rec in self.recs[e]:
                if rec.dma:
                    q = "sp" if e == "sp" else "pool"
                    rec.sem = dring[q][dcnt[q] % NRING]
                    rec.val = (dcnt[q] // NRING + 1) * 16
                    dcnt[q] += 1
                elif rec.inc:
                    ecnt[e] += 1
                    rec.sem = esem[e]
                    rec.val = ecnt[e]

    def replay(self, eng_name, eng):
        watermark = {}
        for rec in self.recs[eng_name]:
            for w in rec.waits:
                assert w.sem is not None, "wait on un-finalized record"
                key = id(w.sem)
                if watermark.get(key, 0) >= w.val:
                    continue
                eng.wait_ge(w.sem, w.val)
                watermark[key] = w.val
            if rec.dma and rec.val > 16:
                key = id(rec.sem)
                if watermark.get(key, 0) < rec.val - 16:
                    eng.wait_ge(rec.sem, rec.val - 16)
                    watermark[key] = rec.val - 16
            if rec.fn is None:
                continue
            instr = rec.fn(eng)
            if rec.sem is not None:
                instr.then_inc(rec.sem, 16 if rec.dma else 1)


def _dft_mats():
    f = np.arange(128)
    tau = np.arange(T)
    ang = 2 * np.pi * np.outer(tau, f) / NF
    cfT = np.cos(ang)
    sfT = -np.sin(ang)
    w = np.full(128, 2.0)
    w[0] = 1.0
    angi = 2 * np.pi * np.outer(f, tau) / NF
    icT = np.cos(angi) * w[:, None] / NF
    isT = -np.sin(angi) * w[:, None] / NF
    m64 = np.concatenate([cfT, sfT, icT, isT], axis=1)
    hi = m64.astype(np.float32)
    lo = (m64 - hi.astype(np.float64)).astype(np.float32)
    return hi, lo


def build_nc():
    nc = bass.Bass("TRN2", target_bir_lowering=False)
    P = Prog()
    EM = P.emit

    dx_tm = nc.dram_tensor("x_tm", [L, D], F32, kind="ExternalInput").ap()
    dx_fm = nc.dram_tensor("x_fm", [NCH, 128, D], F32, kind="ExternalInput").ap()
    dwin = nc.dram_tensor("w_in_r", [128, 8, D], F32, kind="ExternalInput").ap()
    dbin = nc.dram_tensor("b_in_row", [1, D], F32, kind="ExternalInput").ap()
    dgam = nc.dram_tensor("gamma_row", [1, D], F32, kind="ExternalInput").ap()
    dbet = nc.dram_tensor("beta_row", [1, D], F32, kind="ExternalInput").ap()
    dab = nc.dram_tensor("ab_vecs", [DS, 4], F32, kind="ExternalInput").ap()
    ddre = nc.dram_tensor("d_row", [2, D], F32, kind="ExternalInput").ap()
    dcre = nc.dram_tensor("cre_aug", [DS + 1, D], F32R, kind="ExternalInput").ap()
    dcim = nc.dram_tensor("cim_t", [DS, D], F32R, kind="ExternalInput").ap()
    ddft = nc.dram_tensor("dft_mats", [128, 512], F32R, kind="ExternalInput").ap()
    ddftl = nc.dram_tensor("dft_lo", [128, 512], F32R, kind="ExternalInput").ap()
    dones = nc.dram_tensor("ones_row", [1, 128], F32R, kind="ExternalInput").ap()
    donesf = nc.dram_tensor("ones_row_f", [1, 128], F32, kind="ExternalInput").ap()
    diden = nc.dram_tensor("ident_f", [128, 128], F32, kind="ExternalInput").ap()
    didenr = nc.dram_tensor("ident_r", [128, 128], F32R, kind="ExternalInput").ap()
    drw = nc.dram_tensor("rw_r", [128, 8, 8], F32, kind="ExternalInput").ap()
    drb = nc.dram_tensor("rbias", [1, 1], F32, kind="ExternalInput").ap()
    dgup = nc.dram_tensor("gup", [NEXP, 128, 8192], F32, kind="ExternalInput").ap()
    ddown = nc.dram_tensor("down_r", [NEXP, 128, 4096], F32R,
                           kind="ExternalInput").ap()
    dout = nc.dram_tensor("out", [L, D], F32, kind="ExternalOutput").ap()
    if DEV:
        dbg_xg = nc.dram_tensor("dbg_xg", [L, D], F32, kind="ExternalOutput").ap()
        dbg_o1 = nc.dram_tensor("dbg_o1", [L, D], F32, kind="ExternalOutput").ap()
        dbg_w8 = nc.dram_tensor("dbg_w8", [128, NCH, 8], F32,
                                kind="ExternalOutput").ap()
        dbg_y = nc.dram_tensor("dbg_y", [L, D], F32, kind="ExternalOutput").ap()
        dbg_x1 = nc.dram_tensor("dbg_x1", [L, D], F32, kind="ExternalOutput").ap()
        dbg_acc = nc.dram_tensor("dbg_acc", [L, D], F32, kind="ExternalOutput").ap()

    stack = contextlib.ExitStack()

    def S(name, shape, dt):
        return stack.enter_context(nc.sbuf_tensor(name, shape, dt))

    out1_tm = S("out1_tm", [128, NCH, 1024], F32)
    w8n = S("w8n", [128, NCH, 8], F32)
    statA = S("statA", [128, 2, 40], F32)
    iden = S("iden", [128, 128], F32)
    idenr = S("idenr", [128, 128], F32R)
    onesr = S("onesr", [1, 128], F32R)
    onesf = S("onesf", [1, 128], F32)
    epsc = S("epsc", [128, 1], F32)
    rw_sb = S("rw_sb", [128, 8, 8], F32)

    b_out1 = [Buf(f"out1_{c}") for c in range(NCH)]
    b_w8 = [Buf(f"w8_{c}") for c in range(NCH)]
    b_stat = [Buf("statA0"), Buf("statA1")]
    b_tab = Buf("tables")

    stA = contextlib.ExitStack()

    def SA(name, shape, dt):
        return stA.enter_context(nc.sbuf_tensor(name, shape, dt))

    win_sb = SA("win_sb", [128, 8, 1024], BF16)
    x_tm = SA("x_tm_sb", [128, 2, 1024], F32)
    x_fm = SA("x_fm_sb", [128, 2, 1024], BF16)
    xg = SA("xg_sb", [128, 2, 1024], F32R)
    scr = SA("scr", [128, 5, 1024], F32)
    cre_sb = SA("cre_sb", [DS + 1, 1024], F32R)
    cim_sb = SA("cim_sb", [DS, 1024], F32R)
    dft_sb = SA("dft_sb", [128, 512], F32R)
    dftl_sb = SA("dftl_sb", [128, 512], F32R)
    khat = SA("khat", [128, 2048], F32R)
    xh_im = SA("xh_im", [128, 1024], F32)
    pfreq = SA("pfreq", [128, 2048], F32R)
    mprod = SA("mprod", [DS, 2048], F32R)
    mprod2 = SA("mprod2", [DS, 2048], F32R)
    h_re = SA("h_re", [DS, 1024], F32R)
    h_im = SA("h_im", [DS, 1024], F32R)
    srow = SA("srow", [1, 1024], F32)
    dsrow = SA("dsrow", [1, 1024], F32R)
    drow = SA("drow", [1, 1024], F32)
    binrow = SA("binrow", [1, 1024], F32)
    bgam = SA("bgam", [128, 1024], F32)
    bbet = SA("bbet", [128, 1024], F32)
    pbre = SA("pbre", [DS + 1, 128], F32R)
    npbim = SA("npbim", [DS, 128], F32R)
    prevtr = SA("prevtr", [128, 64], F32R)
    prevti = SA("prevti", [128, 64], F32R)
    qre = SA("qre", [DS, 128], F32R)
    nqre = SA("nqre", [DS, 128], F32R)
    nqim = SA("nqim", [DS, 128], F32R)
    dgatre = SA("dgatre", [DS, 64], F32R)
    dgatim = SA("dgatim", [DS, 64], F32R)
    dgnatim = SA("dgnatim", [DS, 64], F32R)
    imask = SA("imask", [DS, 64], F32)

    powt = scr[0:DS, 0, :]
    rowsc = scr[0:1, 1, :]
    abv = scr[0:DS, 3, 0:4]
    krsb = pfreq[:, 0:1024]
    revr = mprod[:, 0:128]
    revi = mprod[:, 128:256]

    ps_a = stA.enter_context(nc.psum_tensor("ps_a", [128, 1024], F32))
    ps_b = stA.enter_context(nc.psum_tensor("ps_b", [128, 1024], F32))
    ps_c = stA.enter_context(nc.psum_tensor("ps_c", [128, 1024], F32))
    px1 = stA.enter_context(nc.psum_tensor("px1", [128, 1024], F32))

    b_win = Buf("win")
    b_xtm = [Buf(f"xtm{i}") for i in range(2)]
    b_xfm = [Buf(f"xfm{i}") for i in range(2)]
    b_xg = [Buf(f"xg{i}") for i in range(2)]
    b_scr = [Buf(f"scr{i}") for i in range(6)]
    b_khat = Buf("khat")
    b_xhim = Buf("xh_im")
    b_pfreq = [Buf("pfreq0"), Buf("pfreq1")]
    b_m1 = [Buf("mprod_0"), Buf("mprod_1")]
    b_m2 = [Buf("mprod2_0"), Buf("mprod2_1")]
    b_h = Buf("h")
    b_srow = Buf("srow")
    b_dsrow = Buf("dsrow")
    b_bc = Buf("bcast_rows")
    b_psa = [Buf("ps_a0"), Buf("ps_a1")]
    b_psb = [Buf("ps_b0"), Buf("ps_b1")]
    b_psc = [Buf("ps_c0"), Buf("ps_c1")]
    b_px1 = [Buf("px1_0"), Buf("px1_1")]
    b_rows = Buf("rowsc")

    EM("sp", lambda e: e.dma_start(abv[:], dab), writes=[b_tab], dma=True)
    EM("sp", lambda e: e.dma_start(cre_sb[:], dcre), writes=[b_tab], dma=True)
    EM("sp", lambda e: e.dma_start(cim_sb[:], dcim), writes=[b_tab], dma=True)
    EM("sp", lambda e: e.dma_start(dft_sb[:], ddft), writes=[b_tab], dma=True)
    EM("sp", lambda e: e.dma_start(dftl_sb[:], ddftl), writes=[b_tab], dma=True)
    EM("sp", lambda e: e.dma_start(idenr[:], didenr), writes=[b_tab], dma=True)
    EM("sp", lambda e: e.dma_start(onesr[:], dones), writes=[b_tab], dma=True)
    EM("sp", lambda e: e.dma_start(onesf[:], donesf), writes=[b_tab], dma=True)
    EM("sp", lambda e: e.dma_start(iden[:], diden), writes=[b_tab], dma=True)
    EM("sp", lambda e: e.dma_start(drow[:], ddre[0:1, :]), writes=[b_tab], dma=True)
    EM("sp", lambda e: e.dma_start(binrow[:], dbin), writes=[b_bc], dma=True)
    EM("sp", lambda e: e.dma_start(rw_sb[:], drw), writes=[b_tab], dma=True)
    EM("pool", lambda e: e.dma_start(win_sb[:], dwin), writes=[b_win], dma=True)
    EM("sp", lambda e: e.dma_start(x_tm[:, 0, :], dx_tm[0:128, :]),
       writes=[b_xtm[0]], dma=True)
    EM("sp", lambda e: e.dma_start(x_tm[:, 1, :], dx_tm[128:256, :]),
       writes=[b_xtm[1]], dma=True)
    EM("pool", lambda e: e.dma_start(x_fm[:, 0, :], dx_fm[0]),
       writes=[b_xfm[0]], dma=True)

    EM("sp", lambda e: e.dma_start(rowsc[:], dgam), writes=[b_rows], dma=True)
    for half in range(2):
        sl = slice(half * 512, half * 512 + 512)
        EM("pe", lambda e, sl=sl: e.matmul(ps_a[:, sl], onesf[:], rowsc[:, sl],
                                           start=True, stop=True),
           reads=[b_rows, b_tab], writes=[b_psa[half]])
    EM("dve", lambda e: e.tensor_copy(bgam[:], ps_a[:]), reads=b_psa, writes=[b_bc])
    EM("sp", lambda e: e.dma_start(rowsc[:], dbet), writes=[b_rows], dma=True)
    for half in range(2):
        sl = slice(half * 512, half * 512 + 512)
        EM("pe", lambda e, sl=sl: e.matmul(ps_b[:, sl], onesf[:], rowsc[:, sl],
                                           start=True, stop=True),
           reads=[b_rows, b_tab], writes=[b_psb[half]])
    EM("dve", lambda e: e.tensor_copy(bbet[:], ps_b[:]), reads=b_psb, writes=[b_bc])
    EM("sp", lambda e: e.dma_start(rowsc[:, 0:1], drb), writes=[b_rows], dma=True)
    EM("pe", lambda e: e.matmul(px1[:, 0:1], onesf[:], rowsc[:, 0:1],
                                start=True, stop=True),
       reads=[b_rows, b_tab], writes=[b_px1[0]])
    EM("dve", lambda e: e.tensor_copy(statA[:, 0, 15:16], px1[:, 0:1]),
       reads=[b_px1[0]], writes=[b_stat[0]])
    EM("dve", lambda e: e.tensor_copy(statA[:, 1, 15:16], px1[:, 0:1]),
       reads=[b_px1[0]], writes=[b_stat[1]])

    for half in range(2):
        sl = slice(half * 512, half * 512 + 512)
        for kt in range(8):
            EM("pe", lambda e, sl=sl, kt=kt: e.matmul(
                px1[:, sl], x_fm[:, 0, kt * 128:kt * 128 + 128],
                win_sb[:, kt, sl], start=(kt == 0), stop=False),
               reads=[b_xfm[0], b_win], writes=[b_px1[half]])
        EM("pe", lambda e, sl=sl: e.matmul(
            px1[:, sl], onesf[:], binrow[:, sl], start=False, stop=True),
           reads=[b_bc], writes=[b_px1[half]])
    EM("pool", lambda e: e.dma_start(x_fm[:, 1, :], dx_fm[1]),
       writes=[b_xfm[1]], dma=True)

    PRE = powt[:, 0:161]
    PIM = powt[:, 192:353]
    are, aim = abv[:, 0:1], abv[:, 1:2]
    bre, bim = abv[:, 2:3], abv[:, 3:4]

    def mkt(fn):
        EM("dve", fn, reads=[b_tab], writes=[b_tab])

    EM("dve", lambda e: e.memset(PRE[:, 0:1], 1.0), writes=[b_tab])
    EM("dve", lambda e: e.memset(PIM[:, 0:1], 0.0), writes=[b_tab])
    mkt(lambda e: e.tensor_copy(PRE[:, 1:2], are))
    mkt(lambda e: e.tensor_copy(PIM[:, 1:2], aim))
    have = 2
    while have < 161:
        n = min(have - 1, 161 - have)
        s_re = PRE[:, have - 1:have]
        s_im = PIM[:, have - 1:have]
        pr1 = PRE[:, 1:1 + n]
        pi1 = PIM[:, 1:1 + n]
        dst_r = PRE[:, have:have + n]
        dst_i = PIM[:, have:have + n]
        t0 = powt[:, 354:354 + n]
        mkt(lambda e, a=dst_r, b=pr1, s=s_re: e.tensor_scalar(a, b, s, None, op.mult))
        mkt(lambda e, a=t0, b=pi1, s=s_im: e.tensor_scalar(a, b, s, None, op.mult))
        mkt(lambda e, a=dst_r, b=t0: e.tensor_tensor(a, a, b, op.subtract))
        mkt(lambda e, a=dst_i, b=pr1, s=s_im: e.tensor_scalar(a, b, s, None, op.mult))
        mkt(lambda e, a=t0, b=pi1, s=s_re: e.tensor_scalar(a, b, s, None, op.mult))
        mkt(lambda e, a=dst_i, b=t0: e.tensor_tensor(a, a, b, op.add))
        have += n

    t1 = powt[:, 354:482]
    mkt(lambda e: e.tensor_scalar(pbre[0:DS, :], PRE[:, 0:128], bre, None, op.mult))
    mkt(lambda e: e.tensor_scalar(t1, PIM[:, 0:128], bim, None, op.mult))
    mkt(lambda e: e.scalar_tensor_tensor(pbre[0:DS, :], pbre[0:DS, :].bitcast(F32),
                                         1.0, t1, op.mult, op.subtract))
    EM("dve", lambda e: e.memset(pbre[DS:DS + 1, :].bitcast(F32), 1.0), writes=[b_tab])
    mkt(lambda e: e.tensor_scalar(npbim[:], PRE[:, 0:128], bim, None, op.mult))
    mkt(lambda e: e.tensor_scalar(t1, PIM[:, 0:128], bre, None, op.mult))
    mkt(lambda e: e.scalar_tensor_tensor(npbim[:], npbim[:].bitcast(F32), -1.0, t1,
                                         op.mult, op.subtract))
    mkt(lambda e: e.tensor_copy(revr[:], pbre[0:DS, ::-1].bitcast(F32)))
    mkt(lambda e: e.tensor_scalar(revi[:], npbim[:, ::-1].bitcast(F32), -1.0, None,
                                  op.mult))
    EM("pe", lambda e: e.transpose(ps_a[:, 0:64].bitcast(F32R), revr[:], idenr[0:64, 0:64]),
       reads=[b_tab], writes=[b_psa[0]])
    EM("pe", lambda e: e.transpose(ps_a[:, 512:576].bitcast(F32R), revi[:], idenr[0:64, 0:64]),
       reads=[b_tab], writes=[b_psa[1]])
    EM("dve", lambda e: e.tensor_copy(prevtr[:], ps_a[:, 0:64].bitcast(F32R)),
       reads=[b_psa[0]], writes=[b_tab])
    EM("dve", lambda e: e.tensor_copy(prevti[:], ps_a[:, 512:576].bitcast(F32R)),
       reads=[b_psa[1]], writes=[b_tab])
    mkt(lambda e: e.tensor_copy(qre[:], PRE[:, 1:129]))
    mkt(lambda e: e.tensor_scalar(nqre[:], PRE[:, 1:129], -1.0, None, op.mult))
    mkt(lambda e: e.tensor_scalar(nqim[:], PIM[:, 1:129], -1.0, None, op.mult))
    EM("pool", lambda e: e.iota(imask[:], [[1, 64]], base=0, channel_multiplier=-1,
                                allow_small_or_imprecise_dtypes=True),
       writes=[b_tab])
    mkt(lambda e: e.tensor_scalar(imask[:], imask[:], 0.0, None, op.is_equal))
    atre = PRE[:, 128:129]
    atim = PIM[:, 128:129]
    mkt(lambda e: e.tensor_scalar(dgatre[:], imask[:], atre, None, op.mult))
    mkt(lambda e: e.tensor_scalar(dgatim[:], imask[:], atim, None, op.mult))
    mkt(lambda e: e.tensor_scalar(dgnatim[:], imask[:], atim, -1.0, op.mult, op.mult))

    for half in range(2):
        sl = slice(half * 512, half * 512 + 512)
        EM("pe", lambda e, sl=sl: e.matmul(ps_b[:, sl], pbre[:], cre_sb[:, sl],
                                           start=True, stop=False),
           reads=[b_tab], writes=[b_psb[half]])
        EM("pe", lambda e, sl=sl: e.matmul(ps_b[:, sl], npbim[:], cim_sb[:, sl],
                                           start=False, stop=True),
           reads=[b_tab], writes=[b_psb[half]])
    EM("act", lambda e: e.copy(krsb[:], ps_b[:]), reads=b_psb, writes=[b_tab])
    for half in range(2):
        sl = slice(half * 512, half * 512 + 512)
        EM("pe", lambda e, sl=sl: e.matmul(ps_c[:, sl], dft_sb[:, 0:128],
                                           krsb[:, sl], start=True, stop=False),
           reads=[b_tab], writes=[b_psc[half]])
        EM("pe", lambda e, sl=sl: e.matmul(ps_c[:, sl], dftl_sb[:, 0:128],
                                           krsb[:, sl], start=False, stop=True),
           reads=[b_tab], writes=[b_psc[half]])
    EM("act", lambda e: e.copy(khat[:, 0:1024], ps_c[:]), reads=b_psc,
       writes=[b_khat])
    for half in range(2):
        sl = slice(half * 512, half * 512 + 512)
        EM("pe", lambda e, sl=sl: e.matmul(ps_c[:, sl], dft_sb[:, 128:256],
                                           krsb[:, sl], start=True, stop=False),
           reads=[b_tab], writes=[b_psc[half]])
        EM("pe", lambda e, sl=sl: e.matmul(ps_c[:, sl], dftl_sb[:, 128:256],
                                           krsb[:, sl], start=False, stop=True),
           reads=[b_tab], writes=[b_psc[half]])
    EM("act", lambda e: e.copy(khat[:, 1024:2048], ps_c[:]), reads=b_psc,
       writes=[b_khat])

    EM("dve", lambda e: e.memset(epsc[:], LN_EPS), writes=[b_tab])
    EM("dve", lambda e: e.memset(w8n[:], 0.0), writes=list(b_w8))
    EM("dve", lambda e: e.memset(h_re[:].bitcast(F32), 0.0), writes=[b_h])
    EM("dve", lambda e: e.memset(h_im[:].bitcast(F32), 0.0), writes=[b_h])
    EM("dve", lambda e: e.memset(srow[:], 0.0), writes=[b_srow])
    P.fence()

    (C_SUM0, C_SUM1, C_SUM, C_MU, C_SSQ, C_VAR, C_RST, C_NMX, C_SEXP, C_RSUM,
     C_CGT, C_THR, C_WSUM, C_WINV, C_TMP, C_RB) = range(16)

    KRE = khat[:, 0:1024]
    KIM = khat[:, 1024:2048]

    def _env(c):
        p = c % 2

        class E:
            pass

        v = E()
        v.p = p
        v.sc = lambda i, p=p: statA[:, p, i:i + 1]
        v.s0 = scr[:, 2 * p + 0, :]
        v.s1 = scr[:, 2 * p + 1, :]
        v.s2 = scr[:, 4, :]
        v.b_s0, v.b_s1 = b_scr[2 * p:2 * p + 2]
        v.b_s2 = b_scr[4]
        v.b_st = b_stat[p]
        v.xg_c = xg[:, p, :]
        v.xtm_c = x_tm[:, p, :]
        v.xfm_c = x_fm[:, p, :]
        v.ds_c = dsrow[:]
        v.b_ds = b_dsrow
        return v

    def emit_x1(c):
        v = _env(c)
        if c + 1 < NCH:
            pn = (c + 1) % 2
            EM("pool", lambda e, c=c, pn=pn: e.dma_start(
                x_fm[:, pn, :], dx_fm[c + 1]), writes=[b_xfm[pn]], dma=True)
        for half in range(2):
            sl = slice(half * 512, half * 512 + 512)
            for kt in range(8):
                EM("pe", lambda e, sl=sl, kt=kt, xfm_c=v.xfm_c: e.matmul(
                    px1[:, sl], xfm_c[:, kt * 128:kt * 128 + 128],
                    win_sb[:, kt, sl], start=(kt == 0), stop=False),
                   reads=[b_xfm[v.p], b_win], writes=[b_px1[half]])
            EM("pe", lambda e, sl=sl: e.matmul(
                px1[:, sl], onesf[:], binrow[:, sl], start=False, stop=True),
               reads=[b_bc], writes=[b_px1[half]])

    def emit_ln1(c):
        v = _env(c)
        s0, s1, sc = v.s0, v.s1, v.sc
        b_s0, b_s1, b_st = v.b_s0, v.b_s1, v.b_st
        EM("dve", lambda e, s0=s0, sc=sc: e.tensor_scalar(
            s0, px1[:], 0.0, None, op.add, op.add, accum_out=sc(C_SUM)),
           reads=b_px1, writes=[b_s0, b_st], touch=sc(C_SUM))
        if DEV:
            EM("sp", lambda e, c=c, s0=s0: e.dma_start(
                dbg_x1[c * 128:(c + 1) * 128, :], s0), reads=[b_s0], dma=True)
        EM("act", lambda e, s0=s0, s1=s1, sc=sc: e.activation(
            s1, s0, AF.Square, accum_out=sc(C_SSQ)),
           reads=[b_s0], writes=[b_s1, b_st], touch=sc(C_SSQ))
        EM("dve", lambda e, sc=sc: e.tensor_scalar(sc(C_MU), sc(C_SUM), 1.0 / D,
                                                   None, op.mult),
           reads=[b_st], writes=[b_st])
        EM("dve", lambda e, sc=sc: e.tensor_scalar(sc(C_TMP), sc(C_MU), sc(C_MU),
                                                   None, op.mult),
           reads=[b_st], writes=[b_st])
        EM("dve", lambda e, sc=sc: e.tensor_scalar(sc(C_VAR), sc(C_SSQ), 1.0 / D,
                                                   sc(C_TMP), op.mult, op.subtract),
           reads=[b_st], writes=[b_st])
        EM("act", lambda e, sc=sc: e.activation(sc(C_TMP), sc(C_VAR), AF.Ln,
                                                bias=epsc[:, 0:1]),
           reads=[b_st], writes=[b_st])
        EM("act", lambda e, sc=sc: e.activation(sc(C_RST), sc(C_TMP), AF.Exp,
                                                scale=-0.5),
           reads=[b_st], writes=[b_st])
        EM("dve", lambda e, s0=s0, s1=s1, sc=sc: e.tensor_scalar(
            s1, s0, sc(C_MU), sc(C_RST), op.subtract, op.mult),
           reads=[b_s0, b_st], writes=[b_s1])
        EM("pool", lambda e, s0=s0, s1=s1: e.tensor_tensor(s0, s1, bgam[:],
                                                           op.mult),
           reads=[b_s1, b_bc], writes=[b_s0])
        EM("pool", lambda e, s0=s0, xg_c=v.xg_c: e.tensor_tensor(
            xg_c, s0, bbet[:], op.add),
           reads=[b_s0, b_bc], writes=[b_xg[v.p]])
        if DEV:
            EM("sp", lambda e, c=c, xg_c=v.xg_c: e.dma_start(
                dbg_xg[c * 128:(c + 1) * 128, :], xg_c.bitcast(F32)),
               reads=[b_xg[v.p]], dma=True)

    def emit_dft(c):
        v = _env(c)
        for half in range(2):
            sl = slice(half * 512, half * 512 + 512)
            EM("pe", lambda e, sl=sl, xg_c=v.xg_c: e.matmul(
                ps_a[:, sl], dft_sb[:, 0:128], xg_c[:, sl], start=True,
                stop=False),
               reads=[b_xg[v.p], b_tab], writes=[b_psa[half]])
            EM("pe", lambda e, sl=sl, xg_c=v.xg_c: e.matmul(
                ps_a[:, sl], dftl_sb[:, 0:128], xg_c[:, sl], start=False,
                stop=True),
               reads=[b_xg[v.p], b_tab], writes=[b_psa[half]])
            EM("pe", lambda e, sl=sl, xg_c=v.xg_c: e.matmul(
                ps_b[:, sl], dft_sb[:, 128:256], xg_c[:, sl], start=True,
                stop=False),
               reads=[b_xg[v.p], b_tab], writes=[b_psb[half]])
            EM("pe", lambda e, sl=sl, xg_c=v.xg_c: e.matmul(
                ps_b[:, sl], dftl_sb[:, 128:256], xg_c[:, sl], start=False,
                stop=True),
               reads=[b_xg[v.p], b_tab], writes=[b_psb[half]])

    def emit_prod(c):
        v = _env(c)
        s2, sc = v.s2, v.sc
        b_s2, b_st = v.b_s2, v.b_st
        EM("dve", lambda e, ds_c=v.ds_c: e.tensor_tensor(ds_c, drow[:], srow[:],
                                                         op.mult),
           reads=[b_srow, b_tab], writes=[v.b_ds])
        EM("dve", lambda e: e.tensor_tensor(srow[:], srow[:], ps_a[0:1, :],
                                            op.add),
           reads=b_psa, writes=[b_srow])
        EM("act", lambda e: e.copy(xh_im[:], ps_b[:]), reads=b_psb,
           writes=[b_xhim])
        for half in range(2):
            sl = slice(half * 512, half * 512 + 512)
            s2h = s2[:, sl]
            PRE_F = pfreq[:, half * 1024:half * 1024 + 512]
            PIM_F = pfreq[:, half * 1024 + 512:half * 1024 + 1024]
            M_RR = mprod[:, half * 1024:half * 1024 + 512]
            M_II = mprod[:, half * 1024 + 512:half * 1024 + 1024]
            M_IR = mprod2[:, half * 1024:half * 1024 + 512]
            M_RI = mprod2[:, half * 1024 + 512:half * 1024 + 1024]
            EM("dve", lambda e, sl=sl, PRE_F=PRE_F: e.tensor_tensor(
                PRE_F, ps_a[:, sl], KRE[:, sl].bitcast(F32), op.mult),
               reads=[b_psa[half], b_khat], writes=[b_pfreq[half]])
            EM("dve", lambda e, sl=sl, s2h=s2h: e.tensor_tensor(
                s2h, xh_im[:, sl], KIM[:, sl].bitcast(F32), op.mult),
               reads=[b_xhim, b_khat], writes=[b_s2])
            EM("dve", lambda e, s2h=s2h, PRE_F=PRE_F: e.tensor_tensor(
                PRE_F, PRE_F.bitcast(F32), s2h, op.subtract),
               reads=[b_s2], writes=[b_pfreq[half]])
            EM("dve", lambda e, sl=sl, PIM_F=PIM_F: e.tensor_tensor(
                PIM_F, ps_a[:, sl], KIM[:, sl].bitcast(F32), op.mult),
               reads=[b_psa[half], b_khat], writes=[b_pfreq[half]])
            EM("dve", lambda e, sl=sl, s2h=s2h: e.tensor_tensor(
                s2h, xh_im[:, sl], KRE[:, sl].bitcast(F32), op.mult),
               reads=[b_xhim, b_khat], writes=[b_s2])
            EM("dve", lambda e, s2h=s2h, PIM_F=PIM_F: e.tensor_tensor(
                PIM_F, PIM_F.bitcast(F32), s2h, op.add),
               reads=[b_s2], writes=[b_pfreq[half]])

    def emit_eprod(c):
        v = _env(c)
        for half in range(2):
            sl = slice(half * 512, half * 512 + 512)
            M_RR = mprod[:, half * 1024:half * 1024 + 512]
            M_II = mprod[:, half * 1024 + 512:half * 1024 + 1024]
            M_IR = mprod2[:, half * 1024:half * 1024 + 512]
            M_RI = mprod2[:, half * 1024 + 512:half * 1024 + 1024]
            EM("pool", lambda e, sl=sl, M_RR=M_RR: e.tensor_tensor(
                M_RR, cre_sb[0:DS, sl].bitcast(F32), h_re[:, sl].bitcast(F32),
                op.mult), reads=[b_h, b_tab], writes=[b_m1[half]])
            EM("pool", lambda e, sl=sl, M_II=M_II: e.tensor_tensor(
                M_II, cim_sb[:, sl].bitcast(F32), h_im[:, sl].bitcast(F32),
                op.mult), reads=[b_h, b_tab], writes=[b_m1[half]])
            EM("pool", lambda e, sl=sl, M_IR=M_IR: e.tensor_tensor(
                M_IR, cim_sb[:, sl].bitcast(F32), h_re[:, sl].bitcast(F32),
                op.mult), reads=[b_h, b_tab], writes=[b_m2[half]])
            EM("pool", lambda e, sl=sl, M_RI=M_RI: e.tensor_tensor(
                M_RI, cre_sb[0:DS, sl].bitcast(F32), h_im[:, sl].bitcast(F32),
                op.mult), reads=[b_h, b_tab], writes=[b_m2[half]])

    def emit_carry(c):
        v = _env(c)
        for half in range(2):
            sl = slice(half * 512, half * 512 + 512)
            PRE_F = pfreq[:, half * 1024:half * 1024 + 512]
            PIM_F = pfreq[:, half * 1024 + 512:half * 1024 + 1024]
            M_RR = mprod[:, half * 1024:half * 1024 + 512]
            M_II = mprod[:, half * 1024 + 512:half * 1024 + 1024]
            M_IR = mprod2[:, half * 1024:half * 1024 + 512]
            M_RI = mprod2[:, half * 1024 + 512:half * 1024 + 1024]
            EM("pe", lambda e, sl=sl, ds_c=v.ds_c: e.matmul(
                ps_c[:, sl], onesr[:], ds_c[:, sl], start=True, stop=False),
               reads=[v.b_ds, b_tab], writes=[b_psc[half]])
            EM("pe", lambda e, sl=sl, PRE_F=PRE_F: e.matmul(
                ps_c[:, sl], dft_sb[:, 256:384], PRE_F, start=False, stop=False),
               reads=[b_pfreq[half], b_tab], writes=[b_psc[half]])
            EM("pe", lambda e, sl=sl, PRE_F=PRE_F: e.matmul(
                ps_c[:, sl], dftl_sb[:, 256:384], PRE_F, start=False,
                stop=False),
               reads=[b_pfreq[half], b_tab], writes=[b_psc[half]])
            EM("pe", lambda e, sl=sl, PIM_F=PIM_F: e.matmul(
                ps_c[:, sl], dft_sb[:, 384:512], PIM_F, start=False, stop=False),
               reads=[b_pfreq[half], b_tab], writes=[b_psc[half]])
            EM("pe", lambda e, sl=sl, PIM_F=PIM_F: e.matmul(
                ps_c[:, sl], dftl_sb[:, 384:512], PIM_F, start=False,
                stop=False),
               reads=[b_pfreq[half], b_tab], writes=[b_psc[half]])
            EM("pe", lambda e, sl=sl, M_RR=M_RR: e.matmul(
                ps_c[:, sl], qre[:], M_RR, start=False, stop=False),
               reads=[b_m1[half], b_tab], writes=[b_psc[half]])
            EM("pe", lambda e, sl=sl, M_II=M_II: e.matmul(
                ps_c[:, sl], nqre[:], M_II, start=False, stop=False),
               reads=[b_m1[half], b_tab], writes=[b_psc[half]])
            EM("pe", lambda e, sl=sl, M_IR=M_IR: e.matmul(
                ps_c[:, sl], nqim[:], M_IR, start=False, stop=False),
               reads=[b_m2[half], b_tab], writes=[b_psc[half]])
            EM("pe", lambda e, sl=sl, M_RI=M_RI: e.matmul(
                ps_c[:, sl], nqim[:], M_RI, start=False, stop=True),
               reads=[b_m2[half], b_tab], writes=[b_psc[half]])

    def emit_postconv(c):
        v = _env(c)
        s0, s1, s2, sc = v.s0, v.s1, v.s2, v.sc
        b_s0, b_s1, b_s2, b_st = v.b_s0, v.b_s1, v.b_s2, v.b_st
        if DEV:
            EM("dve", lambda e, s2=s2: e.tensor_copy(s2, ps_c[:]), reads=b_psc,
               writes=[b_s2])
            EM("sp", lambda e, c=c, s2=s2: e.dma_start(
                dbg_y[c * 128:(c + 1) * 128, :], s2), reads=[b_s2], dma=True)
        EM("dve", lambda e, s0=s0, xtm_c=v.xtm_c: e.tensor_tensor(
            s0, ps_c[:], xtm_c, op.add),
           reads=b_psc + [b_xtm[v.p]], writes=[b_s0])
        EM("dve", lambda e, s0=s0, s1=s1, xg_c=v.xg_c, sc=sc:
           e.scalar_tensor_tensor(
               s1, s0, 0.0, xg_c.bitcast(F32), op.add, op.add,
               accum_out=sc(C_SUM)),
           reads=[b_s0, b_xg[v.p]], writes=[b_s1, b_st], touch=sc(C_SUM))
        EM("act", lambda e, s0=s0, s1=s1, sc=sc: e.activation(
            s0, s1, AF.Square, accum_out=sc(C_SSQ)),
           reads=[b_s1], writes=[b_s0, b_st], touch=sc(C_SSQ))
        EM("dve", lambda e, sc=sc: e.tensor_scalar(sc(C_MU), sc(C_SUM), 1.0 / D,
                                                   None, op.mult),
           reads=[b_st], writes=[b_st])
        EM("dve", lambda e, sc=sc: e.tensor_scalar(sc(C_TMP), sc(C_MU), sc(C_MU),
                                                   None, op.mult),
           reads=[b_st], writes=[b_st])
        EM("dve", lambda e, sc=sc: e.tensor_scalar(sc(C_VAR), sc(C_SSQ), 1.0 / D,
                                                   sc(C_TMP), op.mult,
                                                   op.subtract),
           reads=[b_st], writes=[b_st])
        EM("act", lambda e, sc=sc: e.activation(sc(C_TMP), sc(C_VAR), AF.Ln,
                                                bias=epsc[:, 0:1]),
           reads=[b_st], writes=[b_st])
        EM("act", lambda e, sc=sc: e.activation(sc(C_RST), sc(C_TMP), AF.Exp,
                                                scale=-0.5),
           reads=[b_st], writes=[b_st])
        EM("dve", lambda e, c=c, s1=s1, sc=sc: e.tensor_scalar(
            out1_tm[:, c, :], s1, sc(C_MU), sc(C_RST), op.subtract, op.mult),
           reads=[b_s1, b_st], writes=[b_out1[c]])
        if c + 2 < NCH:
            EM("sp", lambda e, c=c, p=v.p: e.dma_start(
                x_tm[:, p, :], dx_tm[(c + 2) * 128:(c + 3) * 128, :]),
               writes=[b_xtm[v.p]], dma=True)

    def emit_h(c):
        v = _env(c)
        for half in range(2):
            sl = slice(half * 512, half * 512 + 512)
            EM("pe", lambda e, sl=sl, xg_c=v.xg_c: e.matmul(
                ps_b[0:DS, sl], prevtr[:], xg_c[:, sl], start=True, stop=False),
               reads=[b_xg[v.p], b_tab], writes=[b_psb[half]])
            EM("pe", lambda e, sl=sl: e.matmul(
                ps_b[0:DS, sl], dgatre[:], h_re[:, sl], start=False, stop=False),
               reads=[b_h, b_tab], writes=[b_psb[half]])
            EM("pe", lambda e, sl=sl: e.matmul(
                ps_b[0:DS, sl], dgnatim[:], h_im[:, sl], start=False, stop=True),
               reads=[b_h, b_tab], writes=[b_psb[half]])
            EM("pe", lambda e, sl=sl, xg_c=v.xg_c: e.matmul(
                ps_c[0:DS, sl], prevti[:], xg_c[:, sl], start=True, stop=False),
               reads=[b_xg[v.p], b_tab], writes=[b_psc[half]])
            EM("pe", lambda e, sl=sl: e.matmul(
                ps_c[0:DS, sl], dgatre[:], h_im[:, sl], start=False, stop=False),
               reads=[b_h, b_tab], writes=[b_psc[half]])
            EM("pe", lambda e, sl=sl: e.matmul(
                ps_c[0:DS, sl], dgatim[:], h_re[:, sl], start=False, stop=True),
               reads=[b_h, b_tab], writes=[b_psc[half]])
        EM("act", lambda e: e.copy(h_re[:], ps_b[0:DS, :]), reads=b_psb,
           writes=[b_h])
        EM("act", lambda e: e.copy(h_im[:], ps_c[0:DS, :]), reads=b_psc,
           writes=[b_h])

    emit_ln1(0)
    for r in range(NCH):
        emit_dft(r)
        emit_eprod(r)
        emit_prod(r)
        if r + 1 < NCH:
            emit_x1(r + 1)
        emit_carry(r)
        if r + 1 < NCH:
            emit_ln1(r + 1)
        emit_postconv(r)
        emit_h(r)

    if DEV:
        for c in range(NCH):
            EM("sp", lambda e, c=c: e.dma_start(dbg_o1[c * 128:(c + 1) * 128, :],
                                                out1_tm[:, c, :]),
               reads=[b_out1[c]], dma=True)

    P.fence()
    stA.close()

    stB = contextlib.ExitStack()

    def SB(name, shape, dt):
        return stB.enter_context(nc.sbuf_tensor(name, shape, dt))

    out1_fm = SB("out1_fm", [128, 8, 2048], BF16)
    b_o1fm = [Buf(f"o1fm_{c}") for c in range(NCH)]
    gup_sb = SB("gup_sb", [128, 2, 8192], BF16)
    down_sb = SB("down_sb", [128, 2, 4096], F32R)
    a_sl = SB("a_sl", [128, 2, 4, 512], F32R)
    sg_sb = SB("sg_sb", [128, 4, 512], F32)
    bounce = SB("bounce", [128, 2, 1024], F32)
    statB = SB("statB", [128, 64], F32)

    pg = stB.enter_context(nc.psum_tensor("pg", [128, 2048], F32))
    pu = stB.enter_context(nc.psum_tensor("pu", [128, 2048], F32))

    b_gup = [Buf(f"gup{i}") for i in range(2)]
    b_down = [Buf(f"down{i}") for i in range(2)]
    b_asl = [Buf(f"asl{i}") for i in range(2)]
    b_sg = [Buf(f"sg{i}") for i in range(4)]
    b_bnc = [Buf(f"bnc{i}") for i in range(2)]
    b_statB = Buf("statB")
    b_pg = [Buf(f"pg{i}") for i in range(4)]
    b_pu = [Buf(f"pu{i}") for i in range(4)]
    b_outs = [Buf(f"outs{c}") for c in range(NCH)]

    s2b = SB("s2b", [128, 2, 1024], F32)
    b_s2b = [Buf("s2b0"), Buf("s2b1")]
    (C_SUM0, C_SUM1, C_SUM, C_MU, C_SSQ, C_VAR, C_RST, C_NMX, C_SEXP, C_RSUM,
     C_CGT, C_THR, C_WSUM, C_WINV, C_TMP, C_RB) = range(16)

    def emit_tp(c):
        p = c % 2
        tp = pg if (p == 0) else pu
        b_tp = b_pg if (p == 0) else b_pu
        for kt in range(8):
            EM("pe", lambda e, kt=kt, c=c, tp=tp: e.transpose(
                tp[:, kt * 128:kt * 128 + 128],
                out1_tm[:, c, kt * 128:kt * 128 + 128], iden[:]),
               reads=[b_out1[c], b_tab], writes=[b_tp[kt // 4]])
        r_act = EM("act", lambda e, c=c, tp=tp: e.copy(
            out1_fm[:, :, c * 128:c * 128 + 128], tp[:, 0:1024]),
           reads=[b_tp[0], b_tp[1]], writes=[b_o1fm[c]])
        s2 = s2b[:, p, :]
        EM("act", lambda e, s2=s2, tp=tp: e.copy(s2, tp[:, 0:1024]),
           reads=[b_tp[0], b_tp[1]], writes=[b_s2b[p]], extra_waits=[r_act])

    def emit_router(c):
        p = c % 2
        otp = pu if (p == 0) else pg
        b_otp = b_pu if (p == 0) else b_pg
        s2 = s2b[:, p, :]
        bk = 3 if ((c >> 1) & 1) == 0 else 2
        pr_c = otp[:, bk * 512 + 504:bk * 512 + 512]
        b_pr = b_otp[bk]

        def sc(i, p=p):
            return statA[:, p, i:i + 1]

        for kt in range(8):
            EM("pe", lambda e, kt=kt, s2=s2, pr_c=pr_c: e.matmul(
                pr_c, s2[:, kt * 128:kt * 128 + 128], rw_sb[:, kt, :],
                start=(kt == 0), stop=(kt == 7)),
               reads=[b_s2b[p], b_tab], writes=[b_pr])
        LG = statA[:, p, 16:24]
        SCR8 = statA[:, p, 24:32]
        SORT8 = statA[:, p, 0:8]
        TH = statA[:, p, 3:4]
        EQ = statA[:, p, 32:40]
        MQ = w8n[:, c, :]
        b_st = b_stat[p]
        EM("dve", lambda e, sc=sc, pr_c=pr_c: e.tensor_reduce(
            sc(C_NMX), pr_c, AX.X, op.max, negate=True),
           reads=[b_pr], writes=[b_st])
        EM("act", lambda e, LG=LG, sc=sc, pr_c=pr_c: e.activation(
            LG, pr_c, AF.Exp, bias=sc(C_NMX), accum_out=sc(C_SEXP)),
           reads=[b_pr, b_st], writes=[b_st], touch=sc(C_SEXP))
        EM("dve", lambda e, sc=sc: e.reciprocal(sc(C_RSUM), sc(C_SEXP)),
           reads=[b_st], writes=[b_st])
        EM("dve", lambda e, LG=LG, sc=sc: e.tensor_scalar(LG, LG, sc(C_RSUM), None,
                                                          op.mult),
           reads=[b_st], writes=[b_st])
        EM("dve", lambda e, LG=LG, SCR8=SCR8, sc=sc: e.tensor_scalar(
            SCR8, LG, sc(C_RB), None, op.add),
           reads=[b_st], writes=[b_st])
        EM("dve", lambda e, SORT8=SORT8, SCR8=SCR8: e.max(SORT8, SCR8),
           reads=[b_st], writes=[b_st])
        EM("dve", lambda e, MQ=MQ, SCR8=SCR8, TH=TH, sc=sc: e.tensor_scalar(
            MQ, SCR8, TH, None, op.is_gt, op.add, accum_out=sc(C_CGT)),
           reads=[b_st], writes=[b_w8[c], b_st], touch=sc(C_CGT))
        EM("dve", lambda e, sc=sc: e.tensor_scalar(sc(C_THR), sc(C_CGT), -1.0, 4.0,
                                                   op.mult, op.add),
           reads=[b_st], writes=[b_st])
        EM("dve", lambda e, EQ=EQ, SCR8=SCR8, TH=TH: e.tensor_scalar(
            EQ, SCR8, TH, None, op.is_equal),
           reads=[b_st], writes=[b_st])
        EM("dve", lambda e, SCR8=SCR8: e.tensor_scalar(SCR8, SCR8, 0.0, 1.0,
                                                       op.mult, op.add),
           reads=[b_st], writes=[b_st])
        EM("dve", lambda e, SCR8=SCR8, EQ=EQ: e.tensor_tensor_scan(
            SCR8, SCR8, EQ, 0.0, op.mult, op.add),
           reads=[b_st], writes=[b_st])
        EM("dve", lambda e, SCR8=SCR8, EQ=EQ: e.tensor_tensor(SCR8, SCR8, EQ,
                                                              op.subtract),
           reads=[b_st], writes=[b_st])
        EM("dve", lambda e, SCR8=SCR8, sc=sc: e.tensor_scalar(
            SCR8, SCR8, sc(C_THR), None, op.is_lt),
           reads=[b_st], writes=[b_st])
        EM("dve", lambda e, EQ=EQ, SCR8=SCR8: e.tensor_tensor(EQ, EQ, SCR8,
                                                              op.mult),
           reads=[b_st], writes=[b_st])
        EM("dve", lambda e, MQ=MQ, EQ=EQ: e.tensor_tensor(MQ, MQ, EQ, op.add),
           reads=[b_st], writes=[b_w8[c]])
        EM("dve", lambda e, MQ=MQ, LG=LG: e.tensor_tensor(MQ, MQ, LG, op.mult),
           reads=[b_st], writes=[b_w8[c]])
        EM("dve", lambda e, MQ=MQ, EQ=EQ, sc=sc: e.tensor_scalar(
            EQ, MQ, 1.0, None, op.mult, op.add, accum_out=sc(C_WSUM)),
           reads=[b_w8[c]], writes=[b_st], touch=sc(C_WSUM))
        EM("dve", lambda e, sc=sc: e.tensor_scalar(sc(C_WSUM), sc(C_WSUM), 1e-20,
                                                   None, op.add),
           reads=[b_st], writes=[b_st])
        EM("dve", lambda e, sc=sc: e.reciprocal(sc(C_WINV), sc(C_WSUM)),
           reads=[b_st], writes=[b_st])
        EM("dve", lambda e, MQ=MQ, sc=sc: e.tensor_scalar(MQ, MQ, sc(C_WINV), None,
                                                          op.mult),
           reads=[b_st], writes=[b_w8[c]])

    emit_tp(0)
    for c in range(NCH):
        if c + 1 < NCH:
            emit_tp(c + 1)
        emit_router(c)
    if DEV:
        EM("sp", lambda e: e.dma_start(dbg_w8, w8n[:]), reads=list(b_w8), dma=True)

    for e0 in range(2):
        EM("pool", lambda e, e0=e0: e.dma_start(gup_sb[:, e0, :], dgup[e0]),
           writes=[b_gup[e0]], dma=True)
        EM("sp", lambda e, e0=e0: e.dma_start(down_sb[:, e0, :], ddown[e0]),
           writes=[b_down[e0]], dma=True)

    NCHB = 4
    for ex in range(NEXP):
        wslot = ex % 2
        last = (ex == NEXP - 1)
        for cb in range(NCHB):
            cpar = cb % 2
            tok = slice(cb * 512, cb * 512 + 512)
            PGX, b_pgx = (pg, b_pg) if cpar == 0 else (pu, b_pu)
            PUX, b_pux = (pu, b_pu) if cpar == 0 else (pg, b_pg)
            fmc = [b_o1fm[t] for t in range(cb * 4, cb * 4 + 4)]
            for it in range(4):
                psl = PGX[:, it * 512:(it + 1) * 512]
                for kt in range(8):
                    EM("pe", lambda e, psl=psl, kt=kt, it=it, tok=tok, w=wslot:
                       e.matmul(psl,
                                gup_sb[:, w, kt * 512 + it * 128:
                                       kt * 512 + it * 128 + 128],
                                out1_fm[:, kt, tok],
                                start=(kt == 0), stop=(kt == 7)),
                       reads=[b_gup[wslot]] + fmc, writes=[b_pgx[it]])
                EM("act", lambda e, psl=psl, it=it: e.activation(
                    sg_sb[:, it, :], psl, AF.Silu),
                   reads=[b_pgx[it]], writes=[b_sg[it]])
                pul = PUX[:, it * 512:(it + 1) * 512]
                for kt in range(8):
                    EM("pe", lambda e, pul=pul, kt=kt, it=it, tok=tok, w=wslot:
                       e.matmul(pul,
                                gup_sb[:, w, 4096 + kt * 512 + it * 128:
                                       4096 + kt * 512 + it * 128 + 128],
                                out1_fm[:, kt, tok],
                                start=(kt == 0), stop=(kt == 7)),
                       reads=[b_gup[wslot]] + fmc, writes=[b_pux[it]])
                EM("dve", lambda e, pul=pul, it=it, cpar=cpar: e.tensor_tensor(
                    a_sl[:, cpar, it, :], sg_sb[:, it, :], pul, op.mult),
                   reads=[b_sg[it], b_pux[it]], writes=[b_asl[cpar]])
            for lt in range(4):
                c = cb * 4 + lt
                lsl = slice(lt * 128, lt * 128 + 128)
                dbase = (lt % 2) * 1024
                for half in range(2):
                    dsl = PGX[:, dbase + half * 512:dbase + half * 512 + 512]
                    bank = (dbase + half * 512) // 512
                    for it in range(4):
                        EM("pe", lambda e, dsl=dsl, it=it, lsl=lsl, half=half,
                           w=wslot, cpar=cpar: e.matmul(
                               dsl, a_sl[:, cpar, it, lsl],
                               down_sb[:, w, it * 1024 + half * 512:
                                       it * 1024 + half * 512 + 512],
                               start=(it == 0), stop=(it == 3)),
                           reads=[b_asl[cpar], b_down[wslot]],
                           writes=[b_pgx[bank]])
                dfull = PGX[:, dbase:dbase + 1024]
                bk0, bk1 = dbase // 512, dbase // 512 + 1
                if ex < 8:
                    EM("dve", lambda e, dfull=dfull, c=c, ex=ex:
                       e.scalar_tensor_tensor(
                           out1_tm[:, c, :], dfull, w8n[:, c, ex:ex + 1],
                           out1_tm[:, c, :], op.mult, op.add),
                       reads=[b_pgx[bk0], b_pgx[bk1], b_w8[c], b_out1[c]],
                       writes=[b_out1[c]])
                elif not last:
                    EM("dve", lambda e, dfull=dfull, c=c: e.tensor_tensor(
                        out1_tm[:, c, :], out1_tm[:, c, :], dfull, op.add),
                       reads=[b_pgx[bk0], b_pgx[bk1], b_out1[c]],
                       writes=[b_out1[c]])
                else:
                    EM("dve", lambda e, dfull=dfull, c=c: e.scalar_tensor_tensor(
                        out1_tm[:, c, :], dfull, 1.0, out1_tm[:, c, :],
                        op.mult, op.add, accum_out=statB[:, 48:49]),
                       reads=[b_pgx[bk0], b_pgx[bk1], b_out1[c]],
                       writes=[b_out1[c], b_statB], touch=statB[:, 48:49])
                if last:
                    EM("act", lambda e, c=c: e.activation(
                        bounce[:, 1, :], out1_tm[:, c, :], AF.Square,
                        accum_out=statB[:, 49:50]),
                       reads=[b_out1[c]], writes=[b_bnc[1], b_statB],
                       touch=statB[:, 49:50])
                    EM("dve", lambda e, c=c: e.tensor_scalar(
                        statB[:, c:c + 1], statB[:, 48:49], 1.0 / D, None, op.mult),
                       reads=[b_statB], writes=[b_statB])
                    EM("dve", lambda e, c=c: e.tensor_scalar(
                        statB[:, 50:51], statB[:, c:c + 1], statB[:, c:c + 1],
                        None, op.mult),
                       reads=[b_statB], writes=[b_statB])
                    EM("dve", lambda e, c=c: e.tensor_scalar(
                        statB[:, 16 + c:17 + c], statB[:, 49:50], 1.0 / D,
                        statB[:, 50:51], op.mult, op.subtract),
                       reads=[b_statB], writes=[b_statB])
        if ex + 2 < NEXP:
            EM("pool", lambda e, ex=ex, w=(ex + 2) % 2: e.dma_start(
                gup_sb[:, w, :], dgup[ex + 2]), writes=[b_gup[(ex + 2) % 2]],
               dma=True)
            EM("sp", lambda e, ex=ex, w=(ex + 2) % 2: e.dma_start(
                down_sb[:, w, :], ddown[ex + 2]), writes=[b_down[(ex + 2) % 2]],
               dma=True)

    if DEV:
        for c in range(NCH):
            EM("sp", lambda e, c=c: e.dma_start(dbg_acc[c * 128:(c + 1) * 128, :],
                                                out1_tm[:, c, :]),
               reads=[b_out1[c]], dma=True)

    EM("act", lambda e: e.activation(statB[:, 32:48], statB[:, 16:32], AF.Ln,
                                     bias=epsc[:, 0:1]),
       reads=[b_statB, b_tab], writes=[b_statB])
    EM("act", lambda e: e.activation(statB[:, 32:48], statB[:, 32:48], AF.Exp,
                                     scale=-0.5),
       reads=[b_statB], writes=[b_statB])
    for c in range(NCH):
        bp = c % 2
        EM("dve", lambda e, c=c, bp=bp: e.tensor_scalar(
            bounce[:, bp, :], out1_tm[:, c, :], statB[:, c:c + 1],
            statB[:, 32 + c:33 + c], op.subtract, op.mult),
           reads=[b_out1[c], b_statB], writes=[b_bnc[bp]])
        EM("sp", lambda e, c=c, bp=bp: e.dma_start(
            dout[c * 128:(c + 1) * 128, :], bounce[:, bp, :]),
           reads=[b_bnc[bp]], writes=[b_outs[c]], dma=True)

    P.emit("sp", None, reads=list(b_outs))

    P.finalize(nc, stack)
    block = stack.enter_context(nc.Block())
    with stack, stB:
        @block.tensor
        def _(eng):
            P.replay("pe", eng)

        @block.vector
        def _(eng):
            P.replay("dve", eng)

        @block.scalar
        def _(eng):
            P.replay("act", eng)

        @block.gpsimd
        def _(eng):
            P.replay("pool", eng)

        @block.sync
        def _(eng):
            P.replay("sp", eng)

    return nc



def _mk_inputs(core, ins):
    x = np.asarray(ins["L0_attn_x"])
    b = core
    W_in = np.asarray(ins["L0_attn_W_in"])
    win_r = np.ascontiguousarray(W_in.T.reshape(8, 128, 1024).transpose(1, 0, 2))
    xb = np.ascontiguousarray(x[b])
    xfm = np.ascontiguousarray(
        x[b].T.reshape(8, 128, 16, 128).transpose(2, 1, 0, 3).reshape(16, 128, 1024))
    cre_aug = np.concatenate([np.asarray(ins["L0_attn_C_re"]).T,
                              np.asarray(ins["L0_attn_D_re"])[None, :]], axis=0)
    ab = np.stack([np.asarray(ins["L0_attn_A_re"]), np.asarray(ins["L0_attn_A_im"]),
                   np.asarray(ins["L0_attn_B_re"]), np.asarray(ins["L0_attn_B_im"])],
                  axis=1).astype(np.float32)
    rw_r = np.ascontiguousarray(
        np.asarray(ins["L0_ffn_router_w"]).reshape(8, 128, 8).transpose(1, 0, 2))

    def tile_w(w):
        I = w.shape[1]
        return np.ascontiguousarray(
            w.reshape(8, 128, I).transpose(1, 0, 2).reshape(128, 8 * I))

    def tile_d(w):
        return np.ascontiguousarray(
            w.reshape(4, 128, 1024).transpose(1, 0, 2).reshape(128, 4096))

    gate = np.asarray(ins["L0_ffn_gate_w"])
    up = np.asarray(ins["L0_ffn_up_w"])
    down = np.asarray(ins["L0_ffn_down_w"])
    sgw = np.asarray(ins["L0_ffn_s_gate_w"])
    suw = np.asarray(ins["L0_ffn_s_up_w"])
    sdw = np.asarray(ins["L0_ffn_s_down_w"])
    gup = np.stack(
        [np.concatenate([tile_w(gate[e]), tile_w(up[e])], axis=1) for e in range(8)]
        + [np.concatenate([tile_w(sgw[:, q * 512:(q + 1) * 512]),
                           tile_w(suw[:, q * 512:(q + 1) * 512])], axis=1)
           for q in range(4)], axis=0)
    down_r = np.stack([tile_d(down[e]) for e in range(8)]
                      + [tile_d(sdw[q * 512:(q + 1) * 512]) for q in range(4)],
                      axis=0)

    return {
        "x_tm": xb,
        "x_fm": xfm,
        "w_in_r": win_r,
        "b_in_row": np.ascontiguousarray(np.asarray(ins["L0_attn_b_in"])[None, :]),
        "gamma_row": np.ascontiguousarray(np.asarray(ins["L0_attn_gamma"])[None, :]),
        "beta_row": np.ascontiguousarray(np.asarray(ins["L0_attn_beta"])[None, :]),
        "ab_vecs": ab,
        "d_row": np.ascontiguousarray(np.stack([
            np.asarray(ins["L0_attn_D_re"]),
            np.asarray(ins["L0_attn_D_im"])])),
        "cre_aug": np.ascontiguousarray(cre_aug),
        "cim_t": np.ascontiguousarray(np.asarray(ins["L0_attn_C_im"]).T),
        "dft_mats": _dft_mats()[0],
        "dft_lo": _dft_mats()[1],
        "ones_row": np.ones((1, 128), np.float32),
        "ones_row_f": np.ones((1, 128), np.float32),
        "ident_f": np.eye(128, dtype=np.float32),
        "ident_r": np.eye(128, dtype=np.float32),
        "rw_r": rw_r,
        "rbias": np.asarray(ins["L0_ffn_router_bias"]).reshape(1, 1)
        .astype(np.float32),
        "gup": gup,
        "down_r": down_r,
    }


def kernel(**inputs):
    nc = build_nc()
    in_maps = [_mk_inputs(b, inputs) for b in range(B_SZ)]
    res = run_bass_kernel_spmd(nc, in_maps, core_ids=list(range(B_SZ)))
    out = np.stack([np.asarray(res.results[b]["out"]) for b in range(B_SZ)], axis=0)
    return out
